# version 02
# baseline (speedup 1.0000x reference)
import math
from contextlib import ExitStack

import numpy as np
import ml_dtypes

import concourse.bass as bass
import concourse.mybir as mybir
from concourse.bass_utils import run_bass_kernel_spmd

F32 = mybir.dt.float32
BF16 = mybir.dt.bfloat16
I32 = mybir.dt.int32
AF = mybir.ActivationFunctionType
ALU = mybir.AluOpType
AX = mybir.AxisListType

BF = ml_dtypes.bfloat16

D = 1024
NCORES = 8
SEQ = 2048
SPC = 2
T = SPC * SEQ
DCH = D // 128
TB = 512
NTB = T // TB
NTC = T // 128
TBC = TB // 128

F_FF = 4096
FCH = F_FF // 128
NEXP = 8
H_L1 = 3072
HCH = H_L1 // 128

ENGINES = ("tensor", "vector", "scalar", "gpsimd", "sync")



class Chan:
    def __init__(self, nc, name, ctx):
        self.sem = ctx.enter_context(nc.semaphore(name))
        self.n = 0

    def take(self, step):
        self.n += step
        return self.n


class Prog:

    def __init__(self, nc, ctx):
        self.nc = nc
        self.ctx = ctx
        self.chans = {}
        self.banks = [ctx.enter_context(nc.psum_tensor(f"bank{i}", [128, 512], F32))
                      for i in range(8)]
        self.uid = 0
        self.em_last = {}

    def chan(self, name) -> Chan:
        if name not in self.chans:
            self.chans[name] = Chan(self.nc, name, self.ctx)
        return self.chans[name]

    def fresh(self, base):
        self.uid += 1
        return f"{base}_{self.uid}"


class Emit:

    def __init__(self, p: Prog):
        self.p = p
        self.streams = {e: [] for e in ENGINES}
        self.last = dict(p.em_last) if hasattr(p, "em_last") else {}

    def on(self, eng, fn):
        self.streams[eng].append(fn)

    def wait(self, eng, chan: Chan, tok):
        self.on(eng, lambda e: e.wait_ge(chan.sem, tok))

    def dma(self, queue, dst, src, chan: Chan, waits=()):
        if chan.n > 0:
            self.wait(queue, chan, chan.n)
        for (c, t) in waits:
            self.wait(queue, c, t)
        tok = chan.take(16)
        self.on(queue, lambda e: e.dma_start(dst, src).then_inc(chan.sem, 16))
        return tok

    def op(self, eng, chan, fn, waits=()):
        if eng in ("vector", "scalar", "gpsimd") and eng in self.last:
            c0, t0 = self.last[eng]
            self.wait(eng, c0, t0)
        for (c, t) in waits:
            self.wait(eng, c, t)
        tok = chan.take(1) if chan is not None else None
        if chan is not None:
            self.on(eng, lambda e: fn(e).then_inc(chan.sem, 1))
            if eng in ("vector", "scalar", "gpsimd"):
                self.last[eng] = (chan, tok)
                self.p.em_last[eng] = (chan, tok)
        else:
            self.on(eng, lambda e: fn(e))
        return tok

    def replay(self):
        nc = self.p.nc
        streams = self.streams
        with nc.Block() as block:
            if streams["sync"]:
                @block.sync
                def _(e):
                    for fn in streams["sync"]:
                        fn(e)
            if streams["gpsimd"]:
                @block.gpsimd
                def _(e):
                    for fn in streams["gpsimd"]:
                        fn(e)
            if streams["tensor"]:
                @block.tensor
                def _(e):
                    for fn in streams["tensor"]:
                        fn(e)
            if streams["vector"]:
                @block.vector
                def _(e):
                    for fn in streams["vector"]:
                        fn(e)
            if streams["scalar"]:
                @block.scalar
                def _(e):
                    for fn in streams["scalar"]:
                        fn(e)



def host_constants():
    c = {}
    c["ident_f"] = np.eye(128, dtype=np.float32)
    c["ident_b"] = np.eye(128).astype(BF)
    c["ones_col_f"] = np.ones((128, 1), dtype=np.float32)
    c["ones_col_b"] = np.ones((128, 1)).astype(BF)
    c["ones_row_b"] = np.ones((1, TB)).astype(BF)
    c["ones_row_f"] = np.ones((1, TB), dtype=np.float32)
    c["ones_m_b"] = np.ones((1, 128)).astype(BF)
    c["ones_m_f"] = np.ones((1, 128), dtype=np.float32)
    half = D // 2
    theta = 1.0 / (10000.0 ** (np.arange(half, dtype=np.float64) / half))
    pos = np.arange(SEQ, dtype=np.float64)
    ang = np.outer(theta, pos)
    c["rope_cos"] = np.cos(ang).astype(np.float32)
    c["rope_sin"] = np.sin(ang).astype(np.float32)
    i = np.arange(128)
    dif = i[:, None] - i[None, :]
    c["tdiff_T"] = np.maximum(-dif, 0).astype(np.float32)
    c["tlow_T"] = (dif <= 0).astype(np.float32)
    c["bin_causal"] = (dif <= 0).astype(np.float32)
    c["bin_tri_hi"] = (dif >= 0).astype(np.float32)
    row0 = np.zeros((128, 128), np.float32); row0[0, :] = 1.0
    col0 = np.zeros((128, 128), np.float32); col0[:, 0] = 1.0
    c["bin_row0"] = row0
    c["bin_col0"] = col0
    c["bin_zero"] = np.zeros((128, 128), np.float32)
    c["bin_tri_hi_row0"] = np.maximum(c["bin_tri_hi"], row0)
    c["bin_causal_col0"] = np.maximum(c["bin_causal"], col0)
    return c


_CONST = host_constants()


def declare_constants(nc, names):
    h = {}
    for k in names:
        v = _CONST[k]
        dt = {np.dtype(np.float32): F32, np.dtype(BF): BF16, np.dtype(np.int32): I32}[v.dtype]
        h[k] = nc.dram_tensor("c_" + k, list(v.shape), dt, kind="ExternalInput")
    return h


def const_feeds(names):
    return {"c_" + k: _CONST[k] for k in names}




class Pool:

    def __init__(self, p: Prog, pctx, name, nslots, shape, dtype, queue="sync"):
        self.p = p
        self.name = name
        self.nslots = nslots
        self.queue = queue
        self.tiles = [pctx.enter_context(p.nc.sbuf_tensor(p.fresh(f"{name}s{i}"), list(shape), dtype))
                      for i in range(nslots)]
        self.lch = [p.chan(f"{name}_l{i}") for i in range(nslots)]
        self.it = 0
        self.ltok = {}
        self.rel = {}

    def slot(self, i):
        return i % self.nslots

    def tile(self, i):
        return self.tiles[self.slot(i)]

    def begin(self, em, extra_waits=()):
        i = self.it
        self.it += 1
        w = list(extra_waits)
        if i >= self.nslots:
            w.append(self.rel[i - self.nslots])
        for (c, t) in w:
            em.wait(self.queue, c, t)
        return i, self.tiles[self.slot(i)]

    def dma(self, em, i, dst, src):
        self.ltok[i] = em.dma(self.queue, dst, src, self.lch[self.slot(i)])
        return self.ltok[i]

    def tok(self, i):
        return self.ltok[i]

    def chan_of(self, i):
        return self.lch[self.slot(i)]

    def set_release(self, i, chan, tok):
        self.rel[i] = (chan, tok)


class StorePool:

    def __init__(self, p: Prog, pctx, name, nslots, shape, dtype, queue="scalar"):
        self.p = p
        self.nslots = nslots
        self.queue = queue
        self.tiles = [pctx.enter_context(p.nc.sbuf_tensor(p.fresh(f"{name}s{i}"), list(shape), dtype))
                      for i in range(nslots)]
        self.sch = [p.chan(f"{name}_t{i}") for i in range(nslots)]
        self.it = 0
        self.stok = {}

    def slot(self, i):
        return i % self.nslots

    def begin(self):
        i = self.it
        self.it += 1
        w = []
        prev = i - self.nslots
        if prev >= 0:
            w.append((self.sch[self.slot(i)], self.stok[prev]))
        return i, self.tiles[self.slot(i)], w

    def store(self, em, i, dst, src, waits=()):
        for (c, t) in waits:
            em.wait(self.queue, c, t)
        self.stok[i] = em.dma(self.queue, dst, src, self.sch[self.slot(i)])
        return self.stok[i]


def op_multi(em, eng, chan, fn, waits=(), extra_incs=()):
    for (c, t) in waits:
        em.wait(eng, c, t)
    tok = chan.take(1) if chan is not None else None
    if chan is not None:
        em.on(eng, lambda e: fn(e).then_inc(chan.sem, 1))
    else:
        em.on(eng, lambda e: fn(e))
    for (c, s) in extra_incs:
        c.take(s)
        em.on(eng, lambda e, c=c, s=s: e.sem_inc(c.sem, s))
    return tok


def quiesce_dma(p, em):
    for name, ch in list(p.chans.items()):
        if ch.n == 0:
            continue
    for name, ch in list(p.chans.items()):
        if name in ("pe", "vector", "scalar", "gpsimd"):
            continue
        if ch.n > 0:
            em.wait("sync", ch, ch.n)



GLOBAL_CONSTS = ["ident_f", "ident_b", "ones_col_f", "ones_col_b", "ones_row_b",
                 "ones_row_f", "ones_m_b", "ones_m_f", "bin_causal", "bin_zero",
                 "bin_tri_hi", "bin_row0", "bin_col0", "bin_tri_hi_row0",
                 "bin_causal_col0", "tdiff_T", "tlow_T"]


def load_globals(p: Prog, hc):
    em = Emit(p)
    out = {}
    ld = p.chan("gconst")
    tok = None
    for name in GLOBAL_CONSTS:
        if name not in hc:
            continue
        ap = hc[name].ap()
        t = p.ctx.enter_context(p.nc.sbuf_tensor(p.fresh("g" + name), list(ap.shape), ap.dtype))
        out[name] = t
        tok = em.dma("sync", t[:], ap, ld)
    em.wait("sync", ld, tok)
    em.replay()
    return out



def pass_t_in(p, em, g, x_dram, r_dram):
    nc = p.nc
    pe, dve = p.chan("pe"), p.chan("vector")
    with ExitStack() as pctx:
        lp = Pool(p, pctx, "ti_l", 2, [128, D], F32, queue="sync")
        sp = StorePool(p, pctx, "ti_o", 2, [128, D], F32, queue="scalar")
        ident = g["ident_f"]
        x_flat = x_dram.ap().rearrange("b t d -> (b t) d")
        bank_gate = [None, None]
        for tc in range(NTC):
            i, xt = lp.begin(em)
            lp.dma(em, i, xt[:], x_flat[tc * 128:(tc + 1) * 128, :])
            em.wait("tensor", lp.chan_of(i), lp.tok(i))
            par = tc % 2
            if bank_gate[par] is not None:
                em.wait("tensor", dve, bank_gate[par])
            tp_tok = None
            for c in range(DCH):
                bank = p.banks[2 * par + c // 4]
                tp_tok = em.op("tensor", pe,
                               lambda e, c=c, bank=bank, xt=xt: e.transpose(
                                   bank[:, (c % 4) * 128:(c % 4) * 128 + 128],
                                   xt[:, c * 128:(c + 1) * 128], ident[:]))
                if c == DCH - 1:
                    lp.set_release(i, pe, tp_tok)
            j, stg, gate = sp.begin()
            e0 = em.op("vector", dve, lambda e, stg=stg, par=par: e.tensor_copy(stg[:, 0:512], p.banks[2 * par][:]),
                       waits=[(pe, tp_tok)] + gate)
            e1 = em.op("vector", dve, lambda e, stg=stg, par=par: e.tensor_copy(stg[:, 512:1024], p.banks[2 * par + 1][:]))
            bank_gate[par] = e1
            sp.store(em, j, r_dram.ap()[:, :, tc * 128:(tc + 1) * 128].rearrange("c p n -> p c n"),
                     stg.ap().rearrange("p (c n) -> p c n", c=DCH), waits=[(dve, e1)])
        quiesce_dma(p, em)
        em.replay()



def pass_t_out(p, em, g, r_dram, y_dram):
    nc = p.nc
    pe, dve = p.chan("pe"), p.chan("vector")
    with ExitStack() as pctx:
        lp = Pool(p, pctx, "to_l", 2, [128, D], F32, queue="sync")
        sp = StorePool(p, pctx, "to_o", 2, [128, D], F32, queue="scalar")
        ident = g["ident_f"]
        y_flat = y_dram.ap().rearrange("b t d -> (b t) d")
        bank_gate = [None, None]
        for tc in range(NTC):
            i, ft = lp.begin(em)
            lp.dma(em, i, ft.ap().rearrange("p (c n) -> p c n", c=DCH),
                   r_dram.ap()[:, :, tc * 128:(tc + 1) * 128].rearrange("c p n -> p c n"))
            em.wait("tensor", lp.chan_of(i), lp.tok(i))
            par = tc % 2
            if bank_gate[par] is not None:
                em.wait("tensor", dve, bank_gate[par])
            tp_tok = None
            for c in range(DCH):
                bank = p.banks[2 * par + c // 4]
                tp_tok = em.op("tensor", pe,
                               lambda e, c=c, bank=bank, ft=ft: e.transpose(
                                   bank[:, (c % 4) * 128:(c % 4) * 128 + 128],
                                   ft[:, c * 128:(c + 1) * 128], ident[:]))
                if c == DCH - 1:
                    lp.set_release(i, pe, tp_tok)
            j, stg, gate = sp.begin()
            e0 = em.op("vector", dve, lambda e, stg=stg, par=par: e.tensor_copy(stg[:, 0:512], p.banks[2 * par][:]),
                       waits=[(pe, tp_tok)] + gate)
            e1 = em.op("vector", dve, lambda e, stg=stg, par=par: e.tensor_copy(stg[:, 512:1024], p.banks[2 * par + 1][:]))
            bank_gate[par] = e1
            sp.store(em, j, y_flat[tc * 128:(tc + 1) * 128, :], stg[:], waits=[(dve, e1)])
        quiesce_dma(p, em)
        em.replay()



def pass_stats(p, em, g, r_dram, sum_row, sq_row):
    nc = p.nc
    pe, dve, act = p.chan("pe"), p.chan("vector"), p.chan("scalar")
    ones_f = g["ones_col_f"]
    with ExitStack() as pctx:
        lp = Pool(p, pctx, "st_l", 2, [128, DCH, TB], F32, queue="sync")
        sq = [pctx.enter_context(nc.sbuf_tensor(p.fresh("stq"), [128, TB], F32)) for _ in range(2)]
        sq_gate = [None, None]
        row_gate = None
        for tb in range(NTB):
            i, rt = lp.begin(em)
            lp.dma(em, i, rt.ap(), r_dram.ap()[:, :, tb * TB:(tb + 1) * TB].rearrange("c p n -> p c n"))
            mm = None
            for c in range(DCH):
                k = c % 2
                wsq = [(lp.chan_of(i), lp.tok(i))]
                if sq_gate[k] is not None:
                    wsq.append((pe, sq_gate[k]))
                sq_tok = em.op("scalar", act, lambda e, c=c, k=k, rt=rt: e.activation(sq[k][:], rt[:, c, :], AF.Square),
                               waits=wsq)
                w = [(lp.chan_of(i), lp.tok(i)), (act, sq_tok)]
                if c == 0 and row_gate is not None:
                    w.append((dve, row_gate))
                em.op("tensor", pe, lambda e, c=c, rt=rt: e.matmul(
                    p.banks[6][0:1, 0:TB], ones_f[:], rt[:, c, :],
                    start=(c == 0), stop=(c == DCH - 1)), waits=w)
                mm = em.op("tensor", pe, lambda e, c=c, k=k: e.matmul(
                    p.banks[7][0:1, 0:TB], ones_f[:], sq[k][:],
                    start=(c == 0), stop=(c == DCH - 1)))
                if c == DCH - 1:
                    lp.set_release(i, pe, mm)
                sq_gate[k] = mm
            c0 = em.op("vector", dve, lambda e, tb=tb: e.tensor_copy(sum_row[0:1, tb * TB:(tb + 1) * TB], p.banks[6][0:1, 0:TB]),
                       waits=[(pe, mm)])
            row_gate = em.op("vector", dve, lambda e, tb=tb: e.tensor_copy(sq_row[0:1, tb * TB:(tb + 1) * TB], p.banks[7][0:1, 0:TB]))
        quiesce_dma(p, em)
        em.replay()



class Consts1:
    def __init__(self, p: Prog):
        self.p = p
        self.cache = {}

    def prewarm(self, em, vals=(1e-5, 1e-6)):
        for v in vals:
            self.get(em, v)

    def get(self, em, val):
        val = float(val)
        if val not in self.cache:
            t = self.p.ctx.enter_context(self.p.nc.sbuf_tensor(self.p.fresh("k1"), [1, 1], F32))
            tok = em.op("vector", self.p.chan("vector"), lambda e: e.memset(t[:], val))
            self.cache[val] = (t, tok)
        t, tok = self.cache[val]
        return t[0:1, 0:1], tok



def rows_ln(p, em, k1: Consts1, sum_row, sq_row, m_row, sinv_row):
    dve, act = p.chan("vector"), p.chan("scalar")
    b_eps, btok = k1.get(em, 1e-5)
    t1 = em.op("vector", dve, lambda e: e.tensor_scalar_mul(m_row[:], sum_row[:], 1.0 / D))
    t2 = em.op("vector", dve, lambda e: e.tensor_mul(sinv_row[:], m_row[:], m_row[:]), waits=[(dve, t1)])
    t3 = em.op("vector", dve, lambda e: e.scalar_tensor_tensor(sinv_row[:], sq_row[:], 1.0 / D, sinv_row[:],
                                                               ALU.mult, ALU.subtract), waits=[(dve, t2)])
    t4 = em.op("scalar", act, lambda e: e.activation(sinv_row[:], sinv_row[:], AF.Ln, bias=b_eps),
               waits=[(dve, t3), (dve, btok)])
    t5 = em.op("scalar", act, lambda e: e.activation(sinv_row[:], sinv_row[:], AF.Exp, scale=-0.5),
               waits=[(act, t4)])
    return t5


def rows_mul(p, em, out_row, a_row, b_row, wa, wb, scale=1.0):
    dve = p.chan("vector")
    t1 = em.op("vector", dve, lambda e: e.tensor_mul(out_row[:], a_row[:], b_row[:]), waits=list(wa) + list(wb))
    if scale != 1.0:
        t1 = em.op("vector", dve, lambda e: e.tensor_scalar_mul(out_row[:], out_row[:], scale), waits=[(dve, t1)])
    return t1


def row_to_cols(p, em, g, row_sbuf, n, out_cols, row_waits=()):
    pe, dve = p.chan("pe"), p.chan("vector")
    ident = g["ident_f"]
    tok = None
    prev = None
    for c in range(n // 128):
        w = list(row_waits) if c == 0 else []
        if prev is not None:
            w.append((dve, prev))
        tp = em.op("tensor", pe, lambda e, c=c: e.transpose(
            p.banks[5][0:128, 0:1], row_sbuf[0:1, c * 128:(c + 1) * 128], ident[0:1, 0:1]), waits=w)
        prev = em.op("vector", dve, lambda e, c=c: e.tensor_copy(out_cols[:, c:c + 1], p.banks[5][0:128, 0:1]),
                     waits=[(pe, tp)])
        tok = prev
    return tok



def pass_xn(p, em, g, name, r_dram, out_dram, out_dt, m_row, a_row, w_cols=None,
            row_waits=()):
    nc = p.nc
    pe, dve, act = p.chan("pe"), p.chan("vector"), p.chan("scalar")
    ones_m_f = g["ones_m_f"]
    with ExitStack() as pctx:
        lp = Pool(p, pctx, name + "_l", 2, [128, DCH, TB], F32, queue="sync")
        sp = StorePool(p, pctx, name + "_o", 2, [128, DCH, TB], out_dt, queue="scalar")
        mb = pctx.enter_context(nc.sbuf_tensor(p.fresh(name + "mb"), [128, TB], F32))
        ab = pctx.enter_context(nc.sbuf_tensor(p.fresh(name + "ab"), [128, TB], F32))
        tt = pctx.enter_context(nc.sbuf_tensor(p.fresh(name + "tt"), [128, TB], F32))
        bc_gate = None
        for tb in range(NTB):
            i, rt = lp.begin(em)
            lp.dma(em, i, rt.ap(), r_dram.ap()[:, :, tb * TB:(tb + 1) * TB].rearrange("c p n -> p c n"))
            w = list(row_waits) if tb == 0 else []
            if bc_gate is not None:
                w.append((dve, bc_gate))
            if m_row is not None:
                t_m = em.op("tensor", pe, lambda e, tb=tb: e.matmul(
                    p.banks[4][:, 0:TB], ones_m_f[0:1, :], m_row[0:1, tb * TB:(tb + 1) * TB],
                    start=True, stop=True), waits=w)
                em.op("scalar", act, lambda e: e.copy(mb[:], p.banks[4][:]), waits=[(pe, t_m)])
            t_a = em.op("tensor", pe, lambda e, tb=tb: e.matmul(
                p.banks[5][:, 0:TB], ones_m_f[0:1, :], a_row[0:1, tb * TB:(tb + 1) * TB],
                start=True, stop=True), waits=(w if m_row is None else []))
            ab_tok = em.op("scalar", act, lambda e: e.copy(ab[:], p.banks[5][:]), waits=[(pe, t_a)])
            j, stg, gate = sp.begin()
            last = None
            for c in range(DCH):
                wt = [(lp.chan_of(i), lp.tok(i)), (act, ab_tok)] + (gate if c == 0 else [])
                if m_row is not None:
                    t1 = em.op("vector", dve, lambda e, c=c, rt=rt: e.tensor_sub(tt[:], rt[:, c, :], mb[:]),
                               waits=wt)
                    src = tt
                    wt2 = [(dve, t1)]
                else:
                    src, wt2 = None, wt
                if w_cols is not None:
                    if src is None:
                        t2 = em.op("vector", dve, lambda e, c=c, rt=rt: e.tensor_mul(tt[:], rt[:, c, :], ab[:]), waits=wt2)
                    else:
                        t2 = em.op("vector", dve, lambda e: e.tensor_mul(tt[:], tt[:], ab[:]), waits=wt2)
                    last = em.op("vector", dve, lambda e, c=c, stg=stg: e.tensor_single_scalar(
                        stg[:, c, :], tt[:], w_cols[:, c:c + 1], ALU.mult), waits=[(dve, t2)])
                else:
                    if src is None:
                        last = em.op("vector", dve, lambda e, c=c, rt=rt, stg=stg: e.tensor_mul(stg[:, c, :], rt[:, c, :], ab[:]), waits=wt2)
                    else:
                        last = em.op("vector", dve, lambda e, c=c, stg=stg: e.tensor_mul(stg[:, c, :], tt[:], ab[:]), waits=wt2)
                if c == DCH - 1:
                    lp.set_release(i, dve, last)
            bc_gate = last
            sp.store(em, j, out_dram.ap()[:, :, tb * TB:(tb + 1) * TB].rearrange("c p n -> p c n"),
                     stg.ap(), waits=[(dve, last)])
        quiesce_dma(p, em)
        em.replay()



def pass_linear(p, em, g, name, xn_dram, nin, w_handle, nout, *,
                wsrc="f32", bias_row=None, bias_waits=(), w_col0=0, bias_col0=0,
                out_dram=None, epilogue=None):
    nc = p.nc
    pe, dve, act = p.chan("pe"), p.chan("vector"), p.chan("scalar")
    nin_ch = nin // 128
    noc = nout // 128
    ones_row_b = g["ones_row_b"]
    with ExitStack() as pctx:
        wp = Pool(p, pctx, name + "_w", 2, [128, nin_ch, 128], BF16,
                  queue=("gpsimd" if wsrc == "f32" else "sync"))
        xp = Pool(p, pctx, name + "_x", 2, [128, nin_ch, TB], BF16, queue="sync")
        if epilogue is None:
            sp = StorePool(p, pctx, name + "_o", 2, [128, TB], BF16, queue="scalar")
        else:
            sp = StorePool(p, pctx, name + "_o", 2, [128, TB], F32, queue="scalar")
            ap_ = Pool(p, pctx, name + "_a", 2, [128, TB], F32, queue="sync")
            ev = pctx.enter_context(nc.sbuf_tensor(p.fresh(name + "ev"), [128, TB], F32))
            ev_gate = None
        if epilogue is not None:
            kind, add_dram, res_dram, scale = epilogue
        bank_gate = {0: None, 1: None}
        for oc in range(noc):
            iw, wt = wp.begin(em)
            if wsrc == "f32":
                src = w_handle.ap().rearrange("(i p) o -> p i o", p=128)[:, :, w_col0 + oc * 128:w_col0 + (oc + 1) * 128]
            else:
                src = w_handle.ap()[:, :, w_col0 + oc * 128:w_col0 + (oc + 1) * 128].rearrange("i p o -> p i o")
            wp.dma(em, iw, wt.ap(), src)
            last_mm = None
            for tb in range(NTB):
                ix, xt = xp.begin(em)
                xp.dma(em, ix, xt.ap(), xn_dram.ap()[:, :, tb * TB:(tb + 1) * TB].rearrange("c p n -> p c n"))
                bk = tb % 2
                bank = p.banks[bk]
                w0 = [(wp.chan_of(iw), wp.tok(iw)), (xp.chan_of(ix), xp.tok(ix))]
                if bank_gate[bk] is not None:
                    w0.append(bank_gate[bk])
                mm = None
                for i in range(nin_ch):
                    mm = em.op("tensor", pe, lambda e, i=i, wt=wt, xt=xt, bank=bank: e.matmul(
                        bank[:], wt[:, i, :], xt[:, i, :],
                        start=(i == 0), stop=(i == nin_ch - 1 and bias_row is None)),
                        waits=(w0 if i == 0 else []))
                if bias_row is not None:
                    mm = em.op("tensor", pe, lambda e, oc=oc, bank=bank: e.matmul(
                        bank[:], bias_row[0:1, bias_col0 + oc * 128:bias_col0 + (oc + 1) * 128],
                        ones_row_b[0:1, :], start=False, stop=True),
                        waits=(list(bias_waits) if (oc == 0 and tb == 0) else []))
                xp.set_release(ix, pe, mm)
                j, stg, gate = sp.begin()
                if epilogue is None:
                    e1 = em.op("scalar", act, lambda e, stg=stg, bank=bank: e.copy(stg[:], bank[:]),
                               waits=[(pe, mm)] + gate)
                    bank_gate[bk] = (act, e1)
                    sp.store(em, j, out_dram.ap()[oc, :, tb * TB:(tb + 1) * TB], stg.ap(),
                             waits=[(act, e1)])
                else:
                    ia, at = ap_.begin(em)
                    ap_.dma(em, ia, at.ap(), add_dram.ap()[oc, :, tb * TB:(tb + 1) * TB])
                    w1 = [(pe, mm)]
                    if ev_gate is not None:
                        w1.append((dve, ev_gate))
                    e1 = em.op("scalar", act, lambda e, bank=bank: e.activation(ev[:], bank[:], AF.Copy, scale=float(scale)),
                               waits=w1)
                    bank_gate[bk] = (act, e1)
                    e2 = em.op("vector", dve, lambda e, stg=stg, at=at: e.tensor_add(stg[:], ev[:], at[:]),
                               waits=[(act, e1), (ap_.chan_of(ia), ap_.tok(ia))] + gate)
                    ap_.set_release(ia, dve, e2)
                    ev_gate = e2
                    sp.store(em, j, res_dram.ap()[oc, :, tb * TB:(tb + 1) * TB], stg.ap(),
                             waits=[(dve, e2)])
                last_mm = mm
            wp.set_release(iw, pe, last_mm)
        quiesce_dma(p, em)
        em.replay()



def pass_linear_tm(p, em, g, name, xn_dram, nin, w_handle, nout, out_dram, *,
                   wsrc="f32", bias_row=None, bias_waits=(), w_col0=0, bias_col0=0):
    nc = p.nc
    pe, dve, act = p.chan("pe"), p.chan("vector"), p.chan("scalar")
    nin_ch = nin // 128
    ncb = (nout + 511) // 512
    ones_m_b = g["ones_m_b"]
    with ExitStack() as pctx:
        wt = pctx.enter_context(nc.sbuf_tensor(p.fresh(name + "w"), [128, nin_ch, nout], BF16))
        wch = p.chan(name + "_w0")
        if wsrc == "f32":
            wtok = em.dma("gpsimd", wt.ap(),
                          w_handle.ap().rearrange("(i p) o -> p i o", p=128)[:, :, w_col0:w_col0 + nout], wch)
        else:
            wtok = em.dma("sync", wt.ap(),
                          w_handle.ap()[:, :, w_col0:w_col0 + nout].rearrange("i p o -> p i o"), wch)
        xp = Pool(p, pctx, name + "_x", 2, [128, nin_ch, 128], BF16, queue="sync")
        sp = StorePool(p, pctx, name + "_o", 2, [128, nout], BF16, queue="scalar")
        bank_gate = {0: None, 1: None}
        for tc in range(NTC):
            ix, xt = xp.begin(em)
            xp.dma(em, ix, xt.ap(), xn_dram.ap()[:, :, tc * 128:(tc + 1) * 128].rearrange("c p n -> p c n"))
            j, stg, gate = sp.begin()
            last_e = None
            for cb in range(ncb):
                n0, n1 = cb * 512, min((cb + 1) * 512, nout)
                bk = 2 + cb % 2
                bank = p.banks[bk]
                w0 = [(xp.chan_of(ix), xp.tok(ix)), (wch, wtok)]
                if bank_gate[cb % 2] is not None:
                    w0.append(bank_gate[cb % 2])
                mm = None
                for i in range(nin_ch):
                    mm = em.op("tensor", pe, lambda e, i=i, n0=n0, n1=n1, bank=bank, xt=xt: e.matmul(
                        bank[0:128, 0:n1 - n0], xt[:, i, :], wt[:, i, n0:n1],
                        start=(i == 0), stop=(i == nin_ch - 1 and bias_row is None)),
                        waits=(w0 if i == 0 else []))
                if bias_row is not None:
                    mm = em.op("tensor", pe, lambda e, n0=n0, n1=n1, bank=bank: e.matmul(
                        bank[0:128, 0:n1 - n0], ones_m_b[0:1, :], bias_row[0:1, bias_col0 + n0:bias_col0 + n1],
                        start=False, stop=True),
                        waits=(list(bias_waits) if tc == 0 and cb == 0 else []))
                if cb == ncb - 1:
                    xp.set_release(ix, pe, mm)
                e1 = em.op("scalar", act, lambda e, stg=stg, bank=bank, n0=n0, n1=n1: e.copy(
                    stg[:, n0:n1], bank[0:128, 0:n1 - n0]),
                    waits=[(pe, mm)] + (gate if cb == 0 else []))
                bank_gate[cb % 2] = (act, e1)
                last_e = e1
            sp.store(em, j, out_dram.ap()[tc * 128:(tc + 1) * 128, :], stg.ap(), waits=[(act, last_e)])
        quiesce_dma(p, em)
        em.replay()



def pass_att(p, em, g, name, spec, q_dram, k_dram, v_dram, out_dram, gamma_dram=None):
    nc = p.nc
    pe, dve, act = p.chan("pe"), p.chan("vector"), p.chan("scalar")
    nh, dk, dv = spec["nh"], spec["dk"], spec["dv"]
    scale = spec["scale"]
    kch_n, qch_n = spec["kch"], spec["qch"]
    dv_total = spec["dv_total"]
    NKC = SEQ // 128
    NQB = SEQ // TB
    NTB_S = SEQ // TB
    ones_col_b = g["ones_col_b"]
    ones_m_f = g["ones_m_f"]
    full_q = dk > 128
    with ExitStack() as pctx:
        kp = Pool(p, pctx, name + "_k", 1, [128, kch_n, SEQ], BF16, queue="sync")
        vp = Pool(p, pctx, name + "_v", 1, [128, NKC, dv_total], BF16, queue="sync")
        if full_q:
            qp = Pool(p, pctx, name + "_q", 2, [128, qch_n, TB], BF16, queue="sync")
        else:
            qp = Pool(p, pctx, name + "_q", 2, [128, SEQ], BF16, queue="sync")
        P_sb = pctx.enter_context(nc.sbuf_tensor(p.fresh(name + "P"), [128, NKC, TB], BF16))
        qn2 = pctx.enter_context(nc.sbuf_tensor(p.fresh(name + "qn2"), [1, SEQ], F32))
        negb = pctx.enter_context(nc.sbuf_tensor(p.fresh(name + "nb"), [1, SEQ], F32))
        kn2 = pctx.enter_context(nc.sbuf_tensor(p.fresh(name + "kn2"), [1, SEQ], F32))
        mk = pctx.enter_context(nc.sbuf_tensor(p.fresh(name + "mk"), [1, 16], F32))
        sqt = pctx.enter_context(nc.sbuf_tensor(p.fresh(name + "sqt"), [128, TB], BF16))
        nb_b = pctx.enter_context(nc.sbuf_tensor(p.fresh(name + "nbb"), [128, TB], F32))
        rs_row = pctx.enter_context(nc.sbuf_tensor(p.fresh(name + "rs"), [1, TB], F32))
        rb = pctx.enter_context(nc.sbuf_tensor(p.fresh(name + "rb"), [128, TB], F32))
        dtmp = pctx.enter_context(nc.sbuf_tensor(p.fresh(name + "dt"), [128, 128], F32))
        corr = pctx.enter_context(nc.sbuf_tensor(p.fresh(name + "cr"), [128, 128], F32))
        sp = StorePool(p, pctx, name + "_o", 2, [128, TB], BF16, queue="scalar")
        b6_gate = None
        sq_gate = None
        nbb_gate = None
        rb_gate = None
        P_gate = {}
        bank_gate = {0: None, 1: None}
        av_gate = {0: None, 1: None}

        def stat_row(src_fn, n_acc, out_row, tb):
            nonlocal b6_gate, sq_gate
            mm = None
            for c in range(n_acc):
                ap_, rows, base, extra_w = src_fn(c)
                wsq = list(extra_w)
                if sq_gate is not None:
                    wsq.append(sq_gate)
                s1 = em.op("scalar", act, lambda e, ap_=ap_, rows=rows, base=base: e.activation(
                    sqt[base:base + rows, 0:TB], ap_, AF.Square), waits=wsq)
                w = [(act, s1)]
                if c == 0 and b6_gate is not None:
                    w.append(b6_gate)
                mm = em.op("tensor", pe, lambda e, rows=rows, base=base, c=c: e.matmul(
                    p.banks[6][0:1, 0:TB], ones_col_b[base:base + rows, 0:1], sqt[base:base + rows, 0:TB],
                    start=(c == 0), stop=(c == n_acc - 1)), waits=w)
                sq_gate = (pe, mm)
            ev = em.op("vector", dve, lambda e, tb=tb: e.tensor_copy(out_row[0:1, tb * TB:(tb + 1) * TB],
                                                                     p.banks[6][0:1, 0:TB]), waits=[(pe, mm)])
            b6_gate = (dve, ev)
            return ev

        if spec.get("decay"):
            Dt = pctx.enter_context(nc.sbuf_tensor(p.fresh(name + "Dt"), [128, 128], F32))
            Dm1 = pctx.enter_context(nc.sbuf_tensor(p.fresh(name + "Dm"), [128, 128], F32))
            grow = pctx.enter_context(nc.sbuf_tensor(p.fresh(name + "g"), [1, 1], F32))
            gcol = pctx.enter_context(nc.sbuf_tensor(p.fresh(name + "gc"), [128, 1], F32))
            gch = p.chan(name + "_g0")
            gap = gamma_dram.ap()
            if gap.ndim == 0:
                gap = gap.rearrange("-> 1 1")
            elif gap.ndim == 1:
                gap = gap.rearrange("x -> 1 x")
            gtok = em.dma("sync", grow[:], gap, gch)
            t1 = em.op("scalar", act, lambda e: e.activation(grow[:], grow[:], AF.Ln), waits=[(gch, gtok)])
            t2 = em.op("tensor", pe, lambda e: e.matmul(p.banks[4][0:128, 0:1], ones_m_f[0:1, :], grow[0:1, 0:1],
                                                        start=True, stop=True), waits=[(act, t1)])
            t3 = em.op("vector", dve, lambda e: e.tensor_copy(gcol[:], p.banks[4][0:128, 0:1]), waits=[(pe, t2)])
            t4 = em.op("vector", dve, lambda e: e.tensor_single_scalar(Dt[:], g["tdiff_T"][:], gcol[:, 0:1], ALU.mult))
            t5 = em.op("scalar", act, lambda e: e.activation(Dt[:], Dt[:], AF.Exp), waits=[(dve, t4)])
            t6 = em.op("vector", dve, lambda e: e.tensor_mul(Dt[:], Dt[:], g["tlow_T"][:]), waits=[(act, t5)])
            decay_tok = em.op("vector", dve, lambda e: e.tensor_scalar(Dm1[:], Dt[:], -1.0, 1.0, ALU.mult, ALU.add))

        kranges = sorted({spec["k_loc"](h) for h in range(nh)})
        krange_idx = {kr: i for i, kr in enumerate(kranges)}

        for s in range(SPC):
            ik, k_sb = kp.begin(em)
            kp.dma(em, ik, k_sb.ap(), k_dram.ap()[:, :, s * SEQ:(s + 1) * SEQ].rearrange("c p t -> p c t"))
            ktok = (kp.chan_of(ik), kp.tok(ik))
            iv, v_sb = vp.begin(em)
            vp.dma(em, iv, v_sb.ap(),
                   v_dram.ap()[s * SEQ:(s + 1) * SEQ, :].rearrange("(a p) d -> p a d", p=128))
            vtok = (vp.chan_of(iv), vp.tok(iv))
            last_av = None
            mk_toks = {}
            for gi, (kc_ch, kc_off, krows) in enumerate(kranges):
                for tb in range(NTB_S):
                    stat_row(lambda c, kc_ch=kc_ch, kc_off=kc_off, krows=krows, tb=tb: (
                        k_sb[kc_off:kc_off + krows, kc_ch, tb * TB:(tb + 1) * TB], krows, kc_off,
                        [ktok]), 1, kn2, tb)
                s2 = em.op("scalar", act, lambda e: e.activation(kn2[:], kn2[:], AF.Ln), waits=[b6_gate])
                s3 = em.op("scalar", act, lambda e: e.activation(kn2[:], kn2[:], AF.Exp, scale=0.5))
                tok = em.op("vector", dve, lambda e, gi=gi: e.reduce_max(
                    mk[0:1, gi:gi + 1], kn2[0:1, :], axis=AX.X), waits=[(act, s3)])
                mk_toks[gi] = tok
            for h in range(nh):
                q_ch, q_off = spec["q_loc"](h)
                kc_ch, kc_off, krows = spec["k_loc"](h)
                gi = krange_idx[spec["k_loc"](h)]
                vcol = spec["v_col"](h)
                o_ch, o_off = spec["out_loc"](h)
                if not full_q:
                    iq, qt_h = qp.begin(em)
                    qp.dma(em, iq, qt_h[kc_off:kc_off + dk, :],
                           q_dram.ap()[q_ch, q_off:q_off + dk, s * SEQ:(s + 1) * SEQ])
                    q_tok = (qp.chan_of(iq), qp.tok(iq))
                    for tb in range(NTB_S):
                        stat_row(lambda c, tb=tb: (qt_h[kc_off:kc_off + dk, tb * TB:(tb + 1) * TB],
                                                   dk, kc_off, [q_tok]), 1, qn2, tb)
                else:
                    for tb in range(NTB_S):
                        iqb, qt_b = qp.begin(em)
                        qp.dma(em, iqb, qt_b.ap(),
                               q_dram.ap()[:, :, s * SEQ + tb * TB:s * SEQ + (tb + 1) * TB].rearrange("c p t -> p c t"))
                        stat_row(lambda c, qt_b=qt_b, iqb=iqb: (qt_b[:, c, :], 128, 0,
                                                                [(qp.chan_of(iqb), qp.tok(iqb))]),
                                 qch_n, qn2, tb)
                        qp.set_release(iqb, pe, sq_gate[1])
                s2 = em.op("scalar", act, lambda e: e.activation(qn2[:], qn2[:], AF.Ln), waits=[b6_gate])
                s3 = em.op("scalar", act, lambda e: e.activation(qn2[:], qn2[:], AF.Exp, scale=0.5))
                nb_last = em.op("vector", dve, lambda e, gi=gi: e.tensor_scalar(
                    negb[0:1, :], qn2[0:1, :],
                    mk[0:1, gi:gi + 1], -1.0, ALU.mult, ALU.mult),
                    waits=[(act, s3), (dve, mk_toks[gi])])
                last_qk = None
                for qbl in range(NQB):
                    qb = s * NQB + qbl
                    blocks = spec["blocks"](qbl)
                    ncs = len(blocks)
                    if full_q:
                        iqb, qt_b = qp.begin(em)
                        qp.dma(em, iqb, qt_b.ap(),
                               q_dram.ap()[:, :, qb * TB:(qb + 1) * TB].rearrange("c p t -> p c t"))
                        q_tok = (qp.chan_of(iqb), qp.tok(iqb))
                        qt_cur = qt_b
                    if spec.get("decay"):
                        w = [(dve, nb_last)] + ([] if nbb_gate is None else [nbb_gate])
                        tb_ = em.op("tensor", pe, lambda e, qbl=qbl: e.matmul(
                            p.banks[4][:, 0:TB], ones_m_f[0:1, :], negb[0:1, qbl * TB:(qbl + 1) * TB],
                            start=True, stop=True), waits=w)
                        nbb_tok = em.op("scalar", act, lambda e: e.copy(nb_b[:], p.banks[4][:]), waits=[(pe, tb_)])
                        nbb_gate = (act, nbb_tok)
                    for j, (kc, masks) in enumerate(blocks):
                        bk = j % 2
                        bank = p.banks[bk]
                        w0 = [ktok, q_tok]
                        if bank_gate[bk] is not None:
                            w0.append(bank_gate[bk])
                        if full_q:
                            for c in range(qch_n):
                                mm = em.op("tensor", pe, lambda e, c=c, kc=kc, bank=bank, qt_cur=qt_cur: e.matmul(
                                    bank[:], k_sb[:, c, kc * 128:(kc + 1) * 128], qt_cur[:, c, :],
                                    start=(c == 0), stop=False),
                                    waits=(w0 if c == 0 else []))
                        else:
                            qsl = qt_h[kc_off:kc_off + dk, qbl * TB:(qbl + 1) * TB]
                            ksl = k_sb[kc_off:kc_off + dk, kc_ch, kc * 128:(kc + 1) * 128]
                            mm = em.op("tensor", pe, lambda e, bank=bank, ksl=ksl, qsl=qsl: e.matmul(
                                bank[:], ksl, qsl, start=True, stop=False), waits=w0)
                        mm = em.op("tensor", pe, lambda e, qbl=qbl, bank=bank: e.matmul(
                            bank[:], ones_m_f[0:1, :], negb[0:1, qbl * TB:(qbl + 1) * TB],
                            start=False, stop=True), waits=[(dve, nb_last)])
                        last_qk = mm
                        e_w = [(pe, mm)]
                        if j in P_gate:
                            e_w.append(P_gate[j])
                        ex = em.op("scalar", act, lambda e, j=j, bank=bank: e.activation(
                            P_sb[:, j, :], bank[:], AF.Exp, scale=scale), waits=e_w)
                        m_tok = (act, ex)
                        bank_gate[bk] = (act, ex)
                        for (ql, mname) in masks:
                            if mname == "decay":
                                d1 = em.op("vector", dve, lambda e, ql=ql, bank=bank: e.tensor_mul(
                                    dtmp[:], bank[:, ql * 128:(ql + 1) * 128], Dt[:]),
                                    waits=[(pe, mm), (dve, decay_tok), (act, ex)])
                                c1 = em.op("vector", dve, lambda e, ql=ql: e.tensor_mul(
                                    corr[:], Dm1[:], nb_b[:, ql * 128:(ql + 1) * 128]),
                                    waits=[(act, nbb_tok)])
                                c2 = em.op("vector", dve, lambda e: e.tensor_add(dtmp[:], dtmp[:], corr[:]))
                                mx = em.op("scalar", act, lambda e, j=j, ql=ql: e.activation(
                                    P_sb[:, j, ql * 128:(ql + 1) * 128], dtmp[:], AF.Exp, scale=scale),
                                    waits=[(dve, c2)])
                                m_tok = (act, mx)
                                bank_gate[bk] = (dve, d1)
                            else:
                                mx = em.op("vector", dve, lambda e, j=j, ql=ql, mname=mname: e.tensor_mul(
                                    P_sb[:, j, ql * 128:(ql + 1) * 128], P_sb[:, j, ql * 128:(ql + 1) * 128],
                                    g[mname][:]), waits=[(act, ex)])
                                m_tok = (dve, mx)
                        w = [m_tok]
                        if j == 0 and b6_gate is not None:
                            w.append(b6_gate)
                        cs_tok = em.op("tensor", pe, lambda e, j=j, sp=(j == ncs - 1): e.matmul(
                            p.banks[6][0:1, 0:TB], ones_col_b[:, 0:1], P_sb[:, j, :],
                            start=(j == 0), stop=sp), waits=w)
                    if full_q:
                        qp.set_release(iqb, pe, last_qk)
                    r1 = em.op("vector", dve, lambda e: e.tensor_copy(rs_row[:], p.banks[6][0:1, 0:TB]),
                               waits=[(pe, cs_tok)])
                    b6_gate = (dve, r1)
                    r2 = em.op("vector", dve, lambda e: e.reciprocal(rs_row[:], rs_row[:]))
                    w = [(dve, r2)] + ([] if rb_gate is None else [rb_gate])
                    if spec.get("decay"):
                        w.append((act, nbb_tok))
                    r3 = em.op("tensor", pe, lambda e: e.matmul(p.banks[4][:, 0:TB], ones_m_f[0:1, :], rs_row[0:1, :],
                                                                start=True, stop=True), waits=w)
                    r4 = em.op("scalar", act, lambda e: e.copy(rb[:], p.banks[4][:]), waits=[(pe, r3)])
                    ndc = (dv + 127) // 128
                    for dc in range(ndc):
                        c0 = vcol + dc * 128
                        m = min(128, dv - dc * 128)
                        bk = dc % 2
                        bank = p.banks[2 + bk]
                        w0 = [vtok]
                        if av_gate[bk] is not None:
                            w0.append(av_gate[bk])
                        for j, (kc, masks) in enumerate(blocks):
                            mm = em.op("tensor", pe, lambda e, j=j, kc=kc, c0=c0, m=m, bank=bank, sp=(j == ncs - 1): e.matmul(
                                bank[0:m, 0:TB], v_sb[:, kc, c0:c0 + m], P_sb[:, j, :],
                                start=(j == 0), stop=sp),
                                waits=(w0 if j == 0 else []))
                            if dc == ndc - 1:
                                P_gate[j] = (pe, mm)
                        last_av = mm
                        j2, stg, gate = sp.begin()
                        e1 = em.op("vector", dve, lambda e, m=m, bank=bank, stg=stg: e.tensor_mul(
                            stg[0:m, :], bank[0:m, 0:TB], rb[0:m, :]),
                            waits=[(pe, mm), (act, r4)] + gate)
                        av_gate[bk] = (dve, e1)
                        rb_gate = (dve, e1)
                        if dv <= 128:
                            dst = out_dram.ap()[o_ch, o_off:o_off + m, qb * TB:(qb + 1) * TB]
                        else:
                            dst = out_dram.ap()[dc, :, qb * TB:(qb + 1) * TB]
                        sp.store(em, j2, dst, stg[0:m, :], waits=[(dve, e1)])
                if not full_q:
                    qp.set_release(iq, pe, last_qk)
            kp.set_release(ik, pe, last_qk)
            vp.set_release(iv, pe, last_av)
        quiesce_dma(p, em)
        em.replay()



def blocks_causal(qbl):
    out = []
    hi = 4 * qbl + 3
    for kc in range(hi + 1):
        masks = []
        for ql in range(TBC):
            qc = 4 * qbl + ql
            if kc == qc:
                masks.append((ql, "bin_causal"))
            elif kc > qc:
                masks.append((ql, "bin_zero"))
        out.append((kc, masks))
    return out


def blocks_full_decay(qbl):
    out = []
    for kc in range(SEQ // 128):
        masks = []
        for ql in range(TBC):
            qc = 4 * qbl + ql
            if kc == qc:
                masks.append((ql, "decay"))
        out.append((kc, masks))
    return out


def blocks_window(qbl):
    qcs = [4 * qbl + i for i in range(TBC)]
    kcs = sorted(set([0] + [kc for qc in qcs for kc in (qc - 1, qc, qc + 1)
                            if 0 <= kc < SEQ // 128]))
    if qbl == 0:
        kcs = list(range(SEQ // 128))
    out = []
    for kc in kcs:
        masks = []
        for ql in range(TBC):
            qc = 4 * qbl + ql
            d = kc - qc
            if d == 0:
                continue
            elif d == 1:
                masks.append((ql, "bin_causal_col0" if qc == 0 else "bin_causal"))
            elif d == -1:
                if kc == 0:
                    masks.append((ql, "bin_tri_hi_row0"))
                else:
                    masks.append((ql, "bin_tri_hi"))
            else:
                if kc == 0:
                    masks.append((ql, "bin_row0"))
                elif qc == 0:
                    masks.append((ql, "bin_col0"))
                else:
                    masks.append((ql, "bin_zero"))
        out.append((kc, masks))
    return out


def pass_rope(p, em, g, name, r_dram, xr_dram, m_row, a_row, w1_dram, hc, row_waits=()):
    nc = p.nc
    pe, dve, act = p.chan("pe"), p.chan("vector"), p.chan("scalar")
    ones_m_f = g["ones_m_f"]
    HFCH = 4
    with ExitStack() as pctx:
        cos_sb = pctx.enter_context(nc.sbuf_tensor(p.fresh(name + "cos"), [128, HFCH, SEQ], F32))
        sin_sb = pctx.enter_context(nc.sbuf_tensor(p.fresh(name + "sin"), [128, HFCH, SEQ], F32))
        tch, tch2, wch = p.chan(name + "_t"), p.chan(name + "_t2"), p.chan(name + "_w")
        ctok = em.dma("sync", cos_sb.ap(), hc["rope_cos"].ap().rearrange("(c p) t -> p c t", p=128), tch)
        stok = em.dma("sync", sin_sb.ap(), hc["rope_sin"].ap().rearrange("(c p) t -> p c t", p=128), tch2)
        w1row = pctx.enter_context(nc.sbuf_tensor(p.fresh(name + "w1r"), [1, D], F32))
        wsel = pctx.enter_context(nc.sbuf_tensor(p.fresh(name + "wsel"), [1, D], F32))
        wcols = pctx.enter_context(nc.sbuf_tensor(p.fresh(name + "wc"), [128, 2 * HFCH], F32))
        wtok = em.dma("sync", w1row[:], w1_dram.ap().rearrange("(a d) -> a d", a=1), wch)
        c1 = em.op("vector", dve, lambda e: e.tensor_copy(wsel[0:1, 0:512], w1row[0:1, 0:D:2]), waits=[(wch, wtok)])
        c2 = em.op("vector", dve, lambda e: e.tensor_copy(wsel[0:1, 512:1024], w1row[0:1, 1:D:2]))
        wc_tok = row_to_cols(p, em, g, wsel, D, wcols, row_waits=[(dve, c2)])
        lp = Pool(p, pctx, name + "_l", 2, [128, TB], F32, queue="sync")
        lp2 = Pool(p, pctx, name + "_l2", 2, [128, TB], F32, queue="sync")
        sp = StorePool(p, pctx, name + "_o", 2, [128, 2, TB], BF16, queue="scalar")
        mb = pctx.enter_context(nc.sbuf_tensor(p.fresh(name + "mb"), [128, TB], F32))
        ab = pctx.enter_context(nc.sbuf_tensor(p.fresh(name + "ab"), [128, TB], F32))
        xe = pctx.enter_context(nc.sbuf_tensor(p.fresh(name + "xe"), [128, TB], F32))
        xo = pctx.enter_context(nc.sbuf_tensor(p.fresh(name + "xo"), [128, TB], F32))
        tt = pctx.enter_context(nc.sbuf_tensor(p.fresh(name + "tt"), [128, TB], F32))
        tt2 = pctx.enter_context(nc.sbuf_tensor(p.fresh(name + "t2"), [128, TB], F32))
        for tb in range(NTB):
            s = (tb * TB) // SEQ
            pos0 = tb * TB - s * SEQ
            w = list(row_waits) + [(dve, wc_tok)] if tb == 0 else []
            if tb > 0:
                w.append((dve, em.last["vector"][1]))
            t_m = em.op("tensor", pe, lambda e, tb=tb: e.matmul(
                p.banks[4][:, 0:TB], ones_m_f[0:1, :], m_row[0:1, tb * TB:(tb + 1) * TB],
                start=True, stop=True), waits=w)
            mtok = em.op("scalar", act, lambda e: e.copy(mb[:], p.banks[4][:]),
                         waits=[(pe, t_m)] + w)
            t_a = em.op("tensor", pe, lambda e, tb=tb: e.matmul(
                p.banks[5][:, 0:TB], ones_m_f[0:1, :], a_row[0:1, tb * TB:(tb + 1) * TB],
                start=True, stop=True), waits=(w if tb == 0 else []))
            atok = em.op("scalar", act, lambda e: e.copy(ab[:], p.banks[5][:]), waits=[(pe, t_a)])
            for c2_ in range(HFCH):
                ie, et = lp.begin(em)
                lp.dma(em, ie, et[:],
                       r_dram.ap()[2 * c2_:2 * c2_ + 2, 0:128:2, tb * TB:(tb + 1) * TB].rearrange("a p n -> (a p) n"))
                io, ot = lp2.begin(em)
                lp2.dma(em, io, ot[:],
                        r_dram.ap()[2 * c2_:2 * c2_ + 2, 1:128:2, tb * TB:(tb + 1) * TB].rearrange("a p n -> (a p) n"))
                e1 = em.op("vector", dve, lambda e, et=et: e.tensor_sub(xe[:], et[:], mb[:]),
                           waits=[(lp.chan_of(ie), lp.tok(ie)), (act, atok), (dve, wc_tok)])
                lp.set_release(ie, dve, e1)
                em.op("vector", dve, lambda e: e.tensor_mul(xe[:], xe[:], ab[:]))
                em.op("vector", dve, lambda e, c2_=c2_: e.tensor_single_scalar(xe[:], xe[:], wcols[:, c2_:c2_ + 1], ALU.mult))
                o1 = em.op("vector", dve, lambda e, ot=ot: e.tensor_sub(xo[:], ot[:], mb[:]),
                           waits=[(lp2.chan_of(io), lp2.tok(io))])
                lp2.set_release(io, dve, o1)
                em.op("vector", dve, lambda e: e.tensor_mul(xo[:], xo[:], ab[:]))
                em.op("vector", dve, lambda e, c2_=c2_: e.tensor_single_scalar(xo[:], xo[:], wcols[:, 4 + c2_:5 + c2_], ALU.mult))
                j, stg, gate = sp.begin()
                cw = [(tch, ctok), (tch2, stok)] + gate
                cs = cos_sb[:, c2_, pos0:pos0 + TB]
                sn = sin_sb[:, c2_, pos0:pos0 + TB]
                em.op("vector", dve, lambda e, cs=cs: e.tensor_mul(tt[:], xe[:], cs), waits=cw)
                em.op("vector", dve, lambda e, sn=sn: e.tensor_mul(tt2[:], xo[:], sn))
                l1 = em.op("vector", dve, lambda e, stg=stg: e.tensor_sub(stg[:, 0, :], tt[:], tt2[:]))
                em.op("vector", dve, lambda e, sn=sn: e.tensor_mul(tt[:], xe[:], sn))
                em.op("vector", dve, lambda e, cs=cs: e.tensor_mul(tt2[:], xo[:], cs))
                l2 = em.op("vector", dve, lambda e, stg=stg: e.tensor_add(stg[:, 1, :], tt[:], tt2[:]))
                sp.store(em, j, xr_dram.ap()[c2_, :, tb * TB:(tb + 1) * TB], stg[:, 0, :], waits=[(dve, l2)])
                sp.store(em, j, xr_dram.ap()[4 + c2_, :, tb * TB:(tb + 1) * TB], stg[:, 1, :])
        quiesce_dma(p, em)
        em.replay()



def pass_router(p, em, g, name, xn_dram, rw_dram, rb_dram, comb_dram):
    nc = p.nc
    pe, dve, act = p.chan("pe"), p.chan("vector"), p.chan("scalar")
    ones_row_b = g["ones_row_b"]
    ident = g["ident_f"]
    with ExitStack() as pctx:
        rw_sb = pctx.enter_context(nc.sbuf_tensor(p.fresh(name + "rw"), [128, DCH, 8], BF16))
        rb_sb = pctx.enter_context(nc.sbuf_tensor(p.fresh(name + "rb"), [1, 8], BF16))
        wch = p.chan(name + "_w")
        wtok = em.dma("gpsimd", rw_sb.ap(), rw_dram.ap().rearrange("(i p) o -> p i o", p=128), wch)
        btok = em.dma("gpsimd", rb_sb.ap(), rb_dram.ap().rearrange("(a o) -> a o", a=1), wch)
        xp = Pool(p, pctx, name + "_x", 2, [128, DCH, TB], BF16, queue="sync")
        sp = StorePool(p, pctx, name + "_o", 2, [8, TB], F32, queue="scalar")
        lg = pctx.enter_context(nc.sbuf_tensor(p.fresh(name + "lg"), [8, TB], F32))
        lgt = pctx.enter_context(nc.sbuf_tensor(p.fresh(name + "lt"), [128, TBC, 8], F32))
        v1 = pctx.enter_context(nc.sbuf_tensor(p.fresh(name + "v1"), [128, 1], F32))
        v2 = pctx.enter_context(nc.sbuf_tensor(p.fresh(name + "v2"), [128, 1], F32))
        m1 = pctx.enter_context(nc.sbuf_tensor(p.fresh(name + "m1"), [128, 8], F32))
        m2 = pctx.enter_context(nc.sbuf_tensor(p.fresh(name + "m2"), [128, 8], F32))
        l2t = pctx.enter_context(nc.sbuf_tensor(p.fresh(name + "l2"), [128, 8], F32))
        sg = pctx.enter_context(nc.sbuf_tensor(p.fresh(name + "sg"), [128, 1], F32))
        ta = pctx.enter_context(nc.sbuf_tensor(p.fresh(name + "ta"), [128, 1], F32))
        cmb = pctx.enter_context(nc.sbuf_tensor(p.fresh(name + "cm"), [128, 8], F32))
        lg_gate = None
        lg_rd_gate = None
        b2_gate = None
        b3_gate = None
        for tb in range(NTB):
            ix, xt = xp.begin(em)
            xp.dma(em, ix, xt.ap(), xn_dram.ap()[:, :, tb * TB:(tb + 1) * TB].rearrange("c p n -> p c n"))
            w0 = [(wch, wtok), (wch, btok), (xp.chan_of(ix), xp.tok(ix))]
            if lg_gate is not None:
                w0.append(lg_gate)
            mm = None
            for i in range(DCH):
                mm = em.op("tensor", pe, lambda e, i=i, xt=xt: e.matmul(
                    p.banks[0][0:8, 0:TB], rw_sb[:, i, :], xt[:, i, :],
                    start=(i == 0), stop=False),
                    waits=(w0 if i == 0 else []))
            mm = em.op("tensor", pe, lambda e: e.matmul(
                p.banks[0][0:8, 0:TB], rb_sb[0:1, :], ones_row_b[0:1, :],
                start=False, stop=True))
            xp.set_release(ix, pe, mm)
            wlg = [(pe, mm)]
            if lg_rd_gate is not None:
                wlg.append(lg_rd_gate)
            lg_tok = em.op("scalar", act, lambda e: e.copy(lg[:], p.banks[0][0:8, 0:TB]), waits=wlg)
            lg_gate = (act, lg_tok)
            tts = []
            for ql in range(TBC):
                w = [(act, lg_tok)]
                if b2_gate is not None and ql == 0:
                    w.append(b2_gate)
                tt_ = em.op("tensor", pe, lambda e, ql=ql: e.transpose(
                    p.banks[2][0:128, ql * 8:(ql + 1) * 8], lg[0:8, ql * 128:(ql + 1) * 128],
                    ident[0:8, 0:8]), waits=w)
                tts.append(tt_)
                lg_rd_gate = (pe, tt_)
            ev = em.op("vector", dve, lambda e: e.tensor_copy(
                lgt.ap().rearrange("p c o -> p (c o)"), p.banks[2][0:128, 0:TBC * 8]),
                waits=[(pe, tts[-1])])
            b2_gate = (dve, ev)
            for ql in range(TBC):
                src = lgt[:, ql, :]
                em.op("vector", dve, lambda e, src=src: e.reduce_max(v1[:], src, axis=AX.X))
                em.op("vector", dve, lambda e, src=src: e.tensor_single_scalar(m1[:], src, v1[:, 0:1], ALU.is_ge))
                em.op("vector", dve, lambda e, src=src: e.scalar_tensor_tensor(l2t[:], m1[:], -1e30, src, ALU.mult, ALU.add))
                em.op("vector", dve, lambda e: e.reduce_max(v2[:], l2t[:], axis=AX.X))
                sv = em.op("vector", dve, lambda e: e.tensor_sub(sg[:], v1[:], v2[:]))
                sgt = em.op("scalar", act, lambda e: e.activation(sg[:], sg[:], AF.Sigmoid), waits=[(dve, sv)])
                em.op("vector", dve, lambda e: e.tensor_single_scalar(m2[:], l2t[:], v2[:, 0:1], ALU.is_ge),
                      waits=[(act, sgt)])
                em.op("vector", dve, lambda e: e.tensor_scalar(ta[:], sg[:], -1.0, 1.0, ALU.mult, ALU.add))
                em.op("vector", dve, lambda e: e.tensor_single_scalar(m2[:], m2[:], ta[:, 0:1], ALU.mult))
                em.op("vector", dve, lambda e: e.tensor_single_scalar(m1[:], m1[:], sg[:, 0:1], ALU.mult))
                cm_tok = em.op("vector", dve, lambda e: e.tensor_add(cmb[:], m2[:], m1[:]))
                w = [(dve, cm_tok)]
                if b3_gate is not None:
                    w.append(b3_gate)
                tt_ = em.op("tensor", pe, lambda e: e.transpose(
                    p.banks[3][0:8, 0:128], cmb[:], ident[:]), waits=w)
                if ql == 0:
                    j, stg, gate = sp.begin()
                g2 = (gate if ql == 0 else [])
                evc = em.op("vector", dve, lambda e, ql=ql, stg=stg: e.tensor_copy(
                    stg[0:8, ql * 128:(ql + 1) * 128], p.banks[3][0:8, 0:128]),
                    waits=[(pe, tt_)] + g2)
                b3_gate = (dve, evc)
            sp.store(em, j, comb_dram.ap()[:, tb * TB:(tb + 1) * TB], stg.ap(), waits=[(dve, evc)])
        quiesce_dma(p, em)
        em.replay()



def pass_cast_dram(p, em, name, src_dram, dst_dram, n_moves, src_slice, dst_slice,
                   tile_shape=(128, 1024)):
    nc = p.nc
    with ExitStack() as pctx:
        bp = Pool(p, pctx, name + "_b", 2, list(tile_shape), BF16, queue="gpsimd")
        st = [p.chan(name + f"_s{i}") for i in range(2)]
        stok = {}
        for i in range(n_moves):
            ii, bt = bp.begin(em)
            slot = bp.slot(ii)
            if ii >= 2 and (ii - 2) in stok:
                pass
            bp.dma(em, ii, bt.ap(), src_slice(i))
            tok = em.dma("gpsimd", dst_slice(i), bt.ap(), st[slot],
                         waits=[(bp.chan_of(ii), bp.tok(ii))])
            stok[ii] = tok
            bp.set_release(ii, st[slot], tok)
        quiesce_dma(p, em)
        em.replay()



def pass_moe_ph1(p, em, g, name, xn_dram, w1_dram, b1_dram, w2_dram, b2_dram,
                 comb_dram, h_dram):
    nc = p.nc
    pe, dve, act = p.chan("pe"), p.chan("vector"), p.chan("scalar")
    ones_row_b = g["ones_row_b"]
    ones_m_b = g["ones_m_b"]
    with ExitStack() as pctx:
        xn_sb = pctx.enter_context(nc.sbuf_tensor(p.fresh(name + "xn"), [128, DCH, T], BF16))
        xch = p.chan(name + "_x")
        xtok = em.dma("sync", xn_sb.ap(), xn_dram.ap().rearrange("c p t -> p c t"), xch)
        w1p = Pool(p, pctx, name + "_w1", 2, [128, DCH, 128], BF16, queue="gpsimd")
        w2p = Pool(p, pctx, name + "_w2", 2, [128, DCH, 128], BF16, queue="gpsimd")
        b1row = pctx.enter_context(nc.sbuf_tensor(p.fresh(name + "b1r"), [1, F_FF], BF16))
        b2row = pctx.enter_context(nc.sbuf_tensor(p.fresh(name + "b2r"), [1, F_FF], BF16))
        crow = pctx.enter_context(nc.sbuf_tensor(p.fresh(name + "cr"), [1, T], BF16))
        cb_sb = pctx.enter_context(nc.sbuf_tensor(p.fresh(name + "cb"), [128, NTB, TB], BF16))
        sl = pctx.enter_context(nc.sbuf_tensor(p.fresh(name + "sl"), [128, TB], F32))
        tt = pctx.enter_context(nc.sbuf_tensor(p.fresh(name + "tt"), [128, TB], F32))
        sp = StorePool(p, pctx, name + "_o", 3, [128, TB], BF16, queue="scalar")
        bch = p.chan(name + "_b")
        cch = p.chan(name + "_c")
        h1_gate = {0: None, 1: None}
        h2_gate = {0: None, 1: None}
        cb_gate = None
        sl_gate = None
        exp_gate = None
        for ex in range(NEXP):
            wexp = [] if exp_gate is None else [exp_gate]
            btok = em.dma("gpsimd", b1row[0:1, :], b1_dram.ap()[ex:ex + 1, :], bch, waits=wexp)
            btok = em.dma("gpsimd", b2row[0:1, :], b2_dram.ap()[ex:ex + 1, :], bch)
            ctok = em.dma("gpsimd", crow[:], comb_dram.ap()[ex:ex + 1, :], cch, waits=wexp)
            cbt_prev = None
            for tb in range(NTB):
                w = [(cch, ctok)]
                if cb_gate is not None and tb == 0:
                    w.append(cb_gate)
                if cbt_prev is not None:
                    w.append((act, cbt_prev))
                t_ = em.op("tensor", pe, lambda e, tb=tb: e.matmul(
                    p.banks[4][:, 0:TB], ones_m_b[0:1, :], crow[0:1, tb * TB:(tb + 1) * TB],
                    start=True, stop=True), waits=w)
                cbt = em.op("scalar", act, lambda e, tb=tb: e.copy(cb_sb[:, tb, :], p.banks[4][:]),
                            waits=[(pe, t_)])
                cbt_prev = cbt
            for fc in range(FCH):
                i1, w1t = w1p.begin(em)
                w1p.dma(em, i1, w1t.ap(),
                        w1_dram.ap()[ex].rearrange("(i p) f -> p i f", p=128)[:, :, fc * 128:(fc + 1) * 128])
                i2, w2t = w2p.begin(em)
                w2p.dma(em, i2, w2t.ap(),
                        w2_dram.ap()[ex].rearrange("(i p) f -> p i f", p=128)[:, :, fc * 128:(fc + 1) * 128])
                for tb in range(NTB):
                    bk = tb % 2
                    b_h1, b_h2 = p.banks[bk], p.banks[2 + bk]
                    w0 = [(xch, xtok), (w1p.chan_of(i1), w1p.tok(i1)), (bch, btok)]
                    if h1_gate[bk] is not None:
                        w0.append(h1_gate[bk])
                    mm1 = None
                    for i in range(DCH):
                        mm1 = em.op("tensor", pe, lambda e, i=i, tb=tb, w1t=w1t, b_h1=b_h1: e.matmul(
                            b_h1[:], w1t[:, i, :], xn_sb[:, i, tb * TB:(tb + 1) * TB],
                            start=(i == 0), stop=False),
                            waits=(w0 if i == 0 else []))
                    mm1 = em.op("tensor", pe, lambda e, fc=fc, b_h1=b_h1: e.matmul(
                        b_h1[:], b1row[0:1, fc * 128:(fc + 1) * 128], ones_row_b[0:1, :],
                        start=False, stop=True))
                    w0b = [(w2p.chan_of(i2), w2p.tok(i2))]
                    if h2_gate[bk] is not None:
                        w0b.append(h2_gate[bk])
                    mm2 = None
                    for i in range(DCH):
                        mm2 = em.op("tensor", pe, lambda e, i=i, tb=tb, w2t=w2t, b_h2=b_h2: e.matmul(
                            b_h2[:], w2t[:, i, :], xn_sb[:, i, tb * TB:(tb + 1) * TB],
                            start=(i == 0), stop=False),
                            waits=(w0b if i == 0 else []))
                    mm2 = em.op("tensor", pe, lambda e, fc=fc, b_h2=b_h2: e.matmul(
                        b_h2[:], b2row[0:1, fc * 128:(fc + 1) * 128], ones_row_b[0:1, :],
                        start=False, stop=True))
                    if tb == NTB - 1:
                        w1p.set_release(i1, pe, mm1)
                        w2p.set_release(i2, pe, mm2)
                    if tb == NTB - 1 and fc == FCH - 1:
                        exp_gate = (pe, mm2)
                    wsl = [(pe, mm1)]
                    if sl_gate is not None:
                        wsl.append(sl_gate)
                    s1 = em.op("scalar", act, lambda e, b_h1=b_h1: e.activation(sl[:], b_h1[:], AF.Silu),
                               waits=wsl)
                    h1_gate[bk] = (act, s1)
                    d1 = em.op("vector", dve, lambda e, b_h2=b_h2: e.tensor_mul(tt[:], b_h2[:], sl[:]),
                               waits=[(pe, mm2), (act, s1)])
                    sl_gate = (dve, d1)
                    h2_gate[bk] = (dve, d1)
                    j, stg, gate = sp.begin()
                    d2 = em.op("vector", dve, lambda e, tb=tb, stg=stg: e.tensor_mul(
                        stg[:], tt[:], cb_sb[:, tb, :]), waits=[(act, cbt)] + gate)
                    if tb == NTB - 1 and fc == FCH - 1:
                        cb_gate = (dve, d2)
                    sp.store(em, j, h_dram.ap()[ex, fc, :, tb * TB:(tb + 1) * TB], stg.ap(),
                             waits=[(dve, d2)])
        quiesce_dma(p, em)
        em.replay()



def pass_moe_ph2(p, em, g, name, h_dram, w3b_dram, b3_dram, comb_dram,
                 r_dram, h1h_dram, rout_dram):
    nc = p.nc
    pe, dve, act = p.chan("pe"), p.chan("vector"), p.chan("scalar")
    with ExitStack() as pctx:
        b3_sb = pctx.enter_context(nc.sbuf_tensor(p.fresh(name + "b3"), [8, D], BF16))
        cm_sb = pctx.enter_context(nc.sbuf_tensor(p.fresh(name + "cm"), [8, T], BF16))
        bch = p.chan(name + "_b")
        btok = em.dma("gpsimd", b3_sb.ap(), b3_dram.ap(), bch)
        ctok = em.dma("gpsimd", cm_sb.ap(), comb_dram.ap(), bch)
        hp = Pool(p, pctx, name + "_h", 3, [128, TB], BF16, queue="sync")
        wp = Pool(p, pctx, name + "_w", 3, [128, D], BF16, queue="sync")
        ap_ = Pool(p, pctx, name + "_a", 2, [128, DCH, TB], F32, queue="sync")
        rp = Pool(p, pctx, name + "_r", 2, [128, DCH, TB], F32, queue="sync")
        ev = pctx.enter_context(nc.sbuf_tensor(p.fresh(name + "ev"), [128, TB], F32))
        tt = pctx.enter_context(nc.sbuf_tensor(p.fresh(name + "tt"), [128, TB], F32))
        sp = StorePool(p, pctx, name + "_o", 2, [128, DCH, TB], F32, queue="scalar")
        bank_gate = [None] * 8
        ev_gate = None
        for tb in range(NTB):
            ia, at = ap_.begin(em)
            ap_.dma(em, ia, at.ap(), h1h_dram.ap()[:, :, tb * TB:(tb + 1) * TB].rearrange("c p n -> p c n"))
            ir, rt = rp.begin(em)
            rp.dma(em, ir, rt.ap(), r_dram.ap()[:, :, tb * TB:(tb + 1) * TB].rearrange("c p n -> p c n"))
            first = True
            mm_last = {}
            for ex in range(NEXP):
                for fc in range(FCH):
                    ih, ht = hp.begin(em)
                    hp.dma(em, ih, ht.ap(), h_dram.ap()[ex, fc, :, tb * TB:(tb + 1) * TB])
                    iw, wt = wp.begin(em)
                    wp.dma(em, iw, wt.ap(), w3b_dram.ap()[ex, fc, :, :])
                    w0 = [(hp.chan_of(ih), hp.tok(ih)), (wp.chan_of(iw), wp.tok(iw))]
                    mm = None
                    for dc in range(DCH):
                        w = list(w0) if dc == 0 else []
                        if first and bank_gate[dc] is not None:
                            w.append(bank_gate[dc])
                        mm = em.op("tensor", pe, lambda e, dc=dc, wt=wt, ht=ht, first=first: e.matmul(
                            p.banks[dc][:], wt[:, dc * 128:(dc + 1) * 128], ht[:],
                            start=first, stop=False), waits=w)
                        mm_last[dc] = mm
                    first = False
                    hp.set_release(ih, pe, mm)
                    wp.set_release(iw, pe, mm)
            for dc in range(DCH):
                mm = em.op("tensor", pe, lambda e, dc=dc, tb=tb: e.matmul(
                    p.banks[dc][:], b3_sb[0:8, dc * 128:(dc + 1) * 128], cm_sb[0:8, tb * TB:(tb + 1) * TB],
                    start=False, stop=True),
                    waits=([(bch, btok), (bch, ctok)] if tb == 0 and dc == 0 else []))
                mm_last[dc] = mm
            j, stg, gate = sp.begin()
            last = None
            for dc in range(DCH):
                w1 = [(pe, mm_last[dc])]
                if ev_gate is not None:
                    w1.append(ev_gate)
                e1 = em.op("scalar", act, lambda e, dc=dc: e.activation(ev[:], p.banks[dc][:], AF.Copy, scale=0.5),
                           waits=w1)
                bank_gate[dc] = (act, e1)
                e2 = em.op("vector", dve, lambda e, dc=dc, at=at: e.tensor_add(tt[:], ev[:], at[:, dc, :]),
                           waits=[(act, e1), (ap_.chan_of(ia), ap_.tok(ia))] + (gate if dc == 0 else []))
                e3 = em.op("vector", dve, lambda e, dc=dc, rt=rt, stg=stg: e.tensor_add(
                    stg[:, dc, :], tt[:], rt[:, dc, :]),
                    waits=[(rp.chan_of(ir), rp.tok(ir))])
                ev_gate = (dve, e2)
                last = e3
            ap_.set_release(ia, dve, last)
            rp.set_release(ir, dve, last)
            sp.store(em, j, rout_dram.ap()[:, :, tb * TB:(tb + 1) * TB].rearrange("c p n -> p c n"),
                     stg.ap(), waits=[(dve, last)])
        quiesce_dma(p, em)
        em.replay()




def pass_wprep(p, em, g, name, src_dram, dst_dram, R, C, scale):
    nc = p.nc
    pe, dve = p.chan("pe"), p.chan("vector")
    ident_b = g["ident_b"]
    CCH = C // 128
    with ExitStack() as pctx:
        lp = Pool(p, pctx, name + "_l", 2, [128, C], BF16, queue="gpsimd")
        sc = pctx.enter_context(nc.sbuf_tensor(p.fresh(name + "sc"), [128, C], BF16))
        sp = StorePool(p, pctx, name + "_o", 2, [128, CCH, 128], BF16, queue="scalar")
        bank_gate = [None, None]
        for i in range(R // 128):
            ii, wt = lp.begin(em)
            lp.dma(em, ii, wt.ap(), src_dram.ap()[i * 128:(i + 1) * 128, :])
            s1 = em.op("vector", dve, lambda e, wt=wt: e.tensor_scalar_mul(sc[:], wt[:], float(scale)),
                       waits=[(lp.chan_of(ii), lp.tok(ii))])
            lp.set_release(ii, dve, s1)
            j, stg, gate = sp.begin()
            ev = None
            ngrp = (CCH + 7) // 8
            for ccg in range(ngrp):
                par = (i * ngrp + ccg) % 2
                bank_b = p.banks[par].ap().bitcast(BF16)
                ncc = min(8, CCH - ccg * 8)
                tp = None
                for k in range(ncc):
                    cc = ccg * 8 + k
                    w = [(dve, s1)]
                    if k == 0 and bank_gate[par] is not None:
                        w.append(bank_gate[par])
                    tp = em.op("tensor", pe, lambda e, cc=cc, k=k, bank_b=bank_b: e.transpose(
                        bank_b[:, k * 128:(k + 1) * 128],
                        sc[:, cc * 128:(cc + 1) * 128], ident_b[:]), waits=w)
                ev = em.op("vector", dve, lambda e, stg=stg, bank_b=bank_b, ccg=ccg, ncc=ncc: e.tensor_copy(
                    stg.ap().rearrange("p c n -> p (c n)")[:, ccg * 1024:ccg * 1024 + ncc * 128],
                    bank_b[:, 0:ncc * 128]),
                    waits=[(pe, tp)] + (gate if ccg == 0 else []))
                bank_gate[par] = (dve, ev)
            sp.store(em, j, dst_dram.ap()[:, :, i * 128:(i + 1) * 128].rearrange("c p n -> p c n"),
                     stg.ap(), waits=[(dve, ev)])
        quiesce_dma(p, em)
        em.replay()



def pass_l1ffn(p, em, g, name, xn_dram, wgT, wuT, wdT, r_dram, rout_dram):
    nc = p.nc
    pe, dve, act = p.chan("pe"), p.chan("vector"), p.chan("scalar")
    with ExitStack() as pctx:
        xp = Pool(p, pctx, name + "_x", 2, [128, DCH, TB], BF16, queue="sync")
        gp = Pool(p, pctx, name + "_g", 2, [128, DCH, 128], BF16, queue="sync")
        up = Pool(p, pctx, name + "_u", 2, [128, DCH, 128], BF16, queue="sync")
        dp = Pool(p, pctx, name + "_d", 2, [128, HCH, 128], BF16, queue="sync")
        rp = Pool(p, pctx, name + "_r", 2, [128, DCH, TB], F32, queue="sync")
        hdn = pctx.enter_context(nc.sbuf_tensor(p.fresh(name + "h"), [128, HCH, TB], BF16))
        sl = pctx.enter_context(nc.sbuf_tensor(p.fresh(name + "sl"), [128, TB], F32))
        ev = pctx.enter_context(nc.sbuf_tensor(p.fresh(name + "ev"), [128, TB], F32))
        sp = StorePool(p, pctx, name + "_o", 2, [128, DCH, TB], F32, queue="scalar")
        g_gate = {0: None, 1: None}
        u_gate = {0: None, 1: None}
        d_gate = {0: None, 1: None}
        sl_gate = None
        hdn_gate = {}
        ev_gate = None
        for tb in range(NTB):
            ix, xt = xp.begin(em)
            xp.dma(em, ix, xt.ap(), xn_dram.ap()[:, :, tb * TB:(tb + 1) * TB].rearrange("c p n -> p c n"))
            ir, rt = rp.begin(em)
            rp.dma(em, ir, rt.ap(), r_dram.ap()[:, :, tb * TB:(tb + 1) * TB].rearrange("c p n -> p c n"))
            last_up_mm = None
            for hc in range(HCH):
                ig, gt = gp.begin(em)
                gp.dma(em, ig, gt.ap(), wgT.ap()[:, :, hc * 128:(hc + 1) * 128].rearrange("c p n -> p c n"))
                iu, ut = up.begin(em)
                up.dma(em, iu, ut.ap(), wuT.ap()[:, :, hc * 128:(hc + 1) * 128].rearrange("c p n -> p c n"))
                bk = hc % 2
                b_g, b_u = p.banks[bk], p.banks[2 + bk]
                w0 = [(xp.chan_of(ix), xp.tok(ix)), (gp.chan_of(ig), gp.tok(ig))]
                if g_gate[bk] is not None:
                    w0.append(g_gate[bk])
                mmg = None
                for i in range(DCH):
                    mmg = em.op("tensor", pe, lambda e, i=i, tb=tb, gt=gt, b_g=b_g, xt=xt: e.matmul(
                        b_g[:], gt[:, i, :], xt[:, i, :],
                        start=(i == 0), stop=(i == DCH - 1)),
                        waits=(w0 if i == 0 else []))
                gp.set_release(ig, pe, mmg)
                w0b = [(up.chan_of(iu), up.tok(iu))]
                if u_gate[bk] is not None:
                    w0b.append(u_gate[bk])
                mmu = None
                for i in range(DCH):
                    mmu = em.op("tensor", pe, lambda e, i=i, tb=tb, ut=ut, b_u=b_u, xt=xt: e.matmul(
                        b_u[:], ut[:, i, :], xt[:, i, :],
                        start=(i == 0), stop=(i == DCH - 1)),
                        waits=(w0b if i == 0 else []))
                up.set_release(iu, pe, mmu)
                last_up_mm = mmu
                wsl = [(pe, mmg)]
                if sl_gate is not None:
                    wsl.append(sl_gate)
                s1 = em.op("scalar", act, lambda e, b_g=b_g: e.activation(sl[:], b_g[:], AF.Silu),
                           waits=wsl)
                g_gate[bk] = (act, s1)
                w1 = [(pe, mmu), (act, s1)]
                if hc in hdn_gate:
                    w1.append(hdn_gate[hc])
                d1 = em.op("vector", dve, lambda e, hc=hc, b_u=b_u: e.tensor_mul(hdn[:, hc, :], b_u[:], sl[:]),
                           waits=w1)
                sl_gate = (dve, d1)
                u_gate[bk] = (dve, d1)
            xp.set_release(ix, pe, last_up_mm)
            j, stg, gate = sp.begin()
            last = None
            for dc in range(DCH):
                idp, dt_ = dp.begin(em)
                dp.dma(em, idp, dt_.ap(), wdT.ap()[:, :, dc * 128:(dc + 1) * 128].rearrange("c p n -> p c n"))
                bk = 4 + dc % 2
                bank = p.banks[bk]
                w0 = [(dp.chan_of(idp), dp.tok(idp)), (dve, em.last["vector"][1])]
                if d_gate[dc % 2] is not None:
                    w0.append(d_gate[dc % 2])
                mm = None
                for hc in range(HCH):
                    mm = em.op("tensor", pe, lambda e, hc=hc, dt_=dt_, bank=bank: e.matmul(
                        bank[:], dt_[:, hc, :], hdn[:, hc, :],
                        start=(hc == 0), stop=(hc == HCH - 1)),
                        waits=(w0 if hc == 0 else []))
                    hdn_gate[hc] = (pe, mm)
                dp.set_release(idp, pe, mm)
                w1 = [(pe, mm)]
                if ev_gate is not None:
                    w1.append(ev_gate)
                e1 = em.op("scalar", act, lambda e, bank=bank: e.copy(ev[:], bank[:]), waits=w1)
                d_gate[dc % 2] = (act, e1)
                e2 = em.op("vector", dve, lambda e, dc=dc, rt=rt, stg=stg: e.tensor_add(
                    stg[:, dc, :], ev[:], rt[:, dc, :]),
                    waits=[(act, e1), (rp.chan_of(ir), rp.tok(ir))] + (gate if dc == 0 else []))
                ev_gate = (dve, e2)
                last = e2
            rp.set_release(ir, dve, last)
            sp.store(em, j, rout_dram.ap()[:, :, tb * TB:(tb + 1) * TB].rearrange("c p n -> p c n"),
                     stg.ap(), waits=[(dve, last)])
        quiesce_dma(p, em)
        em.replay()



def rows_ln_full(p, em, k1, sum_row, sq_row, m_row, var_row, sinv_row):
    dve, act = p.chan("vector"), p.chan("scalar")
    b_eps, btok = k1.get(em, 1e-5)
    t1 = em.op("vector", dve, lambda e: e.tensor_scalar_mul(m_row[:], sum_row[:], 1.0 / D))
    t2 = em.op("vector", dve, lambda e: e.tensor_mul(var_row[:], m_row[:], m_row[:]), waits=[(dve, t1)])
    t3 = em.op("vector", dve, lambda e: e.scalar_tensor_tensor(var_row[:], sq_row[:], 1.0 / D, var_row[:],
                                                               ALU.mult, ALU.subtract))
    t4 = em.op("scalar", act, lambda e: e.activation(sinv_row[:], var_row[:], AF.Ln, bias=b_eps),
               waits=[(dve, t3), (dve, btok)])
    t5 = em.op("scalar", act, lambda e: e.activation(sinv_row[:], sinv_row[:], AF.Exp, scale=-0.5))
    return t5


def rows_rms1(p, em, k1, var_row, sinv_row, a1_row, sh_row, ln_tok):
    dve, act = p.chan("vector"), p.chan("scalar")
    b_eps2, btok = k1.get(em, 1e-6)
    t1 = em.op("vector", dve, lambda e: e.tensor_scalar_add(a1_row[:], var_row[:], 1e-5), waits=[(act, ln_tok)])
    t2 = em.op("vector", dve, lambda e: e.reciprocal(a1_row[:], a1_row[:]))
    t3 = em.op("vector", dve, lambda e: e.tensor_mul(a1_row[:], var_row[:], a1_row[:]))
    t4 = em.op("scalar", act, lambda e: e.activation(a1_row[:], a1_row[:], AF.Ln, bias=b_eps2),
               waits=[(dve, t3), (dve, btok)])
    t5 = em.op("scalar", act, lambda e: e.activation(a1_row[:], a1_row[:], AF.Exp, scale=-0.5))
    t6 = em.op("vector", dve, lambda e: e.tensor_mul(a1_row[:], a1_row[:], sinv_row[:]), waits=[(act, t5)])
    t7 = em.op("vector", dve, lambda e: e.tensor_scalar_mul(sh_row[:], sinv_row[:], 0.5))
    return t7


def rows_rms2(p, em, k1, sq_row, a2_row):
    dve, act = p.chan("vector"), p.chan("scalar")
    b_eps2, btok = k1.get(em, 1e-6)
    t1 = em.op("vector", dve, lambda e: e.tensor_scalar_mul(a2_row[:], sq_row[:], 4.0 / D))
    t2 = em.op("scalar", act, lambda e: e.activation(a2_row[:], a2_row[:], AF.Ln, bias=b_eps2),
               waits=[(dve, t1), (dve, btok)])
    t3 = em.op("scalar", act, lambda e: e.activation(a2_row[:], a2_row[:], AF.Exp, scale=-0.5))
    t4 = em.op("vector", dve, lambda e: e.tensor_scalar_mul(a2_row[:], a2_row[:], 2.0), waits=[(act, t3)])
    return t4


def load_wcols(p, em, g, name, w_dram, pctx):
    nc = p.nc
    dve = p.chan("vector")
    wrow = pctx.enter_context(nc.sbuf_tensor(p.fresh(name + "r"), [1, D], F32))
    wcols = pctx.enter_context(nc.sbuf_tensor(p.fresh(name + "c"), [128, DCH], F32))
    ch = p.chan(name + "_w")
    tok = em.dma("sync", wrow[:], w_dram.ap().rearrange("(a d) -> a d", a=1), ch)
    t = row_to_cols(p, em, g, wrow, D, wcols, row_waits=[(ch, tok)])
    return wcols, t



ATT_GQA = dict(
    nh=16, dk=64, dv=64, dv_total=256, kch=2, qch=8, scale=1.0 / 8.0,
    q_loc=lambda h: (h // 2, 64 * (h % 2)),
    k_loc=lambda h: ((h // 4) // 2, 64 * ((h // 4) % 2), 64),
    v_col=lambda h: 64 * (h // 4),
    out_loc=lambda h: (h // 2, 64 * (h % 2)),
    blocks=blocks_causal,
)

ATT_RET = dict(
    nh=1, dk=D, dv=D, dv_total=D, kch=DCH, qch=DCH, scale=1.0 / 32.0,
    q_loc=lambda h: (0, 0), k_loc=lambda h: (0, 0, 128),
    v_col=lambda h: 0, out_loc=lambda h: (0, 0),
    blocks=blocks_full_decay, decay=True,
)

ATT_WIN = dict(
    nh=16, dk=64, dv=64, dv_total=D, kch=DCH, qch=DCH, scale=1.0 / 8.0,
    q_loc=lambda h: (h // 2, 64 * (h % 2)),
    k_loc=lambda h: (h // 2, 64 * (h % 2), 64),
    v_col=lambda h: 64 * h,
    out_loc=lambda h: (h // 2, 64 * (h % 2)),
    blocks=blocks_window,
)



class Rows:
    def __init__(self, p, ctx=None):
        ctx = ctx if ctx is not None else p.ctx
        self.r = [ctx.enter_context(p.nc.sbuf_tensor(p.fresh(f"row{i}"), [1, T], F32)) for i in range(6)]
        self.sum, self.sq, self.m, self.var, self.sinv, self.a1 = self.r
        self.sh = self.sq
        self.sq2 = self.var
        self.sum2 = self.sum
        self.a2 = self.sinv


def emit_l0ffn_app(p, g, k1, rows_unused, hc, W, r_in, r_out, scr, tag):
    pe = p.chan("pe")
    rctx = ExitStack()
    rows = Rows(p, rctx)
    em = Emit(p); pass_stats(p, em, g, r_in, rows.sum, rows.sq)
    em = Emit(p)
    em.wait("vector", pe, pe.n)
    t_ln = rows_ln_full(p, em, k1, rows.sum, rows.sq, rows.m, rows.var, rows.sinv)
    t_r1 = rows_rms1(p, em, k1, rows.var, rows.sinv, rows.a1, rows.sh, t_ln)
    em.replay()
    dve = p.chan("vector")
    rw = [(dve, t_r1)]
    em = Emit(p); pass_xn(p, em, g, "xnh", r_in, scr["xn0h"], F32, rows.m, rows.sh, row_waits=rw)
    em = Emit(p); pass_rope(p, em, g, "rp", r_in, scr["xr"], rows.m, rows.a1, W["L0_ffn_norm1_w"], hc, row_waits=rw)
    em = Emit(p); pass_linear(p, em, g, "lq", scr["xr"], D, W["L0_ffn_Wq"], D, wsrc="f32", out_dram=scr["qT"])
    em = Emit(p); pass_linear(p, em, g, "lk", scr["xr"], D, W["L0_ffn_Wk"], 256, wsrc="f32", out_dram=scr["kT"])
    em = Emit(p); pass_linear_tm(p, em, g, "lv", scr["xr"], D, W["L0_ffn_Wv"], 256, scr["vtm"], wsrc="f32")
    em = Emit(p); pass_att(p, em, g, "ga", ATT_GQA, scr["qT"], scr["kT"], scr["vtm"], scr["ao"])
    em = Emit(p); pass_linear(p, em, g, "lo", scr["ao"], D, W["L0_ffn_Wo"], D, wsrc="f32",
                              epilogue=("res", scr["xn0h"], scr["h1h"], 0.5))
    em = Emit(p); pass_stats(p, em, g, scr["h1h"], rows.sum2, rows.sq2)
    em = Emit(p)
    em.wait("vector", pe, pe.n)
    t_a2 = rows_rms2(p, em, k1, rows.sq2, rows.a2)
    em.replay()
    em = Emit(p)
    with ExitStack() as wctx:
        w2cols, w2tok = load_wcols(p, em, g, "w2c", W["L0_ffn_norm2_w"], wctx)
        em.replay()
        em = Emit(p)
        pass_xn(p, em, g, "xn2", scr["h1h"], scr["xn2"], BF16, None, rows.a2, w_cols=w2cols,
                row_waits=[(p.chan("vector"), max(t_a2, w2tok))])
    rctx.close()
    em = Emit(p); pass_router(p, em, g, "rt", scr["xn2"], W["L0_ffn_router_w"], W["L0_ffn_router_b"], scr["comb"])
    em = Emit(p); pass_cast_dram(p, em, "w3c", W["L0_ffn_W3"], scr["w3b"], NEXP * FCH,
                                 lambda i: W["L0_ffn_W3"].ap()[i // FCH, (i % FCH) * 128:((i % FCH) + 1) * 128, :],
                                 lambda i: scr["w3b"].ap()[i // FCH, i % FCH, :, :])
    em = Emit(p); pass_moe_ph1(p, em, g, "p1", scr["xn2"], W["L0_ffn_W1"], W["L0_ffn_b1"],
                               W["L0_ffn_W2"], W["L0_ffn_b2"], scr["comb"], scr["h_d"])
    em = Emit(p); pass_moe_ph2(p, em, g, "p2", scr["h_d"], scr["w3b"], W["L0_ffn_b3"], scr["comb"],
                               r_in, scr["h1h"], r_out)


def emit_l0attn_app(p, g, k1, rows_unused, hc, W, r_in, r_out, scr):
    pe = p.chan("pe")
    brow = p.ctx.enter_context(p.nc.sbuf_tensor(p.fresh("bqkv"), [1, 3 * D], BF16))
    brow2 = p.ctx.enter_context(p.nc.sbuf_tensor(p.fresh("bprj"), [1, D], BF16))
    rctx = ExitStack()
    rows = Rows(p, rctx)
    em = Emit(p); pass_stats(p, em, g, r_in, rows.sum, rows.sq)
    em = Emit(p)
    em.wait("vector", pe, pe.n)
    t_ln = rows_ln_full(p, em, k1, rows.sum, rows.sq, rows.m, rows.var, rows.sinv)
    em.replay()
    rw = [(p.chan("scalar"), t_ln)]
    em = Emit(p); pass_xn(p, em, g, "xna", r_in, scr["xnb"], BF16, rows.m, rows.sinv, row_waits=rw)
    rctx.close()
    em = Emit(p)
    if True:
        bch = p.chan("bqkv_l")
        btok = em.dma("gpsimd", brow[:], W["L0_attn_bqkv"].ap().rearrange("(a d) -> a d", a=1), bch)
        em.replay()
        bw = [(bch, btok)]
        em = Emit(p); pass_linear(p, em, g, "aq", scr["xnb"], D, W["L0_attn_Wqkv"], D, wsrc="f32",
                                  out_dram=scr["qT"], bias_row=brow, bias_waits=bw, w_col0=0)
        em = Emit(p); pass_linear(p, em, g, "ak", scr["xnb"], D, W["L0_attn_Wqkv"], D, wsrc="f32",
                                  out_dram=scr["kTf"], bias_row=brow, bias_waits=bw, w_col0=D, bias_col0=D)
        em = Emit(p); pass_linear_tm(p, em, g, "av", scr["xnb"], D, W["L0_attn_Wqkv"], D, scr["vtmf"],
                                     wsrc="f32", bias_row=brow, bias_waits=bw, w_col0=2 * D, bias_col0=2 * D)
    em = Emit(p); pass_att(p, em, g, "ra", ATT_RET, scr["qT"], scr["kTf"], scr["vtmf"], scr["ao"],
                           gamma_dram=W["L0_attn_gamma"])
    em = Emit(p)
    if True:
        bch = p.chan("bprj_l")
        btok = em.dma("gpsimd", brow2[:], W["L0_attn_bproj"].ap().rearrange("(a d) -> a d", a=1), bch)
        em.replay()
        em = Emit(p); pass_linear(p, em, g, "ap", scr["ao"], D, W["L0_attn_Wproj"], D, wsrc="f32",
                                  bias_row=brow2, bias_waits=[(bch, btok)],
                                  epilogue=("res", r_in, r_out, 1.0))


def emit_l1ffn_app(p, g, k1, rows_unused, hc, W, r_in, r_out, scr, do_prep=True):
    pe = p.chan("pe")
    rctx = ExitStack()
    rows = Rows(p, rctx)
    if do_prep:
        em = Emit(p); pass_wprep(p, em, g, "wg", W["L1_ffn_w_gate"], scr["wgT"], H_L1, D, 1.0 / 127.0)
        em = Emit(p); pass_wprep(p, em, g, "wu", W["L1_ffn_w_up"], scr["wuT"], H_L1, D, 1.0 / 127.0)
        em = Emit(p); pass_wprep(p, em, g, "wd", W["L1_ffn_w_down"], scr["wdT"], D, H_L1, 0.5 / 127.0)
    em = Emit(p); pass_stats(p, em, g, r_in, rows.sum, rows.sq)
    em = Emit(p)
    em.wait("vector", pe, pe.n)
    t_ln = rows_ln_full(p, em, k1, rows.sum, rows.sq, rows.m, rows.var, rows.sinv)
    em.replay()
    rw = [(p.chan("scalar"), t_ln)]
    em = Emit(p); pass_xn(p, em, g, "xnf", r_in, scr["xnb"], BF16, rows.m, rows.sinv, row_waits=rw)
    rctx.close()
    em = Emit(p); pass_l1ffn(p, em, g, "ff", scr["xnb"], scr["wgT"], scr["wuT"], scr["wdT"], r_in, r_out)


def emit_l1attn_app(p, g, k1, rows_unused, hc, W, r_in, r_out, scr, do_prep=True):
    pe = p.chan("pe")
    rctx = ExitStack()
    rows = Rows(p, rctx)
    if do_prep:
        for nm, wt in [("pq", "L1_attn_Wq"), ("pk", "L1_attn_Wk"), ("pv", "L1_attn_Wv"), ("po", "L1_attn_Wo")]:
            em = Emit(p); pass_wprep(p, em, g, nm, W[wt], scr[nm.replace("p", "w") + "T"], D, D, 1.0)
    em = Emit(p); pass_stats(p, em, g, r_in, rows.sum, rows.sq)
    em = Emit(p)
    em.wait("vector", pe, pe.n)
    t_ln = rows_ln_full(p, em, k1, rows.sum, rows.sq, rows.m, rows.var, rows.sinv)
    em.replay()
    rw = [(p.chan("scalar"), t_ln)]
    em = Emit(p); pass_xn(p, em, g, "xnw", r_in, scr["xnb"], BF16, rows.m, rows.sinv, row_waits=rw)
    rctx.close()
    em = Emit(p); pass_linear(p, em, g, "wq", scr["xnb"], D, scr["wqT"], D, wsrc="bf16T", out_dram=scr["qT"])
    em = Emit(p); pass_linear(p, em, g, "wk", scr["xnb"], D, scr["wkT"], D, wsrc="bf16T", out_dram=scr["kTf"])
    em = Emit(p); pass_linear_tm(p, em, g, "wv", scr["xnb"], D, scr["wvT"], D, scr["vtmf"], wsrc="bf16T")
    em = Emit(p); pass_att(p, em, g, "wa", ATT_WIN, scr["qT"], scr["kTf"], scr["vtmf"], scr["ao"])
    em = Emit(p); pass_linear(p, em, g, "wo", scr["ao"], D, scr["woT"], D, wsrc="bf16T",
                              epilogue=("res", r_in, r_out, 1.0))



WEIGHT_SHAPES = {
    "L0_attn_Wqkv": ([D, 3 * D], F32), "L0_attn_bqkv": ([3 * D], F32),
    "L0_attn_Wproj": ([D, D], F32), "L0_attn_bproj": ([D], F32),
    "L0_attn_gamma": ([1, 1], F32),
    "L0_ffn_norm1_w": ([D], F32), "L0_ffn_Wq": ([D, D], F32),
    "L0_ffn_Wk": ([D, 256], F32), "L0_ffn_Wv": ([D, 256], F32),
    "L0_ffn_Wo": ([D, D], F32), "L0_ffn_norm2_w": ([D], F32),
    "L0_ffn_router_w": ([D, 8], F32), "L0_ffn_router_b": ([8], F32),
    "L0_ffn_W1": ([NEXP, D, F_FF], F32), "L0_ffn_b1": ([NEXP, F_FF], F32),
    "L0_ffn_W2": ([NEXP, D, F_FF], F32), "L0_ffn_b2": ([NEXP, F_FF], F32),
    "L0_ffn_W3": ([NEXP, F_FF, D], F32), "L0_ffn_b3": ([NEXP, D], F32),
    "L1_attn_Wq": ([D, D], F32), "L1_attn_Wk": ([D, D], F32),
    "L1_attn_Wv": ([D, D], F32), "L1_attn_Wo": ([D, D], F32),
    "L1_ffn_w_gate": ([H_L1, D], I32), "L1_ffn_w_up": ([H_L1, D], I32),
    "L1_ffn_w_down": ([D, H_L1], I32),
}


def declare_weights(nc, names):
    return {n: nc.dram_tensor(n, WEIGHT_SHAPES[n][0], WEIGHT_SHAPES[n][1], kind="ExternalInput")
            for n in names}


def _prog_common(consts):
    nc = bass.Bass("TRN2", target_bir_lowering=False)
    hc = declare_constants(nc, consts)
    return nc, hc


def build_prog_t_in():
    consts = GLOBAL_CONSTS
    nc, hc = _prog_common(consts)
    x = nc.dram_tensor("x_in", [SPC, SEQ, D], F32, kind="ExternalInput")
    r0 = nc.dram_tensor("r_out", [DCH, 128, T], F32, kind="ExternalOutput")
    with ExitStack() as ctx:
        p = Prog(nc, ctx)
        g = load_globals(p, hc)
        em = Emit(p); pass_t_in(p, em, g, x, r0)
    return nc, consts, ["x_in"]


def build_prog_t_out():
    consts = GLOBAL_CONSTS
    nc, hc = _prog_common(consts)
    r = nc.dram_tensor("r_in", [DCH, 128, T], F32, kind="ExternalInput")
    y = nc.dram_tensor("y_out", [SPC, SEQ, D], F32, kind="ExternalOutput")
    with ExitStack() as ctx:
        p = Prog(nc, ctx)
        g = load_globals(p, hc)
        em = Emit(p); pass_t_out(p, em, g, r, y)
    return nc, consts, ["r_in"]


def build_prog_l0ffn():
    consts = GLOBAL_CONSTS + ["rope_cos", "rope_sin"]
    nc, hc = _prog_common(consts)
    wn = [n for n in WEIGHT_SHAPES if n.startswith("L0_ffn_")]
    W = declare_weights(nc, wn)
    r_in = nc.dram_tensor("r_in", [DCH, 128, T], F32, kind="ExternalInput")
    r_out = nc.dram_tensor("r_out", [DCH, 128, T], F32, kind="ExternalOutput")
    scr = {
        "xn0h": nc.dram_tensor("s_xn0h", [DCH, 128, T], F32, kind="Internal"),
        "xr": nc.dram_tensor("s_xr", [DCH, 128, T], BF16, kind="Internal"),
        "qT": nc.dram_tensor("s_qT", [DCH, 128, T], BF16, kind="Internal"),
        "kT": nc.dram_tensor("s_kT", [2, 128, T], BF16, kind="Internal"),
        "vtm": nc.dram_tensor("s_vtm", [T, 256], BF16, kind="Internal"),
        "ao": nc.dram_tensor("s_ao", [DCH, 128, T], BF16, kind="Internal"),
        "h1h": nc.dram_tensor("s_h1h", [DCH, 128, T], F32, kind="Internal"),
        "xn2": nc.dram_tensor("s_xn2", [DCH, 128, T], BF16, kind="Internal"),
        "comb": nc.dram_tensor("s_comb", [8, T], F32, kind="Internal"),
        "w3b": nc.dram_tensor("s_w3b", [NEXP, FCH, 128, D], BF16, kind="Internal"),
        "h_d": nc.dram_tensor("s_hd", [NEXP, FCH, 128, T], BF16, kind="Internal"),
    }
    with ExitStack() as ctx:
        p = Prog(nc, ctx)
        k1 = Consts1(p)
        g = load_globals(p, hc)
        em = Emit(p); k1.prewarm(em); em.replay()
        emit_l0ffn_app(p, g, k1, None, hc, W, r_in, r_out, scr, "f")
    return nc, consts, ["r_in"] + wn


def build_prog_l0attn():
    consts = GLOBAL_CONSTS
    nc, hc = _prog_common(consts)
    wn = [n for n in WEIGHT_SHAPES if n.startswith("L0_attn_")]
    W = declare_weights(nc, wn)
    r_in = nc.dram_tensor("r_in", [DCH, 128, T], F32, kind="ExternalInput")
    r_out = nc.dram_tensor("r_out", [DCH, 128, T], F32, kind="ExternalOutput")
    scr = {
        "xnb": nc.dram_tensor("s_xnb", [DCH, 128, T], BF16, kind="Internal"),
        "qT": nc.dram_tensor("s_qT", [DCH, 128, T], BF16, kind="Internal"),
        "kTf": nc.dram_tensor("s_kTf", [DCH, 128, T], BF16, kind="Internal"),
        "vtmf": nc.dram_tensor("s_vtmf", [T, D], BF16, kind="Internal"),
        "ao": nc.dram_tensor("s_ao", [DCH, 128, T], BF16, kind="Internal"),
    }
    with ExitStack() as ctx:
        p = Prog(nc, ctx)
        k1 = Consts1(p)
        g = load_globals(p, hc)
        em = Emit(p); k1.prewarm(em); em.replay()
        emit_l0attn_app(p, g, k1, None, hc, W, r_in, r_out, scr)
    return nc, consts, ["r_in"] + wn


def build_prog_l1ffn():
    consts = GLOBAL_CONSTS
    nc, hc = _prog_common(consts)
    wn = ["L1_ffn_w_gate", "L1_ffn_w_up", "L1_ffn_w_down"]
    W = declare_weights(nc, wn)
    r_in = nc.dram_tensor("r_in", [DCH, 128, T], F32, kind="ExternalInput")
    r_out = nc.dram_tensor("r_out", [DCH, 128, T], F32, kind="ExternalOutput")
    scr = {
        "xnb": nc.dram_tensor("s_xnb", [DCH, 128, T], BF16, kind="Internal"),
        "wgT": nc.dram_tensor("s_wgT", [DCH, 128, H_L1], BF16, kind="Internal"),
        "wuT": nc.dram_tensor("s_wuT", [DCH, 128, H_L1], BF16, kind="Internal"),
        "wdT": nc.dram_tensor("s_wdT", [HCH, 128, D], BF16, kind="Internal"),
    }
    with ExitStack() as ctx:
        p = Prog(nc, ctx)
        k1 = Consts1(p)
        g = load_globals(p, hc)
        em = Emit(p); k1.prewarm(em); em.replay()
        emit_l1ffn_app(p, g, k1, None, hc, W, r_in, r_out, scr)
    return nc, consts, ["r_in"] + wn


def build_prog_l1attn():
    consts = GLOBAL_CONSTS
    nc, hc = _prog_common(consts)
    wn = ["L1_attn_Wq", "L1_attn_Wk", "L1_attn_Wv", "L1_attn_Wo"]
    W = declare_weights(nc, wn)
    r_in = nc.dram_tensor("r_in", [DCH, 128, T], F32, kind="ExternalInput")
    r_out = nc.dram_tensor("r_out", [DCH, 128, T], F32, kind="ExternalOutput")
    scr = {
        "xnb": nc.dram_tensor("s_xnb", [DCH, 128, T], BF16, kind="Internal"),
        "qT": nc.dram_tensor("s_qT", [DCH, 128, T], BF16, kind="Internal"),
        "kTf": nc.dram_tensor("s_kTf", [DCH, 128, T], BF16, kind="Internal"),
        "vtmf": nc.dram_tensor("s_vtmf", [T, D], BF16, kind="Internal"),
        "ao": nc.dram_tensor("s_ao", [DCH, 128, T], BF16, kind="Internal"),
        "wqT": nc.dram_tensor("s_wqT", [DCH, 128, D], BF16, kind="Internal"),
        "wkT": nc.dram_tensor("s_wkT", [DCH, 128, D], BF16, kind="Internal"),
        "wvT": nc.dram_tensor("s_wvT", [DCH, 128, D], BF16, kind="Internal"),
        "woT": nc.dram_tensor("s_woT", [DCH, 128, D], BF16, kind="Internal"),
    }
    with ExitStack() as ctx:
        p = Prog(nc, ctx)
        k1 = Consts1(p)
        g = load_globals(p, hc)
        em = Emit(p); k1.prewarm(em); em.replay()
        emit_l1attn_app(p, g, k1, None, hc, W, r_in, r_out, scr)
    return nc, consts, ["r_in"] + wn


_PROGS = {}

def get_prog(name):
    if name not in _PROGS:
        _PROGS[name] = {
            "t_in": build_prog_t_in, "t_out": build_prog_t_out,
            "l0ffn": build_prog_l0ffn, "l0attn": build_prog_l0attn,
            "l1ffn": build_prog_l1ffn, "l1attn": build_prog_l1attn,
        }[name]()
    return _PROGS[name]


def _launch(name, per_core_feeds, weights_np):
    nc, consts, innames = get_prog(name)
    cf = const_feeds(consts)
    in_maps = []
    for core in range(NCORES):
        m = dict(cf)
        for n in innames:
            if n in ("x_in", "r_in"):
                m[n] = per_core_feeds[core][n]
            else:
                m[n] = weights_np[n]
        in_maps.append(m)
    res = run_bass_kernel_spmd(nc, in_maps, core_ids=list(range(NCORES)))
    return res.results


def kernel(**inputs):
    x = np.ascontiguousarray(inputs["L0_attn_x"], dtype=np.float32)
    weights_np = {}
    for n, (shape, dt) in WEIGHT_SHAPES.items():
        if n == "L0_attn_x":
            continue
        v = np.asarray(inputs[n]) if n in inputs else None
        if n == "L0_attn_gamma":
            v = np.asarray(inputs[n]).reshape(1, 1).astype(np.float32)
        weights_np[n] = np.ascontiguousarray(v)
    shards = [np.ascontiguousarray(x[2 * i:2 * i + 2]) for i in range(NCORES)]

    feeds = [{"x_in": shards[i]} for i in range(NCORES)]
    res = _launch("t_in", feeds, weights_np)
    r = [res[i]["r_out"] for i in range(NCORES)]
    res = _launch("l0ffn", [{"r_in": r[i]} for i in range(NCORES)], weights_np)
    r = [res[i]["r_out"] for i in range(NCORES)]
    res = _launch("l0attn", [{"r_in": r[i]} for i in range(NCORES)], weights_np)
    r = [res[i]["r_out"] for i in range(NCORES)]
    res = _launch("l0ffn", [{"r_in": r[i]} for i in range(NCORES)], weights_np)
    r = [res[i]["r_out"] for i in range(NCORES)]
    res = _launch("l1ffn", [{"r_in": r[i]} for i in range(NCORES)], weights_np)
    r = [res[i]["r_out"] for i in range(NCORES)]
    res = _launch("l1attn", [{"r_in": r[i]} for i in range(NCORES)], weights_np)
    r = [res[i]["r_out"] for i in range(NCORES)]
    res = _launch("l1ffn", [{"r_in": r[i]} for i in range(NCORES)], weights_np)
    r = [res[i]["r_out"] for i in range(NCORES)]
    res = _launch("t_out", [{"r_in": r[i]} for i in range(NCORES)], weights_np)
    y = np.concatenate([res[i]["y_out"] for i in range(NCORES)], axis=0)
    return y.reshape(16, SEQ, D)


# version 04
# speedup vs baseline: 1.0934x; 1.0934x over previous
import math
from contextlib import ExitStack

import numpy as np
import ml_dtypes

import concourse.bass as bass
import concourse.mybir as mybir
from concourse.bass_utils import run_bass_kernel_spmd

F32 = mybir.dt.float32
BF16 = mybir.dt.bfloat16
I32 = mybir.dt.int32
AF = mybir.ActivationFunctionType
ALU = mybir.AluOpType
AX = mybir.AxisListType

BF = ml_dtypes.bfloat16

D = 1024
NCORES = 8
SEQ = 2048
SPC = 2
T = SPC * SEQ
DCH = D // 128
TB = 512
NTB = T // TB
NTC = T // 128
TBC = TB // 128

F_FF = 4096
FCH = F_FF // 128
NEXP = 8
H_L1 = 3072
HCH = H_L1 // 128

ENGINES = ("tensor", "vector", "scalar", "gpsimd", "sync")



class Chan:
    def __init__(self, nc, name, ctx):
        self.sem = ctx.enter_context(nc.semaphore(name))
        self.n = 0

    def take(self, step):
        self.n += step
        return self.n


class Prog:

    def __init__(self, nc, ctx):
        self.nc = nc
        self.ctx = ctx
        self.chans = {}
        self.banks = [ctx.enter_context(nc.psum_tensor(f"bank{i}", [128, 512], F32))
                      for i in range(8)]
        self.uid = 0
        self.em_last = {}

    def chan(self, name) -> Chan:
        if name not in self.chans:
            self.chans[name] = Chan(self.nc, name, self.ctx)
        return self.chans[name]

    def fresh(self, base):
        self.uid += 1
        return f"{base}_{self.uid}"


class Emit:

    def __init__(self, p: Prog):
        self.p = p
        self.streams = {e: [] for e in ENGINES}
        self.last = dict(p.em_last) if hasattr(p, "em_last") else {}
        self.hw = p.em_hw if hasattr(p, "em_hw") else {}
        p.em_hw = self.hw

    def on(self, eng, fn):
        self.streams[eng].append(fn)

    def wait(self, eng, chan: Chan, tok):
        key = (eng, id(chan))
        if self.hw.get(key, -1) >= tok:
            return
        self.hw[key] = tok
        self.on(eng, lambda e: e.wait_ge(chan.sem, tok))

    def dma(self, queue, dst, src, chan: Chan, waits=()):
        if chan.n > 0:
            self.wait(queue, chan, chan.n)
        for (c, t) in waits:
            self.wait(queue, c, t)
        tok = chan.take(16)
        self.on(queue, lambda e: e.dma_start(dst, src).then_inc(chan.sem, 16))
        return tok

    def op(self, eng, chan, fn, waits=()):
        if eng in ("vector", "scalar", "gpsimd") and eng in self.last:
            c0, t0 = self.last[eng]
            self.wait(eng, c0, t0)
        for (c, t) in waits:
            self.wait(eng, c, t)
        tok = chan.take(1) if chan is not None else None
        if chan is not None:
            self.on(eng, lambda e: fn(e).then_inc(chan.sem, 1))
            if eng in ("vector", "scalar", "gpsimd"):
                self.last[eng] = (chan, tok)
                self.p.em_last[eng] = (chan, tok)
        else:
            self.on(eng, lambda e: fn(e))
        return tok

    def replay(self):
        nc = self.p.nc
        streams = self.streams
        with nc.Block() as block:
            if streams["sync"]:
                @block.sync
                def _(e):
                    for fn in streams["sync"]:
                        fn(e)
            if streams["gpsimd"]:
                @block.gpsimd
                def _(e):
                    for fn in streams["gpsimd"]:
                        fn(e)
            if streams["tensor"]:
                @block.tensor
                def _(e):
                    for fn in streams["tensor"]:
                        fn(e)
            if streams["vector"]:
                @block.vector
                def _(e):
                    for fn in streams["vector"]:
                        fn(e)
            if streams["scalar"]:
                @block.scalar
                def _(e):
                    for fn in streams["scalar"]:
                        fn(e)



def host_constants():
    c = {}
    c["ident_f"] = np.eye(128, dtype=np.float32)
    c["ident_b"] = np.eye(128).astype(BF)
    c["ones_col_f"] = np.ones((128, 1), dtype=np.float32)
    c["ones_col_b"] = np.ones((128, 1)).astype(BF)
    c["ones_row_b"] = np.ones((1, TB)).astype(BF)
    c["ones_row_f"] = np.ones((1, TB), dtype=np.float32)
    c["ones_m_b"] = np.ones((1, 128)).astype(BF)
    c["ones_m_f"] = np.ones((1, 128), dtype=np.float32)
    half = D // 2
    theta = 1.0 / (10000.0 ** (np.arange(half, dtype=np.float64) / half))
    pos = np.arange(SEQ, dtype=np.float64)
    ang = np.outer(theta, pos)
    c["rope_cos"] = np.cos(ang).astype(np.float32)
    c["rope_sin"] = np.sin(ang).astype(np.float32)
    i = np.arange(128)
    dif = i[:, None] - i[None, :]
    c["tdiff_T"] = np.maximum(-dif, 0).astype(np.float32)
    c["tlow_T"] = (dif <= 0).astype(np.float32)
    c["bin_causal"] = (dif <= 0).astype(np.float32)
    c["bin_tri_hi"] = (dif >= 0).astype(np.float32)
    row0 = np.zeros((128, 128), np.float32); row0[0, :] = 1.0
    col0 = np.zeros((128, 128), np.float32); col0[:, 0] = 1.0
    c["bin_row0"] = row0
    c["bin_col0"] = col0
    c["bin_zero"] = np.zeros((128, 128), np.float32)
    c["bin_tri_hi_row0"] = np.maximum(c["bin_tri_hi"], row0)
    c["bin_causal_col0"] = np.maximum(c["bin_causal"], col0)
    return c


_CONST = host_constants()


def declare_constants(nc, names):
    h = {}
    for k in names:
        v = _CONST[k]
        dt = {np.dtype(np.float32): F32, np.dtype(BF): BF16, np.dtype(np.int32): I32}[v.dtype]
        h[k] = nc.dram_tensor("c_" + k, list(v.shape), dt, kind="ExternalInput")
    return h


def const_feeds(names):
    return {"c_" + k: _CONST[k] for k in names}




class Pool:

    def __init__(self, p: Prog, pctx, name, nslots, shape, dtype, queue="sync"):
        self.p = p
        self.name = name
        self.nslots = nslots
        self.queue = queue
        self.tiles = [pctx.enter_context(p.nc.sbuf_tensor(p.fresh(f"{name}s{i}"), list(shape), dtype))
                      for i in range(nslots)]
        self.lch = [p.chan(f"{name}_l{i}") for i in range(nslots)]
        self.it = 0
        self.ltok = {}
        self.rel = {}

    def slot(self, i):
        return i % self.nslots

    def tile(self, i):
        return self.tiles[self.slot(i)]

    def begin(self, em, extra_waits=()):
        i = self.it
        self.it += 1
        w = list(extra_waits)
        if i >= self.nslots:
            w.append(self.rel[i - self.nslots])
        for (c, t) in w:
            em.wait(self.queue, c, t)
        return i, self.tiles[self.slot(i)]

    def dma(self, em, i, dst, src):
        self.ltok[i] = em.dma(self.queue, dst, src, self.lch[self.slot(i)])
        return self.ltok[i]

    def tok(self, i):
        return self.ltok[i]

    def chan_of(self, i):
        return self.lch[self.slot(i)]

    def set_release(self, i, chan, tok):
        self.rel[i] = (chan, tok)


class StorePool:

    def __init__(self, p: Prog, pctx, name, nslots, shape, dtype, queue="scalar"):
        self.p = p
        self.nslots = nslots
        self.queue = queue
        self.tiles = [pctx.enter_context(p.nc.sbuf_tensor(p.fresh(f"{name}s{i}"), list(shape), dtype))
                      for i in range(nslots)]
        self.sch = [p.chan(f"{name}_t{i}") for i in range(nslots)]
        self.it = 0
        self.stok = {}

    def slot(self, i):
        return i % self.nslots

    def begin(self):
        i = self.it
        self.it += 1
        w = []
        prev = i - self.nslots
        if prev >= 0:
            w.append((self.sch[self.slot(i)], self.stok[prev]))
        return i, self.tiles[self.slot(i)], w

    def store(self, em, i, dst, src, waits=()):
        for (c, t) in waits:
            em.wait(self.queue, c, t)
        self.stok[i] = em.dma(self.queue, dst, src, self.sch[self.slot(i)])
        return self.stok[i]


def op_multi(em, eng, chan, fn, waits=(), extra_incs=()):
    for (c, t) in waits:
        em.wait(eng, c, t)
    tok = chan.take(1) if chan is not None else None
    if chan is not None:
        em.on(eng, lambda e: fn(e).then_inc(chan.sem, 1))
    else:
        em.on(eng, lambda e: fn(e))
    for (c, s) in extra_incs:
        c.take(s)
        em.on(eng, lambda e, c=c, s=s: e.sem_inc(c.sem, s))
    return tok


def quiesce_dma(p, em):
    for name, ch in list(p.chans.items()):
        if ch.n == 0:
            continue
    for name, ch in list(p.chans.items()):
        if name in ("pe", "vector", "scalar", "gpsimd"):
            continue
        if ch.n > 0:
            em.wait("sync", ch, ch.n)



GLOBAL_CONSTS = ["ident_f", "ident_b", "ones_col_f", "ones_col_b", "ones_row_b",
                 "ones_row_f", "ones_m_b", "ones_m_f", "bin_causal", "bin_zero",
                 "bin_tri_hi", "bin_row0", "bin_col0", "bin_tri_hi_row0",
                 "bin_causal_col0", "tdiff_T", "tlow_T"]


def load_globals(p: Prog, hc):
    em = Emit(p)
    out = {}
    ld = p.chan("gconst")
    tok = None
    for name in GLOBAL_CONSTS:
        if name not in hc:
            continue
        ap = hc[name].ap()
        t = p.ctx.enter_context(p.nc.sbuf_tensor(p.fresh("g" + name), list(ap.shape), ap.dtype))
        out[name] = t
        tok = em.dma("sync", t[:], ap, ld)
    em.wait("sync", ld, tok)
    em.replay()
    return out



def pass_t_in(p, em, g, x_dram, r_dram):
    nc = p.nc
    pe, dve = p.chan("pe"), p.chan("vector")
    with ExitStack() as pctx:
        lp = Pool(p, pctx, "ti_l", 2, [128, D], F32, queue="sync")
        sp = StorePool(p, pctx, "ti_o", 2, [128, D], F32, queue="scalar")
        ident = g["ident_f"]
        x_flat = x_dram.ap().rearrange("b t d -> (b t) d")
        bank_gate = [None, None]
        for tc in range(NTC):
            i, xt = lp.begin(em)
            lp.dma(em, i, xt[:], x_flat[tc * 128:(tc + 1) * 128, :])
            em.wait("tensor", lp.chan_of(i), lp.tok(i))
            par = tc % 2
            if bank_gate[par] is not None:
                em.wait("tensor", dve, bank_gate[par])
            tp_tok = None
            for c in range(DCH):
                bank = p.banks[2 * par + c // 4]
                tp_tok = em.op("tensor", pe,
                               lambda e, c=c, bank=bank, xt=xt: e.transpose(
                                   bank[:, (c % 4) * 128:(c % 4) * 128 + 128],
                                   xt[:, c * 128:(c + 1) * 128], ident[:]))
                if c == DCH - 1:
                    lp.set_release(i, pe, tp_tok)
            j, stg, gate = sp.begin()
            e0 = em.op("vector", dve, lambda e, stg=stg, par=par: e.tensor_copy(stg[:, 0:512], p.banks[2 * par][:]),
                       waits=[(pe, tp_tok)] + gate)
            e1 = em.op("vector", dve, lambda e, stg=stg, par=par: e.tensor_copy(stg[:, 512:1024], p.banks[2 * par + 1][:]))
            bank_gate[par] = e1
            sp.store(em, j, r_dram.ap()[:, :, tc * 128:(tc + 1) * 128].rearrange("c p n -> p c n"),
                     stg.ap().rearrange("p (c n) -> p c n", c=DCH), waits=[(dve, e1)])
        quiesce_dma(p, em)
        em.replay()



def pass_t_out(p, em, g, r_dram, y_dram):
    nc = p.nc
    pe, dve = p.chan("pe"), p.chan("vector")
    with ExitStack() as pctx:
        lp = Pool(p, pctx, "to_l", 2, [128, D], F32, queue="sync")
        sp = StorePool(p, pctx, "to_o", 2, [128, D], F32, queue="scalar")
        ident = g["ident_f"]
        y_flat = y_dram.ap().rearrange("b t d -> (b t) d")
        bank_gate = [None, None]
        for tc in range(NTC):
            i, ft = lp.begin(em)
            lp.dma(em, i, ft.ap().rearrange("p (c n) -> p c n", c=DCH),
                   r_dram.ap()[:, :, tc * 128:(tc + 1) * 128].rearrange("c p n -> p c n"))
            em.wait("tensor", lp.chan_of(i), lp.tok(i))
            par = tc % 2
            if bank_gate[par] is not None:
                em.wait("tensor", dve, bank_gate[par])
            tp_tok = None
            for c in range(DCH):
                bank = p.banks[2 * par + c // 4]
                tp_tok = em.op("tensor", pe,
                               lambda e, c=c, bank=bank, ft=ft: e.transpose(
                                   bank[:, (c % 4) * 128:(c % 4) * 128 + 128],
                                   ft[:, c * 128:(c + 1) * 128], ident[:]))
                if c == DCH - 1:
                    lp.set_release(i, pe, tp_tok)
            j, stg, gate = sp.begin()
            e0 = em.op("vector", dve, lambda e, stg=stg, par=par: e.tensor_copy(stg[:, 0:512], p.banks[2 * par][:]),
                       waits=[(pe, tp_tok)] + gate)
            e1 = em.op("vector", dve, lambda e, stg=stg, par=par: e.tensor_copy(stg[:, 512:1024], p.banks[2 * par + 1][:]))
            bank_gate[par] = e1
            sp.store(em, j, y_flat[tc * 128:(tc + 1) * 128, :], stg[:], waits=[(dve, e1)])
        quiesce_dma(p, em)
        em.replay()



def pass_stats(p, em, g, r_dram, sum_row, sq_row):
    nc = p.nc
    pe, dve, act = p.chan("pe"), p.chan("vector"), p.chan("scalar")
    ones_f = g["ones_col_f"]
    with ExitStack() as pctx:
        lp = Pool(p, pctx, "st_l", 2, [128, DCH, TB], F32, queue="sync")
        sq = [pctx.enter_context(nc.sbuf_tensor(p.fresh("stq"), [128, TB], F32)) for _ in range(2)]
        sq_gate = [None, None]
        row_gate = None
        for tb in range(NTB):
            i, rt = lp.begin(em)
            lp.dma(em, i, rt.ap(), r_dram.ap()[:, :, tb * TB:(tb + 1) * TB].rearrange("c p n -> p c n"))
            mm = None
            for c in range(DCH):
                k = c % 2
                wsq = [(lp.chan_of(i), lp.tok(i))]
                if sq_gate[k] is not None:
                    wsq.append((pe, sq_gate[k]))
                sq_tok = em.op("scalar", act, lambda e, c=c, k=k, rt=rt: e.activation(sq[k][:], rt[:, c, :], AF.Square),
                               waits=wsq)
                w = [(lp.chan_of(i), lp.tok(i)), (act, sq_tok)]
                if c == 0 and row_gate is not None:
                    w.append((dve, row_gate))
                em.op("tensor", pe, lambda e, c=c, rt=rt: e.matmul(
                    p.banks[6][0:1, 0:TB], ones_f[:], rt[:, c, :],
                    start=(c == 0), stop=(c == DCH - 1)), waits=w)
                mm = em.op("tensor", pe, lambda e, c=c, k=k: e.matmul(
                    p.banks[7][0:1, 0:TB], ones_f[:], sq[k][:],
                    start=(c == 0), stop=(c == DCH - 1)))
                if c == DCH - 1:
                    lp.set_release(i, pe, mm)
                sq_gate[k] = mm
            c0 = em.op("vector", dve, lambda e, tb=tb: e.tensor_copy(sum_row[0:1, tb * TB:(tb + 1) * TB], p.banks[6][0:1, 0:TB]),
                       waits=[(pe, mm)])
            row_gate = em.op("vector", dve, lambda e, tb=tb: e.tensor_copy(sq_row[0:1, tb * TB:(tb + 1) * TB], p.banks[7][0:1, 0:TB]))
        quiesce_dma(p, em)
        em.replay()



class Consts1:
    def __init__(self, p: Prog):
        self.p = p
        self.cache = {}

    def prewarm(self, em, vals=(1e-5, 1e-6)):
        for v in vals:
            self.get(em, v)

    def get(self, em, val):
        val = float(val)
        if val not in self.cache:
            t = self.p.ctx.enter_context(self.p.nc.sbuf_tensor(self.p.fresh("k1"), [1, 1], F32))
            tok = em.op("vector", self.p.chan("vector"), lambda e: e.memset(t[:], val))
            self.cache[val] = (t, tok)
        t, tok = self.cache[val]
        return t[0:1, 0:1], tok



def rows_ln(p, em, k1: Consts1, sum_row, sq_row, m_row, sinv_row):
    dve, act = p.chan("vector"), p.chan("scalar")
    b_eps, btok = k1.get(em, 1e-5)
    t1 = em.op("vector", dve, lambda e: e.tensor_scalar_mul(m_row[:], sum_row[:], 1.0 / D))
    t2 = em.op("vector", dve, lambda e: e.tensor_mul(sinv_row[:], m_row[:], m_row[:]), waits=[(dve, t1)])
    t3 = em.op("vector", dve, lambda e: e.scalar_tensor_tensor(sinv_row[:], sq_row[:], 1.0 / D, sinv_row[:],
                                                               ALU.mult, ALU.subtract), waits=[(dve, t2)])
    t4 = em.op("scalar", act, lambda e: e.activation(sinv_row[:], sinv_row[:], AF.Ln, bias=b_eps),
               waits=[(dve, t3), (dve, btok)])
    t5 = em.op("scalar", act, lambda e: e.activation(sinv_row[:], sinv_row[:], AF.Exp, scale=-0.5),
               waits=[(act, t4)])
    return t5


def rows_mul(p, em, out_row, a_row, b_row, wa, wb, scale=1.0):
    dve = p.chan("vector")
    t1 = em.op("vector", dve, lambda e: e.tensor_mul(out_row[:], a_row[:], b_row[:]), waits=list(wa) + list(wb))
    if scale != 1.0:
        t1 = em.op("vector", dve, lambda e: e.tensor_scalar_mul(out_row[:], out_row[:], scale), waits=[(dve, t1)])
    return t1


def row_to_cols(p, em, g, row_sbuf, n, out_cols, row_waits=()):
    pe, dve = p.chan("pe"), p.chan("vector")
    ident = g["ident_f"]
    tok = None
    prev = None
    for c in range(n // 128):
        w = (list(row_waits) + [(dve, dve.n)]) if c == 0 else []
        if prev is not None:
            w.append((dve, prev))
        tp = em.op("tensor", pe, lambda e, c=c: e.transpose(
            p.banks[5][0:128, 0:1], row_sbuf[0:1, c * 128:(c + 1) * 128], ident[0:1, 0:1]), waits=w)
        prev = em.op("vector", dve, lambda e, c=c: e.tensor_copy(out_cols[:, c:c + 1], p.banks[5][0:128, 0:1]),
                     waits=[(pe, tp)])
        tok = prev
    return tok



def pass_xn(p, em, g, name, r_dram, out_dram, out_dt, m_row, a_row, w_cols=None,
            row_waits=()):
    nc = p.nc
    pe, dve, act = p.chan("pe"), p.chan("vector"), p.chan("scalar")
    ones_m_f = g["ones_m_f"]
    with ExitStack() as pctx:
        lp = Pool(p, pctx, name + "_l", 2, [128, DCH, TB], F32, queue="sync")
        sp = StorePool(p, pctx, name + "_o", 2, [128, DCH, TB], out_dt, queue="scalar")
        mb = pctx.enter_context(nc.sbuf_tensor(p.fresh(name + "mb"), [128, TB], F32))
        ab = pctx.enter_context(nc.sbuf_tensor(p.fresh(name + "ab"), [128, TB], F32))
        tt = pctx.enter_context(nc.sbuf_tensor(p.fresh(name + "tt"), [128, TB], F32))
        bc_gate = None
        for tb in range(NTB):
            i, rt = lp.begin(em)
            lp.dma(em, i, rt.ap(), r_dram.ap()[:, :, tb * TB:(tb + 1) * TB].rearrange("c p n -> p c n"))
            w = list(row_waits) if tb == 0 else []
            if bc_gate is not None:
                w.append((dve, bc_gate))
            if m_row is not None:
                t_m = em.op("tensor", pe, lambda e, tb=tb: e.matmul(
                    p.banks[4][:, 0:TB], ones_m_f[0:1, :], m_row[0:1, tb * TB:(tb + 1) * TB],
                    start=True, stop=True), waits=w)
                em.op("scalar", act, lambda e: e.copy(mb[:], p.banks[4][:]), waits=[(pe, t_m)])
            t_a = em.op("tensor", pe, lambda e, tb=tb: e.matmul(
                p.banks[5][:, 0:TB], ones_m_f[0:1, :], a_row[0:1, tb * TB:(tb + 1) * TB],
                start=True, stop=True), waits=(w if m_row is None else []))
            ab_tok = em.op("scalar", act, lambda e: e.copy(ab[:], p.banks[5][:]), waits=[(pe, t_a)])
            j, stg, gate = sp.begin()
            last = None
            for c in range(DCH):
                wt = [(lp.chan_of(i), lp.tok(i)), (act, ab_tok)] + (gate if c == 0 else [])
                if m_row is not None:
                    t1 = em.op("vector", dve, lambda e, c=c, rt=rt: e.tensor_sub(tt[:], rt[:, c, :], mb[:]),
                               waits=wt)
                    src = tt
                    wt2 = [(dve, t1)]
                else:
                    src, wt2 = None, wt
                if w_cols is not None:
                    if src is None:
                        t2 = em.op("vector", dve, lambda e, c=c, rt=rt: e.tensor_mul(tt[:], rt[:, c, :], ab[:]), waits=wt2)
                    else:
                        t2 = em.op("vector", dve, lambda e: e.tensor_mul(tt[:], tt[:], ab[:]), waits=wt2)
                    last = em.op("vector", dve, lambda e, c=c, stg=stg: e.tensor_single_scalar(
                        stg[:, c, :], tt[:], w_cols[:, c:c + 1], ALU.mult), waits=[(dve, t2)])
                else:
                    if src is None:
                        last = em.op("vector", dve, lambda e, c=c, rt=rt, stg=stg: e.tensor_mul(stg[:, c, :], rt[:, c, :], ab[:]), waits=wt2)
                    else:
                        last = em.op("vector", dve, lambda e, c=c, stg=stg: e.tensor_mul(stg[:, c, :], tt[:], ab[:]), waits=wt2)
                if c == DCH - 1:
                    lp.set_release(i, dve, last)
            bc_gate = last
            sp.store(em, j, out_dram.ap()[:, :, tb * TB:(tb + 1) * TB].rearrange("c p n -> p c n"),
                     stg.ap(), waits=[(dve, last)])
        quiesce_dma(p, em)
        em.replay()



def pass_linear(p, em, g, name, xn_dram, nin, w_handle, nout, *,
                wsrc="f32", bias_row=None, bias_waits=(), w_col0=0, bias_col0=0,
                out_dram=None, epilogue=None):
    nc = p.nc
    pe, dve, act = p.chan("pe"), p.chan("vector"), p.chan("scalar")
    nin_ch = nin // 128
    noc = nout // 128
    ones_row_b = g["ones_row_b"]
    with ExitStack() as pctx:
        wp = Pool(p, pctx, name + ("_wf" if wsrc == "f32" else "_wb"), 2, [128, nin_ch, 128], BF16,
                  queue=("gpsimd" if wsrc == "f32" else "sync"))
        xp = Pool(p, pctx, name + "_x", 2, [128, nin_ch, TB], BF16, queue="sync")
        if epilogue is None:
            sp = StorePool(p, pctx, name + "_o", 2, [128, TB], BF16, queue="scalar")
        else:
            sp = StorePool(p, pctx, name + "_o", 2, [128, TB], F32, queue="scalar")
            ap_ = Pool(p, pctx, name + "_a", 2, [128, TB], F32, queue="sync")
            ev = pctx.enter_context(nc.sbuf_tensor(p.fresh(name + "ev"), [128, TB], F32))
            ev_gate = None
        if epilogue is not None:
            kind, add_dram, res_dram, scale = epilogue
        bank_gate = {0: None, 1: None}
        for oc in range(noc):
            iw, wt = wp.begin(em)
            if wsrc == "f32":
                src = w_handle.ap().rearrange("(i p) o -> p i o", p=128)[:, :, w_col0 + oc * 128:w_col0 + (oc + 1) * 128]
            else:
                src = w_handle.ap()[:, :, w_col0 + oc * 128:w_col0 + (oc + 1) * 128].rearrange("i p o -> p i o")
            wp.dma(em, iw, wt.ap(), src)
            last_mm = None
            for tb in range(NTB):
                ix, xt = xp.begin(em)
                xp.dma(em, ix, xt.ap(), xn_dram.ap()[:, :, tb * TB:(tb + 1) * TB].rearrange("c p n -> p c n"))
                bk = tb % 2
                bank = p.banks[bk]
                w0 = [(wp.chan_of(iw), wp.tok(iw)), (xp.chan_of(ix), xp.tok(ix))]
                if bank_gate[bk] is not None:
                    w0.append(bank_gate[bk])
                mm = None
                for i in range(nin_ch):
                    mm = em.op("tensor", pe, lambda e, i=i, wt=wt, xt=xt, bank=bank: e.matmul(
                        bank[:], wt[:, i, :], xt[:, i, :],
                        start=(i == 0), stop=(i == nin_ch - 1 and bias_row is None)),
                        waits=(w0 if i == 0 else []))
                if bias_row is not None:
                    mm = em.op("tensor", pe, lambda e, oc=oc, bank=bank: e.matmul(
                        bank[:], bias_row[0:1, bias_col0 + oc * 128:bias_col0 + (oc + 1) * 128],
                        ones_row_b[0:1, :], start=False, stop=True),
                        waits=(list(bias_waits) if (oc == 0 and tb == 0) else []))
                xp.set_release(ix, pe, mm)
                j, stg, gate = sp.begin()
                if epilogue is None:
                    e1 = em.op("scalar", act, lambda e, stg=stg, bank=bank: e.copy(stg[:], bank[:]),
                               waits=[(pe, mm)] + gate)
                    bank_gate[bk] = (act, e1)
                    sp.store(em, j, out_dram.ap()[oc, :, tb * TB:(tb + 1) * TB], stg.ap(),
                             waits=[(act, e1)])
                else:
                    ia, at = ap_.begin(em)
                    ap_.dma(em, ia, at.ap(), add_dram.ap()[oc, :, tb * TB:(tb + 1) * TB])
                    w1 = [(pe, mm)]
                    if ev_gate is not None:
                        w1.append((dve, ev_gate))
                    e1 = em.op("scalar", act, lambda e, bank=bank: e.activation(ev[:], bank[:], AF.Copy, scale=float(scale)),
                               waits=w1)
                    bank_gate[bk] = (act, e1)
                    e2 = em.op("vector", dve, lambda e, stg=stg, at=at: e.tensor_add(stg[:], ev[:], at[:]),
                               waits=[(act, e1), (ap_.chan_of(ia), ap_.tok(ia))] + gate)
                    ap_.set_release(ia, dve, e2)
                    ev_gate = e2
                    sp.store(em, j, res_dram.ap()[oc, :, tb * TB:(tb + 1) * TB], stg.ap(),
                             waits=[(dve, e2)])
                last_mm = mm
            wp.set_release(iw, pe, last_mm)
        quiesce_dma(p, em)
        em.replay()



def pass_linear_tm(p, em, g, name, xn_dram, nin, w_handle, nout, out_dram, *,
                   wsrc="f32", bias_row=None, bias_waits=(), w_col0=0, bias_col0=0):
    nc = p.nc
    pe, dve, act = p.chan("pe"), p.chan("vector"), p.chan("scalar")
    nin_ch = nin // 128
    ncb = (nout + 511) // 512
    ones_m_b = g["ones_m_b"]
    with ExitStack() as pctx:
        wt = pctx.enter_context(nc.sbuf_tensor(p.fresh(name + "w"), [128, nin_ch, nout], BF16))
        wch = p.chan(name + ("_wf0" if wsrc == "f32" else "_wb0"))
        if wsrc == "f32":
            wtok = em.dma("gpsimd", wt.ap(),
                          w_handle.ap().rearrange("(i p) o -> p i o", p=128)[:, :, w_col0:w_col0 + nout], wch)
        else:
            wtok = em.dma("sync", wt.ap(),
                          w_handle.ap()[:, :, w_col0:w_col0 + nout].rearrange("i p o -> p i o"), wch)
        xp = Pool(p, pctx, name + "_x", 2, [128, nin_ch, 128], BF16, queue="sync")
        sp = StorePool(p, pctx, name + "_o", 2, [128, nout], BF16, queue="scalar")
        bank_gate = {0: None, 1: None}
        for tc in range(NTC):
            ix, xt = xp.begin(em)
            xp.dma(em, ix, xt.ap(), xn_dram.ap()[:, :, tc * 128:(tc + 1) * 128].rearrange("c p n -> p c n"))
            j, stg, gate = sp.begin()
            last_e = None
            for cb in range(ncb):
                n0, n1 = cb * 512, min((cb + 1) * 512, nout)
                bk = 2 + cb % 2
                bank = p.banks[bk]
                w0 = [(xp.chan_of(ix), xp.tok(ix)), (wch, wtok)]
                if bank_gate[cb % 2] is not None:
                    w0.append(bank_gate[cb % 2])
                mm = None
                for i in range(nin_ch):
                    mm = em.op("tensor", pe, lambda e, i=i, n0=n0, n1=n1, bank=bank, xt=xt: e.matmul(
                        bank[0:128, 0:n1 - n0], xt[:, i, :], wt[:, i, n0:n1],
                        start=(i == 0), stop=(i == nin_ch - 1 and bias_row is None)),
                        waits=(w0 if i == 0 else []))
                if bias_row is not None:
                    mm = em.op("tensor", pe, lambda e, n0=n0, n1=n1, bank=bank: e.matmul(
                        bank[0:128, 0:n1 - n0], ones_m_b[0:1, :], bias_row[0:1, bias_col0 + n0:bias_col0 + n1],
                        start=False, stop=True),
                        waits=(list(bias_waits) if tc == 0 and cb == 0 else []))
                if cb == ncb - 1:
                    xp.set_release(ix, pe, mm)
                e1 = em.op("scalar", act, lambda e, stg=stg, bank=bank, n0=n0, n1=n1: e.copy(
                    stg[:, n0:n1], bank[0:128, 0:n1 - n0]),
                    waits=[(pe, mm)] + (gate if cb == 0 else []))
                bank_gate[cb % 2] = (act, e1)
                last_e = e1
            sp.store(em, j, out_dram.ap()[tc * 128:(tc + 1) * 128, :], stg.ap(), waits=[(act, last_e)])
        quiesce_dma(p, em)
        em.replay()



def pass_att(p, em, g, name, spec, q_dram, k_dram, v_dram, out_dram, gamma_dram=None):
    nc = p.nc
    pe, dve, act = p.chan("pe"), p.chan("vector"), p.chan("scalar")
    nh, dk, dv = spec["nh"], spec["dk"], spec["dv"]
    scale = spec["scale"]
    kch_n, qch_n = spec["kch"], spec["qch"]
    dv_total = spec["dv_total"]
    NKC = SEQ // 128
    NQB = SEQ // TB
    NTB_S = SEQ // TB
    ones_col_b = g["ones_col_b"]
    ones_m_f = g["ones_m_f"]
    full_q = dk > 128
    with ExitStack() as pctx:
        kp = Pool(p, pctx, name + "_k", 1, [128, kch_n, SEQ], BF16, queue="sync")
        vp = Pool(p, pctx, name + "_v", 1, [128, NKC, dv_total], BF16, queue="sync")
        if full_q:
            qp = Pool(p, pctx, name + "_q", 2, [128, qch_n, TB], BF16, queue="sync")
        else:
            qp = Pool(p, pctx, name + "_q", 2, [128, SEQ], BF16, queue="sync")
        P_sb = pctx.enter_context(nc.sbuf_tensor(p.fresh(name + "P"), [128, NKC, TB], BF16))
        qn2 = pctx.enter_context(nc.sbuf_tensor(p.fresh(name + "qn2"), [1, SEQ], F32))
        negb = pctx.enter_context(nc.sbuf_tensor(p.fresh(name + "nb"), [1, SEQ], F32))
        kn2 = pctx.enter_context(nc.sbuf_tensor(p.fresh(name + "kn2"), [1, SEQ], F32))
        mk = pctx.enter_context(nc.sbuf_tensor(p.fresh(name + "mk"), [1, 16], F32))
        sqt = pctx.enter_context(nc.sbuf_tensor(p.fresh(name + "sqt"), [128, TB], BF16))
        nb_b = pctx.enter_context(nc.sbuf_tensor(p.fresh(name + "nbb"), [128, TB], F32))
        rs_row = pctx.enter_context(nc.sbuf_tensor(p.fresh(name + "rs"), [1, TB], F32))
        rb = pctx.enter_context(nc.sbuf_tensor(p.fresh(name + "rb"), [128, TB], F32))
        dtmp = pctx.enter_context(nc.sbuf_tensor(p.fresh(name + "dt"), [128, 128], F32))
        corr = pctx.enter_context(nc.sbuf_tensor(p.fresh(name + "cr"), [128, 128], F32))
        sp = StorePool(p, pctx, name + "_o", 2, [128, TB], BF16, queue="scalar")
        b6_gate = None
        sq_gate = None
        nbb_gate = None
        rb_gate = None
        P_gate = {}
        bank_gate = {0: None, 1: None}
        av_gate = {0: None, 1: None}

        def stat_row(src_fn, n_acc, out_row, tb):
            nonlocal b6_gate, sq_gate
            mm = None
            for c in range(n_acc):
                ap_, rows, base, extra_w = src_fn(c)
                wsq = list(extra_w)
                if sq_gate is not None:
                    wsq.append(sq_gate)
                s1 = em.op("scalar", act, lambda e, ap_=ap_, rows=rows, base=base: e.activation(
                    sqt[base:base + rows, 0:TB], ap_, AF.Square), waits=wsq)
                w = [(act, s1)]
                if c == 0 and b6_gate is not None:
                    w.append(b6_gate)
                mm = em.op("tensor", pe, lambda e, rows=rows, base=base, c=c: e.matmul(
                    p.banks[6][0:1, 0:TB], ones_col_b[base:base + rows, 0:1], sqt[base:base + rows, 0:TB],
                    start=(c == 0), stop=(c == n_acc - 1)), waits=w)
                sq_gate = (pe, mm)
            ev = em.op("vector", dve, lambda e, tb=tb: e.tensor_copy(out_row[0:1, tb * TB:(tb + 1) * TB],
                                                                     p.banks[6][0:1, 0:TB]), waits=[(pe, mm)])
            b6_gate = (dve, ev)
            return ev

        if spec.get("decay"):
            Dt = pctx.enter_context(nc.sbuf_tensor(p.fresh(name + "Dt"), [128, 128], F32))
            Dm1 = pctx.enter_context(nc.sbuf_tensor(p.fresh(name + "Dm"), [128, 128], F32))
            grow = pctx.enter_context(nc.sbuf_tensor(p.fresh(name + "g"), [1, 1], F32))
            gcol = pctx.enter_context(nc.sbuf_tensor(p.fresh(name + "gc"), [128, 1], F32))
            gch = p.chan(name + "_g0")
            gap = gamma_dram.ap()
            if gap.ndim == 0:
                gap = gap.rearrange("-> 1 1")
            elif gap.ndim == 1:
                gap = gap.rearrange("x -> 1 x")
            gtok = em.dma("sync", grow[:], gap, gch)
            t1 = em.op("scalar", act, lambda e: e.activation(grow[:], grow[:], AF.Ln), waits=[(gch, gtok)])
            t2 = em.op("tensor", pe, lambda e: e.matmul(p.banks[4][0:128, 0:1], ones_m_f[0:1, :], grow[0:1, 0:1],
                                                        start=True, stop=True), waits=[(act, t1)])
            t3 = em.op("vector", dve, lambda e: e.tensor_copy(gcol[:], p.banks[4][0:128, 0:1]), waits=[(pe, t2)])
            t4 = em.op("vector", dve, lambda e: e.tensor_single_scalar(Dt[:], g["tdiff_T"][:], gcol[:, 0:1], ALU.mult))
            t5 = em.op("scalar", act, lambda e: e.activation(Dt[:], Dt[:], AF.Exp), waits=[(dve, t4)])
            t6 = em.op("vector", dve, lambda e: e.tensor_mul(Dt[:], Dt[:], g["tlow_T"][:]), waits=[(act, t5)])
            decay_tok = em.op("vector", dve, lambda e: e.tensor_scalar(Dm1[:], Dt[:], -1.0, 1.0, ALU.mult, ALU.add))

        kranges = sorted({spec["k_loc"](h) for h in range(nh)})
        krange_idx = {kr: i for i, kr in enumerate(kranges)}

        for s in range(SPC):
            ik, k_sb = kp.begin(em)
            kp.dma(em, ik, k_sb.ap(), k_dram.ap()[:, :, s * SEQ:(s + 1) * SEQ].rearrange("c p t -> p c t"))
            ktok = (kp.chan_of(ik), kp.tok(ik))
            iv, v_sb = vp.begin(em)
            vp.dma(em, iv, v_sb.ap(),
                   v_dram.ap()[s * SEQ:(s + 1) * SEQ, :].rearrange("(a p) d -> p a d", p=128))
            vtok = (vp.chan_of(iv), vp.tok(iv))
            last_av = None
            mk_toks = {}
            for gi, (kc_ch, kc_off, krows) in enumerate(kranges):
                for tb in range(NTB_S):
                    stat_row(lambda c, kc_ch=kc_ch, kc_off=kc_off, krows=krows, tb=tb: (
                        k_sb[kc_off:kc_off + krows, kc_ch, tb * TB:(tb + 1) * TB], krows, kc_off,
                        [ktok]), 1, kn2, tb)
                s2 = em.op("scalar", act, lambda e: e.activation(kn2[:], kn2[:], AF.Ln), waits=[b6_gate])
                s3 = em.op("scalar", act, lambda e: e.activation(kn2[:], kn2[:], AF.Exp, scale=0.5))
                tok = em.op("vector", dve, lambda e, gi=gi: e.reduce_max(
                    mk[0:1, gi:gi + 1], kn2[0:1, :], axis=AX.X), waits=[(act, s3)])
                mk_toks[gi] = tok
            for h in range(nh):
                q_ch, q_off = spec["q_loc"](h)
                kc_ch, kc_off, krows = spec["k_loc"](h)
                gi = krange_idx[spec["k_loc"](h)]
                vcol = spec["v_col"](h)
                o_ch, o_off = spec["out_loc"](h)
                if not full_q:
                    iq, qt_h = qp.begin(em)
                    qp.dma(em, iq, qt_h[kc_off:kc_off + dk, :],
                           q_dram.ap()[q_ch, q_off:q_off + dk, s * SEQ:(s + 1) * SEQ])
                    q_tok = (qp.chan_of(iq), qp.tok(iq))
                    for tb in range(NTB_S):
                        stat_row(lambda c, tb=tb: (qt_h[kc_off:kc_off + dk, tb * TB:(tb + 1) * TB],
                                                   dk, kc_off, [q_tok]), 1, qn2, tb)
                else:
                    for tb in range(NTB_S):
                        iqb, qt_b = qp.begin(em)
                        qp.dma(em, iqb, qt_b.ap(),
                               q_dram.ap()[:, :, s * SEQ + tb * TB:s * SEQ + (tb + 1) * TB].rearrange("c p t -> p c t"))
                        stat_row(lambda c, qt_b=qt_b, iqb=iqb: (qt_b[:, c, :], 128, 0,
                                                                [(qp.chan_of(iqb), qp.tok(iqb))]),
                                 qch_n, qn2, tb)
                        qp.set_release(iqb, pe, sq_gate[1])
                s2 = em.op("scalar", act, lambda e: e.activation(qn2[:], qn2[:], AF.Ln), waits=[b6_gate])
                s3 = em.op("scalar", act, lambda e: e.activation(qn2[:], qn2[:], AF.Exp, scale=0.5))
                nb_last = em.op("vector", dve, lambda e, gi=gi: e.tensor_scalar(
                    negb[0:1, :], qn2[0:1, :],
                    mk[0:1, gi:gi + 1], -1.0, ALU.mult, ALU.mult),
                    waits=[(act, s3), (dve, mk_toks[gi])])
                last_qk = None
                for qbl in range(NQB):
                    qb = s * NQB + qbl
                    blocks = spec["blocks"](qbl)
                    ncs = len(blocks)
                    if full_q:
                        iqb, qt_b = qp.begin(em)
                        qp.dma(em, iqb, qt_b.ap(),
                               q_dram.ap()[:, :, qb * TB:(qb + 1) * TB].rearrange("c p t -> p c t"))
                        q_tok = (qp.chan_of(iqb), qp.tok(iqb))
                        qt_cur = qt_b
                    if spec.get("decay"):
                        w = [(dve, nb_last)] + ([] if nbb_gate is None else [nbb_gate])
                        tb_ = em.op("tensor", pe, lambda e, qbl=qbl: e.matmul(
                            p.banks[4][:, 0:TB], ones_m_f[0:1, :], negb[0:1, qbl * TB:(qbl + 1) * TB],
                            start=True, stop=True), waits=w)
                        nbb_tok = em.op("scalar", act, lambda e: e.copy(nb_b[:], p.banks[4][:]), waits=[(pe, tb_)])
                        nbb_gate = (act, nbb_tok)
                    for j, (kc, masks) in enumerate(blocks):
                        bk = j % 2
                        bank = p.banks[bk]
                        w0 = [ktok, q_tok]
                        if bank_gate[bk] is not None:
                            w0.append(bank_gate[bk])
                        if full_q:
                            for c in range(qch_n):
                                mm = em.op("tensor", pe, lambda e, c=c, kc=kc, bank=bank, qt_cur=qt_cur: e.matmul(
                                    bank[:], k_sb[:, c, kc * 128:(kc + 1) * 128], qt_cur[:, c, :],
                                    start=(c == 0), stop=False),
                                    waits=(w0 if c == 0 else []))
                        else:
                            qsl = qt_h[kc_off:kc_off + dk, qbl * TB:(qbl + 1) * TB]
                            ksl = k_sb[kc_off:kc_off + dk, kc_ch, kc * 128:(kc + 1) * 128]
                            mm = em.op("tensor", pe, lambda e, bank=bank, ksl=ksl, qsl=qsl: e.matmul(
                                bank[:], ksl, qsl, start=True, stop=False), waits=w0)
                        mm = em.op("tensor", pe, lambda e, qbl=qbl, bank=bank: e.matmul(
                            bank[:], ones_m_f[0:1, :], negb[0:1, qbl * TB:(qbl + 1) * TB],
                            start=False, stop=True), waits=[(dve, nb_last)])
                        last_qk = mm
                        e_w = [(pe, mm)]
                        if j in P_gate:
                            e_w.append(P_gate[j])
                        ex = em.op("scalar", act, lambda e, j=j, bank=bank: e.activation(
                            P_sb[:, j, :], bank[:], AF.Exp, scale=scale), waits=e_w)
                        m_tok = (act, ex)
                        bank_gate[bk] = (act, ex)
                        for (ql, mname) in masks:
                            if mname == "decay":
                                d1 = em.op("vector", dve, lambda e, ql=ql, bank=bank: e.tensor_mul(
                                    dtmp[:], bank[:, ql * 128:(ql + 1) * 128], Dt[:]),
                                    waits=[(pe, mm), (dve, decay_tok), (act, ex)])
                                c1 = em.op("vector", dve, lambda e, ql=ql: e.tensor_mul(
                                    corr[:], Dm1[:], nb_b[:, ql * 128:(ql + 1) * 128]),
                                    waits=[(act, nbb_tok)])
                                c2 = em.op("vector", dve, lambda e: e.tensor_add(dtmp[:], dtmp[:], corr[:]))
                                mx = em.op("scalar", act, lambda e, j=j, ql=ql: e.activation(
                                    P_sb[:, j, ql * 128:(ql + 1) * 128], dtmp[:], AF.Exp, scale=scale),
                                    waits=[(dve, c2)])
                                m_tok = (act, mx)
                                bank_gate[bk] = (dve, d1)
                            else:
                                mx = em.op("vector", dve, lambda e, j=j, ql=ql, mname=mname: e.tensor_mul(
                                    P_sb[:, j, ql * 128:(ql + 1) * 128], P_sb[:, j, ql * 128:(ql + 1) * 128],
                                    g[mname][:]), waits=[(act, ex)])
                                m_tok = (dve, mx)
                        w = [m_tok]
                        if j == 0 and b6_gate is not None:
                            w.append(b6_gate)
                        cs_tok = em.op("tensor", pe, lambda e, j=j, sp=(j == ncs - 1): e.matmul(
                            p.banks[6][0:1, 0:TB], ones_col_b[:, 0:1], P_sb[:, j, :],
                            start=(j == 0), stop=sp), waits=w)
                    if full_q:
                        qp.set_release(iqb, pe, last_qk)
                    r1 = em.op("vector", dve, lambda e: e.tensor_copy(rs_row[:], p.banks[6][0:1, 0:TB]),
                               waits=[(pe, cs_tok)])
                    b6_gate = (dve, r1)
                    r2 = em.op("vector", dve, lambda e: e.reciprocal(rs_row[:], rs_row[:]))
                    w = [(dve, r2)] + ([] if rb_gate is None else [rb_gate])
                    if spec.get("decay"):
                        w.append((act, nbb_tok))
                    r3 = em.op("tensor", pe, lambda e: e.matmul(p.banks[4][:, 0:TB], ones_m_f[0:1, :], rs_row[0:1, :],
                                                                start=True, stop=True), waits=w)
                    r4 = em.op("scalar", act, lambda e: e.copy(rb[:], p.banks[4][:]), waits=[(pe, r3)])
                    ndc = (dv + 127) // 128
                    for dc in range(ndc):
                        c0 = vcol + dc * 128
                        m = min(128, dv - dc * 128)
                        bk = dc % 2
                        bank = p.banks[2 + bk]
                        w0 = [vtok]
                        if av_gate[bk] is not None:
                            w0.append(av_gate[bk])
                        for j, (kc, masks) in enumerate(blocks):
                            mm = em.op("tensor", pe, lambda e, j=j, kc=kc, c0=c0, m=m, bank=bank, sp=(j == ncs - 1): e.matmul(
                                bank[0:m, 0:TB], v_sb[:, kc, c0:c0 + m], P_sb[:, j, :],
                                start=(j == 0), stop=sp),
                                waits=(w0 if j == 0 else []))
                            if dc == ndc - 1:
                                P_gate[j] = (pe, mm)
                        last_av = mm
                        j2, stg, gate = sp.begin()
                        e1 = em.op("vector", dve, lambda e, m=m, bank=bank, stg=stg: e.tensor_mul(
                            stg[0:m, :], bank[0:m, 0:TB], rb[0:m, :]),
                            waits=[(pe, mm), (act, r4)] + gate)
                        av_gate[bk] = (dve, e1)
                        rb_gate = (dve, e1)
                        if dv <= 128:
                            dst = out_dram.ap()[o_ch, o_off:o_off + m, qb * TB:(qb + 1) * TB]
                        else:
                            dst = out_dram.ap()[dc, :, qb * TB:(qb + 1) * TB]
                        sp.store(em, j2, dst, stg[0:m, :], waits=[(dve, e1)])
                if not full_q:
                    qp.set_release(iq, pe, last_qk)
            kp.set_release(ik, pe, last_qk)
            vp.set_release(iv, pe, last_av)
        quiesce_dma(p, em)
        em.replay()



def blocks_causal(qbl):
    out = []
    hi = 4 * qbl + 3
    for kc in range(hi + 1):
        masks = []
        for ql in range(TBC):
            qc = 4 * qbl + ql
            if kc == qc:
                masks.append((ql, "bin_causal"))
            elif kc > qc:
                masks.append((ql, "bin_zero"))
        out.append((kc, masks))
    return out


def blocks_full_decay(qbl):
    out = []
    for kc in range(SEQ // 128):
        masks = []
        for ql in range(TBC):
            qc = 4 * qbl + ql
            if kc == qc:
                masks.append((ql, "decay"))
        out.append((kc, masks))
    return out


def blocks_window(qbl):
    qcs = [4 * qbl + i for i in range(TBC)]
    kcs = sorted(set([0] + [kc for qc in qcs for kc in (qc - 1, qc, qc + 1)
                            if 0 <= kc < SEQ // 128]))
    if qbl == 0:
        kcs = list(range(SEQ // 128))
    out = []
    for kc in kcs:
        masks = []
        for ql in range(TBC):
            qc = 4 * qbl + ql
            d = kc - qc
            if d == 0:
                continue
            elif d == 1:
                masks.append((ql, "bin_causal_col0" if qc == 0 else "bin_causal"))
            elif d == -1:
                if kc == 0:
                    masks.append((ql, "bin_tri_hi_row0"))
                else:
                    masks.append((ql, "bin_tri_hi"))
            else:
                if kc == 0:
                    masks.append((ql, "bin_row0"))
                elif qc == 0:
                    masks.append((ql, "bin_col0"))
                else:
                    masks.append((ql, "bin_zero"))
        out.append((kc, masks))
    return out


def pass_rope(p, em, g, name, r_dram, xr_dram, m_row, a_row, w1_dram, hc, row_waits=()):
    nc = p.nc
    pe, dve, act = p.chan("pe"), p.chan("vector"), p.chan("scalar")
    ones_m_f = g["ones_m_f"]
    HFCH = 4
    with ExitStack() as pctx:
        cos_sb = pctx.enter_context(nc.sbuf_tensor(p.fresh(name + "cos"), [128, HFCH, SEQ], F32))
        sin_sb = pctx.enter_context(nc.sbuf_tensor(p.fresh(name + "sin"), [128, HFCH, SEQ], F32))
        tch, tch2, wch = p.chan(name + "_t"), p.chan(name + "_t2"), p.chan(name + "_w")
        ctok = em.dma("sync", cos_sb.ap(), hc["rope_cos"].ap().rearrange("(c p) t -> p c t", p=128), tch)
        stok = em.dma("sync", sin_sb.ap(), hc["rope_sin"].ap().rearrange("(c p) t -> p c t", p=128), tch2)
        w1row = pctx.enter_context(nc.sbuf_tensor(p.fresh(name + "w1r"), [1, D], F32))
        wsel = pctx.enter_context(nc.sbuf_tensor(p.fresh(name + "wsel"), [1, D], F32))
        wcols = pctx.enter_context(nc.sbuf_tensor(p.fresh(name + "wc"), [128, 2 * HFCH], F32))
        wtok = em.dma("sync", w1row[:], w1_dram.ap().rearrange("(a d) -> a d", a=1), wch)
        c1 = em.op("vector", dve, lambda e: e.tensor_copy(wsel[0:1, 0:512], w1row[0:1, 0:D:2]), waits=[(wch, wtok)])
        c2 = em.op("vector", dve, lambda e: e.tensor_copy(wsel[0:1, 512:1024], w1row[0:1, 1:D:2]))
        wc_tok = row_to_cols(p, em, g, wsel, D, wcols, row_waits=[(dve, c2)])
        lp = Pool(p, pctx, name + "_l", 2, [128, TB], F32, queue="sync")
        lp2 = Pool(p, pctx, name + "_l2", 2, [128, TB], F32, queue="sync")
        sp = StorePool(p, pctx, name + "_o", 2, [128, 2, TB], BF16, queue="scalar")
        mb = pctx.enter_context(nc.sbuf_tensor(p.fresh(name + "mb"), [128, TB], F32))
        ab = pctx.enter_context(nc.sbuf_tensor(p.fresh(name + "ab"), [128, TB], F32))
        xe = pctx.enter_context(nc.sbuf_tensor(p.fresh(name + "xe"), [128, TB], F32))
        xo = pctx.enter_context(nc.sbuf_tensor(p.fresh(name + "xo"), [128, TB], F32))
        tt = pctx.enter_context(nc.sbuf_tensor(p.fresh(name + "tt"), [128, TB], F32))
        tt2 = pctx.enter_context(nc.sbuf_tensor(p.fresh(name + "t2"), [128, TB], F32))
        for tb in range(NTB):
            s = (tb * TB) // SEQ
            pos0 = tb * TB - s * SEQ
            w = list(row_waits) + [(dve, wc_tok)] if tb == 0 else []
            if tb > 0:
                w.append((dve, em.last["vector"][1]))
            t_m = em.op("tensor", pe, lambda e, tb=tb: e.matmul(
                p.banks[4][:, 0:TB], ones_m_f[0:1, :], m_row[0:1, tb * TB:(tb + 1) * TB],
                start=True, stop=True), waits=w)
            mtok = em.op("scalar", act, lambda e: e.copy(mb[:], p.banks[4][:]),
                         waits=[(pe, t_m)] + w)
            t_a = em.op("tensor", pe, lambda e, tb=tb: e.matmul(
                p.banks[5][:, 0:TB], ones_m_f[0:1, :], a_row[0:1, tb * TB:(tb + 1) * TB],
                start=True, stop=True), waits=(w if tb == 0 else []))
            atok = em.op("scalar", act, lambda e: e.copy(ab[:], p.banks[5][:]), waits=[(pe, t_a)])
            for c2_ in range(HFCH):
                ie, et = lp.begin(em)
                lp.dma(em, ie, et[:],
                       r_dram.ap()[2 * c2_:2 * c2_ + 2, 0:128:2, tb * TB:(tb + 1) * TB].rearrange("a p n -> (a p) n"))
                io, ot = lp2.begin(em)
                lp2.dma(em, io, ot[:],
                        r_dram.ap()[2 * c2_:2 * c2_ + 2, 1:128:2, tb * TB:(tb + 1) * TB].rearrange("a p n -> (a p) n"))
                e1 = em.op("vector", dve, lambda e, et=et: e.tensor_sub(xe[:], et[:], mb[:]),
                           waits=[(lp.chan_of(ie), lp.tok(ie)), (act, atok), (dve, wc_tok)])
                lp.set_release(ie, dve, e1)
                em.op("vector", dve, lambda e: e.tensor_mul(xe[:], xe[:], ab[:]))
                em.op("vector", dve, lambda e, c2_=c2_: e.tensor_single_scalar(xe[:], xe[:], wcols[:, c2_:c2_ + 1], ALU.mult))
                o1 = em.op("vector", dve, lambda e, ot=ot: e.tensor_sub(xo[:], ot[:], mb[:]),
                           waits=[(lp2.chan_of(io), lp2.tok(io))])
                lp2.set_release(io, dve, o1)
                em.op("vector", dve, lambda e: e.tensor_mul(xo[:], xo[:], ab[:]))
                em.op("vector", dve, lambda e, c2_=c2_: e.tensor_single_scalar(xo[:], xo[:], wcols[:, 4 + c2_:5 + c2_], ALU.mult))
                j, stg, gate = sp.begin()
                cw = [(tch, ctok), (tch2, stok)] + gate
                cs = cos_sb[:, c2_, pos0:pos0 + TB]
                sn = sin_sb[:, c2_, pos0:pos0 + TB]
                em.op("vector", dve, lambda e, cs=cs: e.tensor_mul(tt[:], xe[:], cs), waits=cw)
                em.op("vector", dve, lambda e, sn=sn: e.tensor_mul(tt2[:], xo[:], sn))
                l1 = em.op("vector", dve, lambda e, stg=stg: e.tensor_sub(stg[:, 0, :], tt[:], tt2[:]))
                em.op("vector", dve, lambda e, sn=sn: e.tensor_mul(tt[:], xe[:], sn))
                em.op("vector", dve, lambda e, cs=cs: e.tensor_mul(tt2[:], xo[:], cs))
                l2 = em.op("vector", dve, lambda e, stg=stg: e.tensor_add(stg[:, 1, :], tt[:], tt2[:]))
                sp.store(em, j, xr_dram.ap()[c2_, :, tb * TB:(tb + 1) * TB], stg[:, 0, :], waits=[(dve, l2)])
                sp.store(em, j, xr_dram.ap()[4 + c2_, :, tb * TB:(tb + 1) * TB], stg[:, 1, :])
        quiesce_dma(p, em)
        em.replay()



def pass_router(p, em, g, name, xn_dram, rw_dram, rb_dram, comb_dram):
    nc = p.nc
    pe, dve, act = p.chan("pe"), p.chan("vector"), p.chan("scalar")
    ones_row_b = g["ones_row_b"]
    ident = g["ident_f"]
    with ExitStack() as pctx:
        rw_sb = pctx.enter_context(nc.sbuf_tensor(p.fresh(name + "rw"), [128, DCH, 8], BF16))
        rb_sb = pctx.enter_context(nc.sbuf_tensor(p.fresh(name + "rb"), [1, 8], BF16))
        wch = p.chan(name + "_w")
        wtok = em.dma("gpsimd", rw_sb.ap(), rw_dram.ap().rearrange("(i p) o -> p i o", p=128), wch)
        btok = em.dma("gpsimd", rb_sb.ap(), rb_dram.ap().rearrange("(a o) -> a o", a=1), wch)
        xp = Pool(p, pctx, name + "_x", 2, [128, DCH, TB], BF16, queue="sync")
        sp = StorePool(p, pctx, name + "_o", 2, [8, TB], F32, queue="scalar")
        lg = pctx.enter_context(nc.sbuf_tensor(p.fresh(name + "lg"), [8, TB], F32))
        lgt = pctx.enter_context(nc.sbuf_tensor(p.fresh(name + "lt"), [128, TBC, 8], F32))
        v1 = pctx.enter_context(nc.sbuf_tensor(p.fresh(name + "v1"), [128, 1], F32))
        v2 = pctx.enter_context(nc.sbuf_tensor(p.fresh(name + "v2"), [128, 1], F32))
        m1 = pctx.enter_context(nc.sbuf_tensor(p.fresh(name + "m1"), [128, 8], F32))
        m2 = pctx.enter_context(nc.sbuf_tensor(p.fresh(name + "m2"), [128, 8], F32))
        l2t = pctx.enter_context(nc.sbuf_tensor(p.fresh(name + "l2"), [128, 8], F32))
        sg = pctx.enter_context(nc.sbuf_tensor(p.fresh(name + "sg"), [128, 1], F32))
        ta = pctx.enter_context(nc.sbuf_tensor(p.fresh(name + "ta"), [128, 1], F32))
        cmb = pctx.enter_context(nc.sbuf_tensor(p.fresh(name + "cm"), [128, 8], F32))
        lg_gate = None
        lg_rd_gate = None
        b2_gate = None
        b3_gate = None
        for tb in range(NTB):
            ix, xt = xp.begin(em)
            xp.dma(em, ix, xt.ap(), xn_dram.ap()[:, :, tb * TB:(tb + 1) * TB].rearrange("c p n -> p c n"))
            w0 = [(wch, wtok), (wch, btok), (xp.chan_of(ix), xp.tok(ix))]
            if lg_gate is not None:
                w0.append(lg_gate)
            mm = None
            for i in range(DCH):
                mm = em.op("tensor", pe, lambda e, i=i, xt=xt: e.matmul(
                    p.banks[0][0:8, 0:TB], rw_sb[:, i, :], xt[:, i, :],
                    start=(i == 0), stop=False),
                    waits=(w0 if i == 0 else []))
            mm = em.op("tensor", pe, lambda e: e.matmul(
                p.banks[0][0:8, 0:TB], rb_sb[0:1, :], ones_row_b[0:1, :],
                start=False, stop=True))
            xp.set_release(ix, pe, mm)
            wlg = [(pe, mm)]
            if lg_rd_gate is not None:
                wlg.append(lg_rd_gate)
            lg_tok = em.op("scalar", act, lambda e: e.copy(lg[:], p.banks[0][0:8, 0:TB]), waits=wlg)
            lg_gate = (act, lg_tok)
            tts = []
            for ql in range(TBC):
                w = [(act, lg_tok)]
                if b2_gate is not None and ql == 0:
                    w.append(b2_gate)
                tt_ = em.op("tensor", pe, lambda e, ql=ql: e.transpose(
                    p.banks[2][0:128, ql * 8:(ql + 1) * 8], lg[0:8, ql * 128:(ql + 1) * 128],
                    ident[0:8, 0:8]), waits=w)
                tts.append(tt_)
                lg_rd_gate = (pe, tt_)
            ev = em.op("vector", dve, lambda e: e.tensor_copy(
                lgt.ap().rearrange("p c o -> p (c o)"), p.banks[2][0:128, 0:TBC * 8]),
                waits=[(pe, tts[-1])])
            b2_gate = (dve, ev)
            for ql in range(TBC):
                src = lgt[:, ql, :]
                em.op("vector", dve, lambda e, src=src: e.reduce_max(v1[:], src, axis=AX.X))
                em.op("vector", dve, lambda e, src=src: e.tensor_single_scalar(m1[:], src, v1[:, 0:1], ALU.is_ge))
                em.op("vector", dve, lambda e, src=src: e.scalar_tensor_tensor(l2t[:], m1[:], -1e30, src, ALU.mult, ALU.add))
                em.op("vector", dve, lambda e: e.reduce_max(v2[:], l2t[:], axis=AX.X))
                sv = em.op("vector", dve, lambda e: e.tensor_sub(sg[:], v1[:], v2[:]))
                sgt = em.op("scalar", act, lambda e: e.activation(sg[:], sg[:], AF.Sigmoid), waits=[(dve, sv)])
                em.op("vector", dve, lambda e: e.tensor_single_scalar(m2[:], l2t[:], v2[:, 0:1], ALU.is_ge),
                      waits=[(act, sgt)])
                em.op("vector", dve, lambda e: e.tensor_scalar(ta[:], sg[:], -1.0, 1.0, ALU.mult, ALU.add))
                em.op("vector", dve, lambda e: e.tensor_single_scalar(m2[:], m2[:], ta[:, 0:1], ALU.mult))
                em.op("vector", dve, lambda e: e.tensor_single_scalar(m1[:], m1[:], sg[:, 0:1], ALU.mult))
                cm_tok = em.op("vector", dve, lambda e: e.tensor_add(cmb[:], m2[:], m1[:]))
                w = [(dve, cm_tok)]
                if b3_gate is not None:
                    w.append(b3_gate)
                tt_ = em.op("tensor", pe, lambda e: e.transpose(
                    p.banks[3][0:8, 0:128], cmb[:], ident[:]), waits=w)
                if ql == 0:
                    j, stg, gate = sp.begin()
                g2 = (gate if ql == 0 else [])
                evc = em.op("vector", dve, lambda e, ql=ql, stg=stg: e.tensor_copy(
                    stg[0:8, ql * 128:(ql + 1) * 128], p.banks[3][0:8, 0:128]),
                    waits=[(pe, tt_)] + g2)
                b3_gate = (dve, evc)
            sp.store(em, j, comb_dram.ap()[:, tb * TB:(tb + 1) * TB], stg.ap(), waits=[(dve, evc)])
        quiesce_dma(p, em)
        em.replay()



def pass_cast_dram(p, em, name, src_dram, dst_dram, n_moves, src_slice, dst_slice,
                   tile_shape=(128, 1024)):
    nc = p.nc
    with ExitStack() as pctx:
        bp = Pool(p, pctx, name + "_b", 2, list(tile_shape), BF16, queue="gpsimd")
        st = [p.chan(name + f"_s{i}") for i in range(2)]
        stok = {}
        for i in range(n_moves):
            ii, bt = bp.begin(em)
            slot = bp.slot(ii)
            if ii >= 2 and (ii - 2) in stok:
                pass
            bp.dma(em, ii, bt.ap(), src_slice(i))
            tok = em.dma("gpsimd", dst_slice(i), bt.ap(), st[slot],
                         waits=[(bp.chan_of(ii), bp.tok(ii))])
            stok[ii] = tok
            bp.set_release(ii, st[slot], tok)
        quiesce_dma(p, em)
        em.replay()



def pass_moe_ph1(p, em, g, name, xn_dram, w1_dram, b1c, w2_dram, b2c,
                 comb_dram, h_dram):
    nc = p.nc
    pe, dve, act = p.chan("pe"), p.chan("vector"), p.chan("scalar")
    ones_m_b = g["ones_m_b"]
    FCG = 4
    with ExitStack() as pctx:
        xn_sb = pctx.enter_context(nc.sbuf_tensor(p.fresh(name + "xn"), [128, DCH, T], BF16))
        xch = p.chan(name + "_x")
        xtok = em.dma("sync", xn_sb.ap(), xn_dram.ap().rearrange("c p t -> p c t"), xch)
        w1p = Pool(p, pctx, name + "_w1", 2, [128, DCH, FCG * 128], BF16, queue="gpsimd")
        w2p = Pool(p, pctx, name + "_w2", 2, [128, DCH, FCG * 128], BF16, queue="gpsimd")
        crow = pctx.enter_context(nc.sbuf_tensor(p.fresh(name + "cr"), [1, T], BF16))
        cb_sb = pctx.enter_context(nc.sbuf_tensor(p.fresh(name + "cb"), [128, NTB, TB], BF16))
        sl = pctx.enter_context(nc.sbuf_tensor(p.fresh(name + "sl"), [128, TB], F32))
        sp = StorePool(p, pctx, name + "_o", 3, [128, NTB, TB], BF16, queue="scalar")
        cch = p.chan(name + "_c")
        h1_gate = {0: None, 1: None}
        h2_gate = {0: None, 1: None}
        cb_gate = None
        sl_gate = None
        exp_gate = None
        for ex in range(NEXP):
            wexp = [] if exp_gate is None else [exp_gate]
            ctok = em.dma("gpsimd", crow[:], comb_dram.ap()[ex:ex + 1, :], cch, waits=wexp)
            cbt_prev = None
            for tb in range(NTB):
                w = [(cch, ctok)]
                if cb_gate is not None and tb == 0:
                    w.append(cb_gate)
                if cbt_prev is not None:
                    w.append((act, cbt_prev))
                t_ = em.op("tensor", pe, lambda e, tb=tb: e.matmul(
                    p.banks[4][:, 0:TB], ones_m_b[0:1, :], crow[0:1, tb * TB:(tb + 1) * TB],
                    start=True, stop=True), waits=w)
                cbt = em.op("scalar", act, lambda e, tb=tb: e.copy(cb_sb[:, tb, :], p.banks[4][:]),
                            waits=[(pe, t_)])
                cbt_prev = cbt
            for fcg in range(FCH // FCG):
                i1, w1t = w1p.begin(em)
                w1p.dma(em, i1, w1t.ap(),
                        w1_dram.ap()[ex].rearrange("(i p) f -> p i f", p=128)[:, :, fcg * FCG * 128:(fcg + 1) * FCG * 128])
                i2, w2t = w2p.begin(em)
                w2p.dma(em, i2, w2t.ap(),
                        w2_dram.ap()[ex].rearrange("(i p) f -> p i f", p=128)[:, :, fcg * FCG * 128:(fcg + 1) * FCG * 128])
                for f4 in range(FCG):
                    fc = fcg * FCG + f4
                    j, stg, gate = sp.begin()
                    for tb in range(NTB):
                        bk = tb % 2
                        b_h1, b_h2 = p.banks[bk], p.banks[2 + bk]
                        w0 = [(xch, xtok), (w1p.chan_of(i1), w1p.tok(i1)), (w2p.chan_of(i2), w2p.tok(i2))]
                        if h1_gate[bk] is not None:
                            w0.append(h1_gate[bk])
                        if h2_gate[bk] is not None:
                            w0.append(h2_gate[bk])
                        mm1 = None
                        for i in range(DCH):
                            mm1 = em.op("tensor", pe, lambda e, i=i, tb=tb, f4=f4, w1t=w1t, b_h1=b_h1: e.matmul(
                                b_h1[:], w1t[:, i, f4 * 128:(f4 + 1) * 128], xn_sb[:, i, tb * TB:(tb + 1) * TB],
                                start=(i == 0), stop=(i == DCH - 1)),
                                waits=(w0 if i == 0 else []))
                        mm2 = None
                        for i in range(DCH):
                            mm2 = em.op("tensor", pe, lambda e, i=i, tb=tb, f4=f4, w2t=w2t, b_h2=b_h2: e.matmul(
                                b_h2[:], w2t[:, i, f4 * 128:(f4 + 1) * 128], xn_sb[:, i, tb * TB:(tb + 1) * TB],
                                start=(i == 0), stop=(i == DCH - 1)))
                        if tb == NTB - 1 and f4 == FCG - 1:
                            w1p.set_release(i1, pe, mm1)
                            w2p.set_release(i2, pe, mm2)
                        if tb == NTB - 1 and fcg == FCH // FCG - 1 and f4 == FCG - 1:
                            exp_gate = (pe, mm2)
                        wsl = [(pe, mm1)]
                        if sl_gate is not None:
                            wsl.append(sl_gate)
                        s1 = em.op("scalar", act, lambda e, b_h1=b_h1, ex=ex, fc=fc: e.activation(
                            sl[:], b_h1[:], AF.Silu, bias=b1c[:, ex, fc:fc + 1]), waits=wsl)
                        h1_gate[bk] = (act, s1)
                        d1 = em.op("vector", dve, lambda e, b_h2=b_h2, ex=ex, fc=fc, tb=tb, stg=stg: e.scalar_tensor_tensor(
                            stg[:, tb, :], b_h2[:], b2c[:, ex, fc:fc + 1], sl[:], ALU.add, ALU.mult),
                            waits=[(pe, mm2), (act, s1)] + (gate if tb == 0 else []))
                        sl_gate = (dve, d1)
                        h2_gate[bk] = (dve, d1)
                        d2 = em.op("vector", dve, lambda e, tb=tb, stg=stg: e.tensor_mul(
                            stg[:, tb, :], stg[:, tb, :], cb_sb[:, tb, :]), waits=[(act, cbt_prev)])
                        if tb == NTB - 1 and fc == FCH - 1:
                            cb_gate = (dve, d2)
                        last_d2 = d2
                    sp.store(em, j, h_dram.ap()[ex, fc, :, :].rearrange("p (a n) -> p a n", n=TB),
                             stg.ap(), waits=[(dve, last_d2)])
        quiesce_dma(p, em)
        em.replay()


def prep_bias_cols(p, em, g, name, b1_dram, b2_dram, pctx):
    nc = p.nc
    dve = p.chan("vector")
    ch = p.chan(name + "_b")
    row = pctx.enter_context(nc.sbuf_tensor(p.fresh(name + "r"), [1, F_FF], F32))
    b1c = pctx.enter_context(nc.sbuf_tensor(p.fresh(name + "1"), [128, NEXP, FCH], F32))
    b2c = pctx.enter_context(nc.sbuf_tensor(p.fresh(name + "2"), [128, NEXP, FCH], F32))
    tok = None
    for (dram, cols) in ((b1_dram, b1c), (b2_dram, b2c)):
        for ex in range(NEXP):
            t = em.dma("sync", row[:], dram.ap()[ex:ex + 1, :], ch)
            tok = row_to_cols(p, em, g, row, F_FF, cols.ap()[:, ex, :], row_waits=[(ch, t)])
            em.wait("sync", dve, tok)
    return b1c, b2c, tok



def pass_moe_ph2(p, em, g, name, h_dram, w3b_dram, b3_dram, comb_dram,
                 r_dram, h1h_dram, rout_dram):
    nc = p.nc
    pe, dve, act = p.chan("pe"), p.chan("vector"), p.chan("scalar")
    FCG = 4
    with ExitStack() as pctx:
        b3_sb = pctx.enter_context(nc.sbuf_tensor(p.fresh(name + "b3"), [8, D], BF16))
        cm_sb = pctx.enter_context(nc.sbuf_tensor(p.fresh(name + "cm"), [8, T], BF16))
        bch = p.chan(name + "_b")
        btok = em.dma("gpsimd", b3_sb.ap(), b3_dram.ap(), bch)
        ctok = em.dma("gpsimd", cm_sb.ap(), comb_dram.ap(), bch)
        hp = Pool(p, pctx, name + "_h", 3, [128, FCG, TB], BF16, queue="sync")
        wp = Pool(p, pctx, name + "_w", 3, [128, FCG, D], BF16, queue="sync")
        ap_ = Pool(p, pctx, name + "_a", 2, [128, DCH, TB], F32, queue="sync")
        rp = Pool(p, pctx, name + "_r", 2, [128, DCH, TB], F32, queue="sync")
        ev = pctx.enter_context(nc.sbuf_tensor(p.fresh(name + "ev"), [128, TB], F32))
        tt = pctx.enter_context(nc.sbuf_tensor(p.fresh(name + "tt"), [128, TB], F32))
        sp = StorePool(p, pctx, name + "_o", 2, [128, DCH, TB], F32, queue="scalar")
        bank_gate = [None] * 8
        ev_gate = None
        for tb in range(NTB):
            ia, at = ap_.begin(em)
            ap_.dma(em, ia, at.ap(), h1h_dram.ap()[:, :, tb * TB:(tb + 1) * TB].rearrange("c p n -> p c n"))
            ir, rt = rp.begin(em)
            rp.dma(em, ir, rt.ap(), r_dram.ap()[:, :, tb * TB:(tb + 1) * TB].rearrange("c p n -> p c n"))
            first = True
            mm_last = {}
            for ex in range(NEXP):
                for fcg in range(FCH // FCG):
                    ih, ht = hp.begin(em)
                    hp.dma(em, ih, ht.ap(),
                           h_dram.ap()[ex, fcg * FCG:(fcg + 1) * FCG, :, tb * TB:(tb + 1) * TB].rearrange("a p n -> p a n"))
                    iw, wt = wp.begin(em)
                    wp.dma(em, iw, wt.ap(), w3b_dram.ap()[ex, fcg * FCG:(fcg + 1) * FCG, :, :].rearrange("a p n -> p a n"))
                    w0 = [(hp.chan_of(ih), hp.tok(ih)), (wp.chan_of(iw), wp.tok(iw))]
                    mm = None
                    for f4 in range(FCG):
                        for dc in range(DCH):
                            w = list(w0) if (f4 == 0 and dc == 0) else []
                            if first and bank_gate[dc] is not None:
                                w.append(bank_gate[dc])
                            mm = em.op("tensor", pe, lambda e, dc=dc, f4=f4, wt=wt, ht=ht, first=first: e.matmul(
                                p.banks[dc][:], wt[:, f4, dc * 128:(dc + 1) * 128], ht[:, f4, :],
                                start=first, stop=False), waits=w)
                            mm_last[dc] = mm
                        first = False
                    hp.set_release(ih, pe, mm)
                    wp.set_release(iw, pe, mm)
            for dc in range(DCH):
                mm = em.op("tensor", pe, lambda e, dc=dc, tb=tb: e.matmul(
                    p.banks[dc][:], b3_sb[0:8, dc * 128:(dc + 1) * 128], cm_sb[0:8, tb * TB:(tb + 1) * TB],
                    start=False, stop=True),
                    waits=([(bch, btok), (bch, ctok)] if tb == 0 and dc == 0 else []))
                mm_last[dc] = mm
            j, stg, gate = sp.begin()
            last = None
            for dc in range(DCH):
                w1 = [(pe, mm_last[dc])]
                if ev_gate is not None:
                    w1.append(ev_gate)
                e1 = em.op("scalar", act, lambda e, dc=dc: e.activation(ev[:], p.banks[dc][:], AF.Copy, scale=0.5),
                           waits=w1)
                bank_gate[dc] = (act, e1)
                e2 = em.op("vector", dve, lambda e, dc=dc, at=at: e.tensor_add(tt[:], ev[:], at[:, dc, :]),
                           waits=[(act, e1), (ap_.chan_of(ia), ap_.tok(ia))] + (gate if dc == 0 else []))
                e3 = em.op("vector", dve, lambda e, dc=dc, rt=rt, stg=stg: e.tensor_add(
                    stg[:, dc, :], tt[:], rt[:, dc, :]),
                    waits=[(rp.chan_of(ir), rp.tok(ir))])
                ev_gate = (dve, e2)
                last = e3
            ap_.set_release(ia, dve, last)
            rp.set_release(ir, dve, last)
            sp.store(em, j, rout_dram.ap()[:, :, tb * TB:(tb + 1) * TB].rearrange("c p n -> p c n"),
                     stg.ap(), waits=[(dve, last)])
        quiesce_dma(p, em)
        em.replay()



ATT_GQA = dict(
    nh=16, dk=64, dv=64, dv_total=256, kch=2, qch=8, scale=1.0 / 8.0,
    q_loc=lambda h: (h // 2, 64 * (h % 2)),
    k_loc=lambda h: ((h // 4) // 2, 64 * ((h // 4) % 2), 64),
    v_col=lambda h: 64 * (h // 4),
    out_loc=lambda h: (h // 2, 64 * (h % 2)),
    blocks=blocks_causal,
)

ATT_RET = dict(
    nh=1, dk=D, dv=D, dv_total=D, kch=DCH, qch=DCH, scale=1.0 / 32.0,
    q_loc=lambda h: (0, 0), k_loc=lambda h: (0, 0, 128),
    v_col=lambda h: 0, out_loc=lambda h: (0, 0),
    blocks=blocks_full_decay, decay=True,
)

ATT_WIN = dict(
    nh=16, dk=64, dv=64, dv_total=D, kch=DCH, qch=DCH, scale=1.0 / 8.0,
    q_loc=lambda h: (h // 2, 64 * (h % 2)),
    k_loc=lambda h: (h // 2, 64 * (h % 2), 64),
    v_col=lambda h: 64 * h,
    out_loc=lambda h: (h // 2, 64 * (h % 2)),
    blocks=blocks_window,
)





def pass_wprep(p, em, g, name, src_dram, dst_dram, R, C, scale):
    nc = p.nc
    pe, dve = p.chan("pe"), p.chan("vector")
    ident_b = g["ident_b"]
    CCH = C // 128
    with ExitStack() as pctx:
        lp = Pool(p, pctx, name + "_l", 2, [128, C], BF16, queue="gpsimd")
        sc = pctx.enter_context(nc.sbuf_tensor(p.fresh(name + "sc"), [128, C], BF16))
        sp = StorePool(p, pctx, name + "_o", 2, [128, CCH, 128], BF16, queue="scalar")
        bank_gate = [None, None]
        for i in range(R // 128):
            ii, wt = lp.begin(em)
            lp.dma(em, ii, wt.ap(), src_dram.ap()[i * 128:(i + 1) * 128, :])
            s1 = em.op("vector", dve, lambda e, wt=wt: e.tensor_scalar_mul(sc[:], wt[:], float(scale)),
                       waits=[(lp.chan_of(ii), lp.tok(ii))])
            lp.set_release(ii, dve, s1)
            j, stg, gate = sp.begin()
            ev = None
            ngrp = (CCH + 7) // 8
            for ccg in range(ngrp):
                par = (i * ngrp + ccg) % 2
                bank_b = p.banks[par].ap().bitcast(BF16)
                ncc = min(8, CCH - ccg * 8)
                tp = None
                for k in range(ncc):
                    cc = ccg * 8 + k
                    w = [(dve, s1)]
                    if k == 0 and bank_gate[par] is not None:
                        w.append(bank_gate[par])
                    tp = em.op("tensor", pe, lambda e, cc=cc, k=k, bank_b=bank_b: e.transpose(
                        bank_b[:, k * 128:(k + 1) * 128],
                        sc[:, cc * 128:(cc + 1) * 128], ident_b[:]), waits=w)
                ev = em.op("vector", dve, lambda e, stg=stg, bank_b=bank_b, ccg=ccg, ncc=ncc: e.tensor_copy(
                    stg.ap().rearrange("p c n -> p (c n)")[:, ccg * 1024:ccg * 1024 + ncc * 128],
                    bank_b[:, 0:ncc * 128]),
                    waits=[(pe, tp)] + (gate if ccg == 0 else []))
                bank_gate[par] = (dve, ev)
            sp.store(em, j, dst_dram.ap()[:, :, i * 128:(i + 1) * 128].rearrange("c p n -> p c n"),
                     stg.ap(), waits=[(dve, ev)])
        quiesce_dma(p, em)
        em.replay()


def pass_l1ffn(p, em, g, name, xn_dram, wgT, wuT, wdT, r_dram, rout_dram):
    nc = p.nc
    pe, dve, act = p.chan("pe"), p.chan("vector"), p.chan("scalar")
    with ExitStack() as pctx:
        xp = Pool(p, pctx, name + "_x", 2, [128, DCH, TB], BF16, queue="sync")
        gp = Pool(p, pctx, name + "_g", 2, [128, DCH, 128], BF16, queue="sync")
        up = Pool(p, pctx, name + "_u", 2, [128, DCH, 128], BF16, queue="sync")
        dp = Pool(p, pctx, name + "_d", 2, [128, HCH, 128], BF16, queue="sync")
        rp = Pool(p, pctx, name + "_r", 2, [128, DCH, TB], F32, queue="sync")
        hdn = pctx.enter_context(nc.sbuf_tensor(p.fresh(name + "h"), [128, HCH, TB], BF16))
        sl = pctx.enter_context(nc.sbuf_tensor(p.fresh(name + "sl"), [128, TB], F32))
        ev = pctx.enter_context(nc.sbuf_tensor(p.fresh(name + "ev"), [128, TB], F32))
        sp = StorePool(p, pctx, name + "_o", 2, [128, DCH, TB], F32, queue="scalar")
        g_gate = {0: None, 1: None}
        u_gate = {0: None, 1: None}
        d_gate = {0: None, 1: None}
        sl_gate = None
        hdn_gate = {}
        ev_gate = None
        for tb in range(NTB):
            ix, xt = xp.begin(em)
            xp.dma(em, ix, xt.ap(), xn_dram.ap()[:, :, tb * TB:(tb + 1) * TB].rearrange("c p n -> p c n"))
            ir, rt = rp.begin(em)
            rp.dma(em, ir, rt.ap(), r_dram.ap()[:, :, tb * TB:(tb + 1) * TB].rearrange("c p n -> p c n"))
            last_up_mm = None
            for hc in range(HCH):
                ig, gt = gp.begin(em)
                gp.dma(em, ig, gt.ap(), wgT.ap()[:, :, hc * 128:(hc + 1) * 128].rearrange("c p n -> p c n"))
                iu, ut = up.begin(em)
                up.dma(em, iu, ut.ap(), wuT.ap()[:, :, hc * 128:(hc + 1) * 128].rearrange("c p n -> p c n"))
                bk = hc % 2
                b_g, b_u = p.banks[bk], p.banks[2 + bk]
                w0 = [(xp.chan_of(ix), xp.tok(ix)), (gp.chan_of(ig), gp.tok(ig))]
                if g_gate[bk] is not None:
                    w0.append(g_gate[bk])
                mmg = None
                for i in range(DCH):
                    mmg = em.op("tensor", pe, lambda e, i=i, tb=tb, gt=gt, b_g=b_g, xt=xt: e.matmul(
                        b_g[:], gt[:, i, :], xt[:, i, :],
                        start=(i == 0), stop=(i == DCH - 1)),
                        waits=(w0 if i == 0 else []))
                gp.set_release(ig, pe, mmg)
                w0b = [(up.chan_of(iu), up.tok(iu))]
                if u_gate[bk] is not None:
                    w0b.append(u_gate[bk])
                mmu = None
                for i in range(DCH):
                    mmu = em.op("tensor", pe, lambda e, i=i, tb=tb, ut=ut, b_u=b_u, xt=xt: e.matmul(
                        b_u[:], ut[:, i, :], xt[:, i, :],
                        start=(i == 0), stop=(i == DCH - 1)),
                        waits=(w0b if i == 0 else []))
                up.set_release(iu, pe, mmu)
                last_up_mm = mmu
                wsl = [(pe, mmg)]
                if sl_gate is not None:
                    wsl.append(sl_gate)
                s1 = em.op("scalar", act, lambda e, b_g=b_g: e.activation(sl[:], b_g[:], AF.Silu),
                           waits=wsl)
                g_gate[bk] = (act, s1)
                w1 = [(pe, mmu), (act, s1)]
                if hc in hdn_gate:
                    w1.append(hdn_gate[hc])
                d1 = em.op("vector", dve, lambda e, hc=hc, b_u=b_u: e.tensor_mul(hdn[:, hc, :], b_u[:], sl[:]),
                           waits=w1)
                sl_gate = (dve, d1)
                u_gate[bk] = (dve, d1)
            xp.set_release(ix, pe, last_up_mm)
            j, stg, gate = sp.begin()
            last = None
            for dc in range(DCH):
                idp, dt_ = dp.begin(em)
                dp.dma(em, idp, dt_.ap(), wdT.ap()[:, :, dc * 128:(dc + 1) * 128].rearrange("c p n -> p c n"))
                bk = 4 + dc % 2
                bank = p.banks[bk]
                w0 = [(dp.chan_of(idp), dp.tok(idp)), (dve, em.last["vector"][1])]
                if d_gate[dc % 2] is not None:
                    w0.append(d_gate[dc % 2])
                mm = None
                for hc in range(HCH):
                    mm = em.op("tensor", pe, lambda e, hc=hc, dt_=dt_, bank=bank: e.matmul(
                        bank[:], dt_[:, hc, :], hdn[:, hc, :],
                        start=(hc == 0), stop=(hc == HCH - 1)),
                        waits=(w0 if hc == 0 else []))
                    hdn_gate[hc] = (pe, mm)
                dp.set_release(idp, pe, mm)
                w1 = [(pe, mm)]
                if ev_gate is not None:
                    w1.append(ev_gate)
                e1 = em.op("scalar", act, lambda e, bank=bank: e.copy(ev[:], bank[:]), waits=w1)
                d_gate[dc % 2] = (act, e1)
                e2 = em.op("vector", dve, lambda e, dc=dc, rt=rt, stg=stg: e.tensor_add(
                    stg[:, dc, :], ev[:], rt[:, dc, :]),
                    waits=[(act, e1), (rp.chan_of(ir), rp.tok(ir))] + (gate if dc == 0 else []))
                ev_gate = (dve, e2)
                last = e2
            rp.set_release(ir, dve, last)
            sp.store(em, j, rout_dram.ap()[:, :, tb * TB:(tb + 1) * TB].rearrange("c p n -> p c n"),
                     stg.ap(), waits=[(dve, last)])
        quiesce_dma(p, em)
        em.replay()


def rows_ln_full(p, em, k1, sum_row, sq_row, m_row, var_row, sinv_row):
    dve, act = p.chan("vector"), p.chan("scalar")
    b_eps, btok = k1.get(em, 1e-5)
    t1 = em.op("vector", dve, lambda e: e.tensor_scalar_mul(m_row[:], sum_row[:], 1.0 / D))
    t2 = em.op("vector", dve, lambda e: e.tensor_mul(var_row[:], m_row[:], m_row[:]), waits=[(dve, t1)])
    t3 = em.op("vector", dve, lambda e: e.scalar_tensor_tensor(var_row[:], sq_row[:], 1.0 / D, var_row[:],
                                                               ALU.mult, ALU.subtract))
    t4 = em.op("scalar", act, lambda e: e.activation(sinv_row[:], var_row[:], AF.Ln, bias=b_eps),
               waits=[(dve, t3), (dve, btok)])
    t5 = em.op("scalar", act, lambda e: e.activation(sinv_row[:], sinv_row[:], AF.Exp, scale=-0.5))
    return t5


def rows_rms1(p, em, k1, var_row, sinv_row, a1_row, sh_row, ln_tok):
    dve, act = p.chan("vector"), p.chan("scalar")
    b_eps2, btok = k1.get(em, 1e-6)
    t1 = em.op("vector", dve, lambda e: e.tensor_scalar_add(a1_row[:], var_row[:], 1e-5), waits=[(act, ln_tok)])
    t2 = em.op("vector", dve, lambda e: e.reciprocal(a1_row[:], a1_row[:]))
    t3 = em.op("vector", dve, lambda e: e.tensor_mul(a1_row[:], var_row[:], a1_row[:]))
    t4 = em.op("scalar", act, lambda e: e.activation(a1_row[:], a1_row[:], AF.Ln, bias=b_eps2),
               waits=[(dve, t3), (dve, btok)])
    t5 = em.op("scalar", act, lambda e: e.activation(a1_row[:], a1_row[:], AF.Exp, scale=-0.5))
    t6 = em.op("vector", dve, lambda e: e.tensor_mul(a1_row[:], a1_row[:], sinv_row[:]), waits=[(act, t5)])
    t7 = em.op("vector", dve, lambda e: e.tensor_scalar_mul(sh_row[:], sinv_row[:], 0.5))
    return t7


def rows_rms2(p, em, k1, sq_row, a2_row):
    dve, act = p.chan("vector"), p.chan("scalar")
    b_eps2, btok = k1.get(em, 1e-6)
    t1 = em.op("vector", dve, lambda e: e.tensor_scalar_mul(a2_row[:], sq_row[:], 4.0 / D))
    t2 = em.op("scalar", act, lambda e: e.activation(a2_row[:], a2_row[:], AF.Ln, bias=b_eps2),
               waits=[(dve, t1), (dve, btok)])
    t3 = em.op("scalar", act, lambda e: e.activation(a2_row[:], a2_row[:], AF.Exp, scale=-0.5))
    t4 = em.op("vector", dve, lambda e: e.tensor_scalar_mul(a2_row[:], a2_row[:], 2.0), waits=[(act, t3)])
    return t4


def load_wcols(p, em, g, name, w_dram, pctx):
    nc = p.nc
    dve = p.chan("vector")
    wrow = pctx.enter_context(nc.sbuf_tensor(p.fresh(name + "r"), [1, D], F32))
    wcols = pctx.enter_context(nc.sbuf_tensor(p.fresh(name + "c"), [128, DCH], F32))
    ch = p.chan(name + "_w")
    tok = em.dma("sync", wrow[:], w_dram.ap().rearrange("(a d) -> a d", a=1), ch)
    t = row_to_cols(p, em, g, wrow, D, wcols, row_waits=[(ch, tok)])
    return wcols, t


class Rows:
    def __init__(self, p, ctx=None):
        ctx = ctx if ctx is not None else p.ctx
        self.r = [ctx.enter_context(p.nc.sbuf_tensor(p.fresh(f"row{i}"), [1, T], F32)) for i in range(5)]
        self.sum, self.sq, self.m, self.var, self.sinv = self.r
        self.a1 = self.sum
        self.sh = self.sq
        self.sq2 = self.var
        self.sum2 = self.sum
        self.a2 = self.sinv


def emit_l0ffn_app(p, g, k1, rows_unused, hc, W, r_in, r_out, scr, tag):
    pe = p.chan("pe")
    rctx = ExitStack()
    rows = Rows(p, rctx)
    em = Emit(p); pass_stats(p, em, g, r_in, rows.sum, rows.sq)
    em = Emit(p)
    em.wait("vector", pe, pe.n)
    t_ln = rows_ln_full(p, em, k1, rows.sum, rows.sq, rows.m, rows.var, rows.sinv)
    t_r1 = rows_rms1(p, em, k1, rows.var, rows.sinv, rows.a1, rows.sh, t_ln)
    em.replay()
    dve = p.chan("vector")
    rw = [(dve, t_r1)]
    em = Emit(p); pass_xn(p, em, g, "xn", r_in, scr["xn0h"], F32, rows.m, rows.sh, row_waits=rw)
    em = Emit(p); pass_rope(p, em, g, "rp", r_in, scr["xr"], rows.m, rows.a1, W["L0_ffn_norm1_w"], hc, row_waits=rw)
    em = Emit(p); pass_linear(p, em, g, "li", scr["xr"], D, W["L0_ffn_Wq"], D, wsrc="f32", out_dram=scr["qT"])
    em = Emit(p); pass_linear(p, em, g, "li", scr["xr"], D, W["L0_ffn_Wk"], 256, wsrc="f32", out_dram=scr["kT"])
    em = Emit(p); pass_linear_tm(p, em, g, "lt", scr["xr"], D, W["L0_ffn_Wv"], 256, scr["vtm"], wsrc="f32")
    em = Emit(p); pass_att(p, em, g, "at", ATT_GQA, scr["qT"], scr["kT"], scr["vtm"], scr["ao"])
    em = Emit(p); pass_linear(p, em, g, "li", scr["ao"], D, W["L0_ffn_Wo"], D, wsrc="f32",
                              epilogue=("res", scr["xn0h"], scr["h1h"], 0.5))
    em = Emit(p); pass_stats(p, em, g, scr["h1h"], rows.sum2, rows.sq2)
    em = Emit(p)
    em.wait("vector", pe, pe.n)
    t_a2 = rows_rms2(p, em, k1, rows.sq2, rows.a2)
    em.replay()
    em = Emit(p)
    with ExitStack() as wctx:
        w2cols, w2tok = load_wcols(p, em, g, "w2c", W["L0_ffn_norm2_w"], wctx)
        em.replay()
        em = Emit(p)
        pass_xn(p, em, g, "xn", scr["h1h"], scr["xn2"], BF16, None, rows.a2, w_cols=w2cols,
                row_waits=[(p.chan("vector"), max(t_a2, w2tok))])
    rctx.close()
    if not hasattr(p, "_b12c"):
        em = Emit(p)
        b1c, b2c, _ = prep_bias_cols(p, em, g, "b12", W["L0_ffn_b1"], W["L0_ffn_b2"], p.ctx)
        em.replay()
        p._b12c = (b1c, b2c)
    b1c, b2c = p._b12c
    em = Emit(p); pass_router(p, em, g, "rt", scr["xn2"], W["L0_ffn_router_w"], W["L0_ffn_router_b"], scr["comb"])
    em = Emit(p); pass_cast_dram(p, em, "w3c", W["L0_ffn_W3"], scr["w3b"], NEXP * FCH,
                                 lambda i: W["L0_ffn_W3"].ap()[i // FCH, (i % FCH) * 128:((i % FCH) + 1) * 128, :],
                                 lambda i: scr["w3b"].ap()[i // FCH, i % FCH, :, :])
    em = Emit(p); pass_moe_ph1(p, em, g, "p1", scr["xn2"], W["L0_ffn_W1"], b1c,
                               W["L0_ffn_W2"], b2c, scr["comb"], scr["h_d"])
    em = Emit(p); pass_moe_ph2(p, em, g, "p2", scr["h_d"], scr["w3b"], W["L0_ffn_b3"], scr["comb"],
                               r_in, scr["h1h"], r_out)


def emit_l0attn_app(p, g, k1, rows_unused, hc, W, r_in, r_out, scr):
    pe = p.chan("pe")
    sctx = ExitStack()
    brow = sctx.enter_context(p.nc.sbuf_tensor(p.fresh("bqkv"), [1, 3 * D], BF16))
    brow2 = sctx.enter_context(p.nc.sbuf_tensor(p.fresh("bprj"), [1, D], BF16))
    rctx = ExitStack()
    rows = Rows(p, rctx)
    em = Emit(p); pass_stats(p, em, g, r_in, rows.sum, rows.sq)
    em = Emit(p)
    em.wait("vector", pe, pe.n)
    t_ln = rows_ln_full(p, em, k1, rows.sum, rows.sq, rows.m, rows.var, rows.sinv)
    em.replay()
    rw = [(p.chan("scalar"), t_ln)]
    em = Emit(p); pass_xn(p, em, g, "xn", r_in, scr["xnb"], BF16, rows.m, rows.sinv, row_waits=rw)
    rctx.close()
    em = Emit(p)
    if True:
        bch = p.chan("bqkv_l")
        btok = em.dma("gpsimd", brow[:], W["L0_attn_bqkv"].ap().rearrange("(a d) -> a d", a=1), bch)
        em.replay()
        bw = [(bch, btok)]
        em = Emit(p); pass_linear(p, em, g, "li", scr["xnb"], D, W["L0_attn_Wqkv"], D, wsrc="f32",
                                  out_dram=scr["qT"], bias_row=brow, bias_waits=bw, w_col0=0)
        em = Emit(p); pass_linear(p, em, g, "li", scr["xnb"], D, W["L0_attn_Wqkv"], D, wsrc="f32",
                                  out_dram=scr["kTf"], bias_row=brow, bias_waits=bw, w_col0=D, bias_col0=D)
        em = Emit(p); pass_linear_tm(p, em, g, "lt", scr["xnb"], D, W["L0_attn_Wqkv"], D, scr["vtmf"],
                                     wsrc="f32", bias_row=brow, bias_waits=bw, w_col0=2 * D, bias_col0=2 * D)
    em = Emit(p); pass_att(p, em, g, "at", ATT_RET, scr["qT"], scr["kTf"], scr["vtmf"], scr["ao"],
                           gamma_dram=W["L0_attn_gamma"])
    em = Emit(p)
    if True:
        bch = p.chan("bprj_l")
        btok = em.dma("gpsimd", brow2[:], W["L0_attn_bproj"].ap().rearrange("(a d) -> a d", a=1), bch)
        em.replay()
        em = Emit(p); pass_linear(p, em, g, "li", scr["ao"], D, W["L0_attn_Wproj"], D, wsrc="f32",
                                  bias_row=brow2, bias_waits=[(bch, btok)],
                                  epilogue=("res", r_in, r_out, 1.0))
    sctx.close()


def emit_l1ffn_app(p, g, k1, rows_unused, hc, W, r_in, r_out, scr, do_prep=True):
    pe = p.chan("pe")
    rctx = ExitStack()
    rows = Rows(p, rctx)
    if do_prep:
        em = Emit(p); pass_wprep(p, em, g, "wp", W["L1_ffn_w_gate"], scr["wgT"], H_L1, D, 1.0 / 127.0)
        em = Emit(p); pass_wprep(p, em, g, "wp", W["L1_ffn_w_up"], scr["wuT"], H_L1, D, 1.0 / 127.0)
        em = Emit(p); pass_wprep(p, em, g, "wp", W["L1_ffn_w_down"], scr["wdT"], D, H_L1, 0.5 / 127.0)
    em = Emit(p); pass_stats(p, em, g, r_in, rows.sum, rows.sq)
    em = Emit(p)
    em.wait("vector", pe, pe.n)
    t_ln = rows_ln_full(p, em, k1, rows.sum, rows.sq, rows.m, rows.var, rows.sinv)
    em.replay()
    rw = [(p.chan("scalar"), t_ln)]
    em = Emit(p); pass_xn(p, em, g, "xn", r_in, scr["xnb"], BF16, rows.m, rows.sinv, row_waits=rw)
    rctx.close()
    em = Emit(p); pass_l1ffn(p, em, g, "ff", scr["xnb"], scr["wgT"], scr["wuT"], scr["wdT"], r_in, r_out)


def emit_l1attn_app(p, g, k1, rows_unused, hc, W, r_in, r_out, scr, do_prep=True):
    pe = p.chan("pe")
    rctx = ExitStack()
    rows = Rows(p, rctx)
    if do_prep:
        for nm, wt in [("pq", "L1_attn_Wq"), ("pk", "L1_attn_Wk"), ("pv", "L1_attn_Wv"), ("po", "L1_attn_Wo")]:
            em = Emit(p); pass_wprep(p, em, g, "wp", W[wt], scr[nm.replace("p", "w") + "T"], D, D, 1.0)
    em = Emit(p); pass_stats(p, em, g, r_in, rows.sum, rows.sq)
    em = Emit(p)
    em.wait("vector", pe, pe.n)
    t_ln = rows_ln_full(p, em, k1, rows.sum, rows.sq, rows.m, rows.var, rows.sinv)
    em.replay()
    rw = [(p.chan("scalar"), t_ln)]
    em = Emit(p); pass_xn(p, em, g, "xn", r_in, scr["xnb"], BF16, rows.m, rows.sinv, row_waits=rw)
    rctx.close()
    em = Emit(p); pass_linear(p, em, g, "li", scr["xnb"], D, scr["wqT"], D, wsrc="bf16T", out_dram=scr["qT"])
    em = Emit(p); pass_linear(p, em, g, "li", scr["xnb"], D, scr["wkT"], D, wsrc="bf16T", out_dram=scr["kTf"])
    em = Emit(p); pass_linear_tm(p, em, g, "lt", scr["xnb"], D, scr["wvT"], D, scr["vtmf"], wsrc="bf16T")
    em = Emit(p); pass_att(p, em, g, "at", ATT_WIN, scr["qT"], scr["kTf"], scr["vtmf"], scr["ao"])
    em = Emit(p); pass_linear(p, em, g, "li", scr["ao"], D, scr["woT"], D, wsrc="bf16T",
                              epilogue=("res", r_in, r_out, 1.0))



WEIGHT_SHAPES = {
    "L0_attn_Wqkv": ([D, 3 * D], F32), "L0_attn_bqkv": ([3 * D], F32),
    "L0_attn_Wproj": ([D, D], F32), "L0_attn_bproj": ([D], F32),
    "L0_attn_gamma": ([1, 1], F32),
    "L0_ffn_norm1_w": ([D], F32), "L0_ffn_Wq": ([D, D], F32),
    "L0_ffn_Wk": ([D, 256], F32), "L0_ffn_Wv": ([D, 256], F32),
    "L0_ffn_Wo": ([D, D], F32), "L0_ffn_norm2_w": ([D], F32),
    "L0_ffn_router_w": ([D, 8], F32), "L0_ffn_router_b": ([8], F32),
    "L0_ffn_W1": ([NEXP, D, F_FF], F32), "L0_ffn_b1": ([NEXP, F_FF], F32),
    "L0_ffn_W2": ([NEXP, D, F_FF], F32), "L0_ffn_b2": ([NEXP, F_FF], F32),
    "L0_ffn_W3": ([NEXP, F_FF, D], F32), "L0_ffn_b3": ([NEXP, D], F32),
    "L1_attn_Wq": ([D, D], F32), "L1_attn_Wk": ([D, D], F32),
    "L1_attn_Wv": ([D, D], F32), "L1_attn_Wo": ([D, D], F32),
    "L1_ffn_w_gate": ([H_L1, D], I32), "L1_ffn_w_up": ([H_L1, D], I32),
    "L1_ffn_w_down": ([D, H_L1], I32),
}


def declare_weights(nc, names):
    return {n: nc.dram_tensor(n, WEIGHT_SHAPES[n][0], WEIGHT_SHAPES[n][1], kind="ExternalInput")
            for n in names}


def _prog_common(consts):
    nc = bass.Bass("TRN2", target_bir_lowering=False)
    hc = declare_constants(nc, consts)
    return nc, hc


def build_prog_t_in():
    consts = GLOBAL_CONSTS
    nc, hc = _prog_common(consts)
    x = nc.dram_tensor("x_in", [SPC, SEQ, D], F32, kind="ExternalInput")
    r0 = nc.dram_tensor("r_out", [DCH, 128, T], F32, kind="ExternalOutput")
    with ExitStack() as ctx:
        p = Prog(nc, ctx)
        g = load_globals(p, hc)
        em = Emit(p); pass_t_in(p, em, g, x, r0)
    return nc, consts, ["x_in"]


def build_prog_t_out():
    consts = GLOBAL_CONSTS
    nc, hc = _prog_common(consts)
    r = nc.dram_tensor("r_in", [DCH, 128, T], F32, kind="ExternalInput")
    y = nc.dram_tensor("y_out", [SPC, SEQ, D], F32, kind="ExternalOutput")
    with ExitStack() as ctx:
        p = Prog(nc, ctx)
        g = load_globals(p, hc)
        em = Emit(p); pass_t_out(p, em, g, r, y)
    return nc, consts, ["r_in"]


def build_prog_l0ffn():
    consts = GLOBAL_CONSTS + ["rope_cos", "rope_sin"]
    nc, hc = _prog_common(consts)
    wn = [n for n in WEIGHT_SHAPES if n.startswith("L0_ffn_")]
    W = declare_weights(nc, wn)
    r_in = nc.dram_tensor("r_in", [DCH, 128, T], F32, kind="ExternalInput")
    r_out = nc.dram_tensor("r_out", [DCH, 128, T], F32, kind="ExternalOutput")
    scr = {
        "xn0h": nc.dram_tensor("s_xn0h", [DCH, 128, T], F32, kind="Internal"),
        "xr": nc.dram_tensor("s_xr", [DCH, 128, T], BF16, kind="Internal"),
        "qT": nc.dram_tensor("s_qT", [DCH, 128, T], BF16, kind="Internal"),
        "kT": nc.dram_tensor("s_kT", [2, 128, T], BF16, kind="Internal"),
        "vtm": nc.dram_tensor("s_vtm", [T, 256], BF16, kind="Internal"),
        "ao": nc.dram_tensor("s_ao", [DCH, 128, T], BF16, kind="Internal"),
        "h1h": nc.dram_tensor("s_h1h", [DCH, 128, T], F32, kind="Internal"),
        "xn2": nc.dram_tensor("s_xn2", [DCH, 128, T], BF16, kind="Internal"),
        "comb": nc.dram_tensor("s_comb", [8, T], F32, kind="Internal"),
        "w3b": nc.dram_tensor("s_w3b", [NEXP, FCH, 128, D], BF16, kind="Internal"),
        "h_d": nc.dram_tensor("s_hd", [NEXP, FCH, 128, T], BF16, kind="Internal"),
    }
    with ExitStack() as ctx:
        p = Prog(nc, ctx)
        k1 = Consts1(p)
        g = load_globals(p, hc)
        em = Emit(p); k1.prewarm(em); em.replay()
        emit_l0ffn_app(p, g, k1, None, hc, W, r_in, r_out, scr, "f")
    return nc, consts, ["r_in"] + wn


def build_prog_l0attn():
    consts = GLOBAL_CONSTS
    nc, hc = _prog_common(consts)
    wn = [n for n in WEIGHT_SHAPES if n.startswith("L0_attn_")]
    W = declare_weights(nc, wn)
    r_in = nc.dram_tensor("r_in", [DCH, 128, T], F32, kind="ExternalInput")
    r_out = nc.dram_tensor("r_out", [DCH, 128, T], F32, kind="ExternalOutput")
    scr = {
        "xnb": nc.dram_tensor("s_xnb", [DCH, 128, T], BF16, kind="Internal"),
        "qT": nc.dram_tensor("s_qT", [DCH, 128, T], BF16, kind="Internal"),
        "kTf": nc.dram_tensor("s_kTf", [DCH, 128, T], BF16, kind="Internal"),
        "vtmf": nc.dram_tensor("s_vtmf", [T, D], BF16, kind="Internal"),
        "ao": nc.dram_tensor("s_ao", [DCH, 128, T], BF16, kind="Internal"),
    }
    with ExitStack() as ctx:
        p = Prog(nc, ctx)
        k1 = Consts1(p)
        g = load_globals(p, hc)
        em = Emit(p); k1.prewarm(em); em.replay()
        emit_l0attn_app(p, g, k1, None, hc, W, r_in, r_out, scr)
    return nc, consts, ["r_in"] + wn


def build_prog_l1ffn():
    consts = GLOBAL_CONSTS
    nc, hc = _prog_common(consts)
    wn = ["L1_ffn_w_gate", "L1_ffn_w_up", "L1_ffn_w_down"]
    W = declare_weights(nc, wn)
    r_in = nc.dram_tensor("r_in", [DCH, 128, T], F32, kind="ExternalInput")
    r_out = nc.dram_tensor("r_out", [DCH, 128, T], F32, kind="ExternalOutput")
    scr = {
        "xnb": nc.dram_tensor("s_xnb", [DCH, 128, T], BF16, kind="Internal"),
        "wgT": nc.dram_tensor("s_wgT", [DCH, 128, H_L1], BF16, kind="Internal"),
        "wuT": nc.dram_tensor("s_wuT", [DCH, 128, H_L1], BF16, kind="Internal"),
        "wdT": nc.dram_tensor("s_wdT", [HCH, 128, D], BF16, kind="Internal"),
    }
    with ExitStack() as ctx:
        p = Prog(nc, ctx)
        k1 = Consts1(p)
        g = load_globals(p, hc)
        em = Emit(p); k1.prewarm(em); em.replay()
        emit_l1ffn_app(p, g, k1, None, hc, W, r_in, r_out, scr)
    return nc, consts, ["r_in"] + wn


def build_prog_l1attn():
    consts = GLOBAL_CONSTS
    nc, hc = _prog_common(consts)
    wn = ["L1_attn_Wq", "L1_attn_Wk", "L1_attn_Wv", "L1_attn_Wo"]
    W = declare_weights(nc, wn)
    r_in = nc.dram_tensor("r_in", [DCH, 128, T], F32, kind="ExternalInput")
    r_out = nc.dram_tensor("r_out", [DCH, 128, T], F32, kind="ExternalOutput")
    scr = {
        "xnb": nc.dram_tensor("s_xnb", [DCH, 128, T], BF16, kind="Internal"),
        "qT": nc.dram_tensor("s_qT", [DCH, 128, T], BF16, kind="Internal"),
        "kTf": nc.dram_tensor("s_kTf", [DCH, 128, T], BF16, kind="Internal"),
        "vtmf": nc.dram_tensor("s_vtmf", [T, D], BF16, kind="Internal"),
        "ao": nc.dram_tensor("s_ao", [DCH, 128, T], BF16, kind="Internal"),
        "wqT": nc.dram_tensor("s_wqT", [DCH, 128, D], BF16, kind="Internal"),
        "wkT": nc.dram_tensor("s_wkT", [DCH, 128, D], BF16, kind="Internal"),
        "wvT": nc.dram_tensor("s_wvT", [DCH, 128, D], BF16, kind="Internal"),
        "woT": nc.dram_tensor("s_woT", [DCH, 128, D], BF16, kind="Internal"),
    }
    with ExitStack() as ctx:
        p = Prog(nc, ctx)
        k1 = Consts1(p)
        g = load_globals(p, hc)
        em = Emit(p); k1.prewarm(em); em.replay()
        emit_l1attn_app(p, g, k1, None, hc, W, r_in, r_out, scr)
    return nc, consts, ["r_in"] + wn


_PROGS = {}

def get_prog(name):
    if name not in _PROGS:
        _PROGS[name] = {
            "t_in": build_prog_t_in, "t_out": build_prog_t_out,
            "l0ffn": build_prog_l0ffn, "l0attn": build_prog_l0attn,
            "l1ffn": build_prog_l1ffn, "l1attn": build_prog_l1attn,
            "fused": build_prog_fused,
        }[name]()
    return _PROGS[name]


def _launch(name, per_core_feeds, weights_np):
    nc, consts, innames = get_prog(name)
    cf = const_feeds(consts)
    in_maps = []
    for core in range(NCORES):
        m = dict(cf)
        for n in innames:
            if n in ("x_in", "r_in"):
                m[n] = per_core_feeds[core][n]
            else:
                m[n] = weights_np[n]
        in_maps.append(m)
    res = run_bass_kernel_spmd(nc, in_maps, core_ids=list(range(NCORES)))
    return res.results


def kernel_unfused(**inputs):
    x = np.ascontiguousarray(inputs["L0_attn_x"], dtype=np.float32)
    weights_np = {}
    for n, (shape, dt) in WEIGHT_SHAPES.items():
        if n == "L0_attn_x":
            continue
        v = np.asarray(inputs[n]) if n in inputs else None
        if n == "L0_attn_gamma":
            v = np.asarray(inputs[n]).reshape(1, 1).astype(np.float32)
        weights_np[n] = np.ascontiguousarray(v)
    shards = [np.ascontiguousarray(x[2 * i:2 * i + 2]) for i in range(NCORES)]

    feeds = [{"x_in": shards[i]} for i in range(NCORES)]
    res = _launch("t_in", feeds, weights_np)
    r = [res[i]["r_out"] for i in range(NCORES)]
    res = _launch("l0ffn", [{"r_in": r[i]} for i in range(NCORES)], weights_np)
    r = [res[i]["r_out"] for i in range(NCORES)]
    res = _launch("l0attn", [{"r_in": r[i]} for i in range(NCORES)], weights_np)
    r = [res[i]["r_out"] for i in range(NCORES)]
    res = _launch("l0ffn", [{"r_in": r[i]} for i in range(NCORES)], weights_np)
    r = [res[i]["r_out"] for i in range(NCORES)]
    res = _launch("l1ffn", [{"r_in": r[i]} for i in range(NCORES)], weights_np)
    r = [res[i]["r_out"] for i in range(NCORES)]
    res = _launch("l1attn", [{"r_in": r[i]} for i in range(NCORES)], weights_np)
    r = [res[i]["r_out"] for i in range(NCORES)]
    res = _launch("l1ffn", [{"r_in": r[i]} for i in range(NCORES)], weights_np)
    r = [res[i]["r_out"] for i in range(NCORES)]
    res = _launch("t_out", [{"r_in": r[i]} for i in range(NCORES)], weights_np)
    y = np.concatenate([res[i]["y_out"] for i in range(NCORES)], axis=0)
    return y.reshape(16, SEQ, D)



def build_prog_fused():
    consts = GLOBAL_CONSTS + ["rope_cos", "rope_sin"]
    nc, hc = _prog_common(consts)
    wn = list(WEIGHT_SHAPES.keys())
    W = declare_weights(nc, wn)
    x = nc.dram_tensor("x_in", [SPC, SEQ, D], F32, kind="ExternalInput")
    y = nc.dram_tensor("y_out", [SPC, SEQ, D], F32, kind="ExternalOutput")
    r_a = nc.dram_tensor("r_a", [DCH, 128, T], F32, kind="Internal")
    r_b = nc.dram_tensor("r_b", [DCH, 128, T], F32, kind="Internal")
    scr = {
        "xn0h": nc.dram_tensor("s_xn0h", [DCH, 128, T], F32, kind="Internal"),
        "xr": nc.dram_tensor("s_xr", [DCH, 128, T], BF16, kind="Internal"),
        "qT": nc.dram_tensor("s_qT", [DCH, 128, T], BF16, kind="Internal"),
        "kT": nc.dram_tensor("s_kT", [2, 128, T], BF16, kind="Internal"),
        "kTf": nc.dram_tensor("s_kTf", [DCH, 128, T], BF16, kind="Internal"),
        "vtm": nc.dram_tensor("s_vtm", [T, 256], BF16, kind="Internal"),
        "vtmf": nc.dram_tensor("s_vtmf", [T, D], BF16, kind="Internal"),
        "ao": nc.dram_tensor("s_ao", [DCH, 128, T], BF16, kind="Internal"),
        "h1h": nc.dram_tensor("s_h1h", [DCH, 128, T], F32, kind="Internal"),
        "xn2": nc.dram_tensor("s_xn2", [DCH, 128, T], BF16, kind="Internal"),
        "comb": nc.dram_tensor("s_comb", [8, T], F32, kind="Internal"),
        "w3b": nc.dram_tensor("s_w3b", [NEXP, FCH, 128, D], BF16, kind="Internal"),
        "h_d": nc.dram_tensor("s_hd", [NEXP, FCH, 128, T], BF16, kind="Internal"),
        "xnb": nc.dram_tensor("s_xnb", [DCH, 128, T], BF16, kind="Internal"),
        "wgT": nc.dram_tensor("s_wgT", [DCH, 128, H_L1], BF16, kind="Internal"),
        "wuT": nc.dram_tensor("s_wuT", [DCH, 128, H_L1], BF16, kind="Internal"),
        "wdT": nc.dram_tensor("s_wdT", [HCH, 128, D], BF16, kind="Internal"),
        "wqT": nc.dram_tensor("s_wqT", [DCH, 128, D], BF16, kind="Internal"),
        "wkT": nc.dram_tensor("s_wkT", [DCH, 128, D], BF16, kind="Internal"),
        "wvT": nc.dram_tensor("s_wvT", [DCH, 128, D], BF16, kind="Internal"),
        "woT": nc.dram_tensor("s_woT", [DCH, 128, D], BF16, kind="Internal"),
    }
    with ExitStack() as ctx:
        p = Prog(nc, ctx)
        k1 = Consts1(p)
        g = load_globals(p, hc)
        em = Emit(p); k1.prewarm(em); em.replay()
        em = Emit(p); pass_t_in(p, em, g, x, r_a)
        emit_l0ffn_app(p, g, k1, None, hc, W, r_a, r_b, scr, "s1")
        emit_l0attn_app(p, g, k1, None, hc, W, r_b, r_a, scr)
        emit_l0ffn_app(p, g, k1, None, hc, W, r_a, r_b, scr, "s3")
        emit_l1ffn_app(p, g, k1, None, hc, W, r_b, r_a, scr, do_prep=True)
        emit_l1attn_app(p, g, k1, None, hc, W, r_a, r_b, scr, do_prep=True)
        emit_l1ffn_app(p, g, k1, None, hc, W, r_b, r_a, scr, do_prep=False)
        em = Emit(p); pass_t_out(p, em, g, r_a, y)
    return nc, consts, ["x_in"] + wn


def kernel(**inputs):
    x = np.ascontiguousarray(inputs["L0_attn_x"], dtype=np.float32)
    weights_np = {}
    for n in WEIGHT_SHAPES:
        v = np.asarray(inputs[n])
        if n == "L0_attn_gamma":
            v = v.reshape(1, 1).astype(np.float32)
        weights_np[n] = np.ascontiguousarray(v)
    shards = [np.ascontiguousarray(x[2 * i:2 * i + 2]) for i in range(NCORES)]
    nc, consts, innames = get_prog("fused")
    cf = const_feeds(consts)
    in_maps = []
    for core in range(NCORES):
        m = dict(cf)
        for n in innames:
            m[n] = shards[core] if n == "x_in" else weights_np[n]
        in_maps.append(m)
    res = run_bass_kernel_spmd(nc, in_maps, core_ids=list(range(NCORES)))
    y = np.concatenate([res.results[i]["y_out"] for i in range(NCORES)], axis=0)
    return y.reshape(16, SEQ, D)


# version 05 (final)
# speedup vs baseline: 1.3881x; 1.2696x over previous
import math
from contextlib import ExitStack

import numpy as np
import ml_dtypes

import concourse.bass as bass
import concourse.mybir as mybir
from concourse.bass_utils import run_bass_kernel_spmd

F32 = mybir.dt.float32
BF16 = mybir.dt.bfloat16
I32 = mybir.dt.int32
AF = mybir.ActivationFunctionType
ALU = mybir.AluOpType
AX = mybir.AxisListType

BF = ml_dtypes.bfloat16

D = 1024
NCORES = 8
SEQ = 2048
SPC = 2
T = SPC * SEQ
DCH = D // 128
TB = 512
NTB = T // TB
NTC = T // 128
TBC = TB // 128

F_FF = 4096
FCH = F_FF // 128
NEXP = 8
H_L1 = 3072
HCH = H_L1 // 128

ENGINES = ("tensor", "vector", "scalar", "gpsimd", "sync")



class Chan:
    def __init__(self, nc, name, ctx):
        self.sem = ctx.enter_context(nc.semaphore(name))
        self.n = 0

    def take(self, step):
        self.n += step
        return self.n


class Prog:

    def __init__(self, nc, ctx):
        self.nc = nc
        self.ctx = ctx
        self.chans = {}
        self.banks = [ctx.enter_context(nc.psum_tensor(f"bank{i}", [128, 512], F32))
                      for i in range(8)]
        self.uid = 0
        self.em_last = {}

    def chan(self, name) -> Chan:
        if name not in self.chans:
            self.chans[name] = Chan(self.nc, name, self.ctx)
        return self.chans[name]

    def fresh(self, base):
        self.uid += 1
        return f"{base}_{self.uid}"


class Emit:

    def __init__(self, p: Prog):
        self.p = p
        self.streams = {e: [] for e in ENGINES}
        self.last = dict(p.em_last) if hasattr(p, "em_last") else {}
        self.hw = p.em_hw if hasattr(p, "em_hw") else {}
        p.em_hw = self.hw

    def on(self, eng, fn):
        self.streams[eng].append(fn)

    def wait(self, eng, chan: Chan, tok):
        key = (eng, id(chan))
        if self.hw.get(key, -1) >= tok:
            return
        self.hw[key] = tok
        self.on(eng, lambda e: e.wait_ge(chan.sem, tok))

    def dma(self, queue, dst, src, chan: Chan, waits=()):
        if chan.n > 0:
            self.wait(queue, chan, chan.n)
        for (c, t) in waits:
            self.wait(queue, c, t)
        tok = chan.take(16)
        self.on(queue, lambda e: e.dma_start(dst, src).then_inc(chan.sem, 16))
        return tok

    def op(self, eng, chan, fn, waits=()):
        if eng in ("vector", "scalar", "gpsimd") and eng in self.last:
            c0, t0 = self.last[eng]
            self.wait(eng, c0, t0)
        for (c, t) in waits:
            self.wait(eng, c, t)
        tok = chan.take(1) if chan is not None else None
        if chan is not None:
            self.on(eng, lambda e: fn(e).then_inc(chan.sem, 1))
            if eng in ("vector", "scalar", "gpsimd"):
                self.last[eng] = (chan, tok)
                self.p.em_last[eng] = (chan, tok)
        else:
            self.on(eng, lambda e: fn(e))
        return tok

    def replay(self):
        nc = self.p.nc
        streams = self.streams
        with nc.Block() as block:
            if streams["sync"]:
                @block.sync
                def _(e):
                    for fn in streams["sync"]:
                        fn(e)
            if streams["gpsimd"]:
                @block.gpsimd
                def _(e):
                    for fn in streams["gpsimd"]:
                        fn(e)
            if streams["tensor"]:
                @block.tensor
                def _(e):
                    for fn in streams["tensor"]:
                        fn(e)
            if streams["vector"]:
                @block.vector
                def _(e):
                    for fn in streams["vector"]:
                        fn(e)
            if streams["scalar"]:
                @block.scalar
                def _(e):
                    for fn in streams["scalar"]:
                        fn(e)



def host_constants():
    c = {}
    c["ident_f"] = np.eye(128, dtype=np.float32)
    c["ident_b"] = np.eye(128).astype(BF)
    c["ones_col_f"] = np.ones((128, 1), dtype=np.float32)
    c["ones_col_b"] = np.ones((128, 1)).astype(BF)
    c["ones_row_b"] = np.ones((1, TB)).astype(BF)
    c["ones_row_f"] = np.ones((1, TB), dtype=np.float32)
    c["ones_m_b"] = np.ones((1, 128)).astype(BF)
    c["ones_m_f"] = np.ones((1, 128), dtype=np.float32)
    half = D // 2
    theta = 1.0 / (10000.0 ** (np.arange(half, dtype=np.float64) / half))
    pos = np.arange(SEQ, dtype=np.float64)
    ang = np.outer(theta, pos)
    c["rope_cos"] = np.cos(ang).astype(np.float32)
    c["rope_sin"] = np.sin(ang).astype(np.float32)
    i = np.arange(128)
    dif = i[:, None] - i[None, :]
    c["tdiff_T"] = np.maximum(-dif, 0).astype(np.float32)
    c["tlow_T"] = (dif <= 0).astype(np.float32)
    c["bin_causal"] = (dif <= 0).astype(np.float32)
    c["bin_tri_hi"] = (dif >= 0).astype(np.float32)
    row0 = np.zeros((128, 128), np.float32); row0[0, :] = 1.0
    col0 = np.zeros((128, 128), np.float32); col0[:, 0] = 1.0
    c["bin_row0"] = row0
    c["bin_col0"] = col0
    c["bin_zero"] = np.zeros((128, 128), np.float32)
    c["bin_tri_hi_row0"] = np.maximum(c["bin_tri_hi"], row0)
    c["bin_causal_col0"] = np.maximum(c["bin_causal"], col0)
    return c


_CONST = host_constants()


def declare_constants(nc, names):
    h = {}
    for k in names:
        v = _CONST[k]
        dt = {np.dtype(np.float32): F32, np.dtype(BF): BF16, np.dtype(np.int32): I32}[v.dtype]
        h[k] = nc.dram_tensor("c_" + k, list(v.shape), dt, kind="ExternalInput")
    return h


def const_feeds(names):
    return {"c_" + k: _CONST[k] for k in names}




class Pool:

    def __init__(self, p: Prog, pctx, name, nslots, shape, dtype, queue="sync"):
        self.p = p
        self.name = name
        self.nslots = nslots
        self.queue = queue
        self.tiles = [pctx.enter_context(p.nc.sbuf_tensor(p.fresh(f"{name}s{i}"), list(shape), dtype))
                      for i in range(nslots)]
        self.lch = [p.chan(f"{name}_l{i}") for i in range(nslots)]
        self.it = 0
        self.ltok = {}
        self.rel = {}

    def slot(self, i):
        return i % self.nslots

    def tile(self, i):
        return self.tiles[self.slot(i)]

    def begin(self, em, extra_waits=()):
        i = self.it
        self.it += 1
        w = list(extra_waits)
        if i >= self.nslots:
            w.append(self.rel[i - self.nslots])
        for (c, t) in w:
            em.wait(self.queue, c, t)
        return i, self.tiles[self.slot(i)]

    def dma(self, em, i, dst, src):
        self.ltok[i] = em.dma(self.queue, dst, src, self.lch[self.slot(i)])
        return self.ltok[i]

    def tok(self, i):
        return self.ltok[i]

    def chan_of(self, i):
        return self.lch[self.slot(i)]

    def set_release(self, i, chan, tok):
        self.rel[i] = (chan, tok)


class StorePool:

    def __init__(self, p: Prog, pctx, name, nslots, shape, dtype, queue="scalar"):
        self.p = p
        self.nslots = nslots
        self.queue = queue
        self.tiles = [pctx.enter_context(p.nc.sbuf_tensor(p.fresh(f"{name}s{i}"), list(shape), dtype))
                      for i in range(nslots)]
        self.sch = [p.chan(f"{name}_t{i}") for i in range(nslots)]
        self.it = 0
        self.stok = {}

    def slot(self, i):
        return i % self.nslots

    def begin(self):
        i = self.it
        self.it += 1
        w = []
        prev = i - self.nslots
        if prev >= 0:
            w.append((self.sch[self.slot(i)], self.stok[prev]))
        return i, self.tiles[self.slot(i)], w

    def store(self, em, i, dst, src, waits=()):
        for (c, t) in waits:
            em.wait(self.queue, c, t)
        self.stok[i] = em.dma(self.queue, dst, src, self.sch[self.slot(i)])
        return self.stok[i]


def op_multi(em, eng, chan, fn, waits=(), extra_incs=()):
    for (c, t) in waits:
        em.wait(eng, c, t)
    tok = chan.take(1) if chan is not None else None
    if chan is not None:
        em.on(eng, lambda e: fn(e).then_inc(chan.sem, 1))
    else:
        em.on(eng, lambda e: fn(e))
    for (c, s) in extra_incs:
        c.take(s)
        em.on(eng, lambda e, c=c, s=s: e.sem_inc(c.sem, s))
    return tok


def quiesce_dma(p, em):
    for name, ch in list(p.chans.items()):
        if ch.n == 0:
            continue
    for name, ch in list(p.chans.items()):
        if name in ("pe", "vector", "scalar", "gpsimd"):
            continue
        if ch.n > 0:
            em.wait("sync", ch, ch.n)



GLOBAL_CONSTS = ["ident_f", "ident_b", "ones_col_f", "ones_col_b", "ones_row_b",
                 "ones_row_f", "ones_m_b", "ones_m_f", "bin_causal", "bin_zero",
                 "bin_tri_hi", "bin_row0", "bin_col0", "bin_tri_hi_row0",
                 "bin_causal_col0", "tdiff_T", "tlow_T"]


def load_globals(p: Prog, hc):
    em = Emit(p)
    out = {}
    ld = p.chan("gconst")
    tok = None
    for name in GLOBAL_CONSTS:
        if name not in hc:
            continue
        ap = hc[name].ap()
        t = p.ctx.enter_context(p.nc.sbuf_tensor(p.fresh("g" + name), list(ap.shape), ap.dtype))
        out[name] = t
        tok = em.dma("sync", t[:], ap, ld)
    em.wait("sync", ld, tok)
    em.replay()
    return out



def pass_t_in(p, em, g, x_dram, r_dram):
    nc = p.nc
    pe, dve = p.chan("pe"), p.chan("vector")
    with ExitStack() as pctx:
        lp = Pool(p, pctx, "ti_l", 2, [128, D], F32, queue="sync")
        sp = StorePool(p, pctx, "ti_o", 2, [128, D], F32, queue="scalar")
        ident = g["ident_f"]
        x_flat = x_dram.ap().rearrange("b t d -> (b t) d")
        bank_gate = [None, None]
        for tc in range(NTC):
            i, xt = lp.begin(em)
            lp.dma(em, i, xt[:], x_flat[tc * 128:(tc + 1) * 128, :])
            em.wait("tensor", lp.chan_of(i), lp.tok(i))
            par = tc % 2
            if bank_gate[par] is not None:
                em.wait("tensor", dve, bank_gate[par])
            tp_tok = None
            for c in range(DCH):
                bank = p.banks[2 * par + c // 4]
                tp_tok = em.op("tensor", pe,
                               lambda e, c=c, bank=bank, xt=xt: e.transpose(
                                   bank[:, (c % 4) * 128:(c % 4) * 128 + 128],
                                   xt[:, c * 128:(c + 1) * 128], ident[:]))
                if c == DCH - 1:
                    lp.set_release(i, pe, tp_tok)
            j, stg, gate = sp.begin()
            e0 = em.op("vector", dve, lambda e, stg=stg, par=par: e.tensor_copy(stg[:, 0:512], p.banks[2 * par][:]),
                       waits=[(pe, tp_tok)] + gate)
            e1 = em.op("vector", dve, lambda e, stg=stg, par=par: e.tensor_copy(stg[:, 512:1024], p.banks[2 * par + 1][:]))
            bank_gate[par] = e1
            sp.store(em, j, r_dram.ap()[:, :, tc * 128:(tc + 1) * 128].rearrange("c p n -> p c n"),
                     stg.ap().rearrange("p (c n) -> p c n", c=DCH), waits=[(dve, e1)])
        quiesce_dma(p, em)
        em.replay()



def pass_t_out(p, em, g, r_dram, y_dram):
    nc = p.nc
    pe, dve = p.chan("pe"), p.chan("vector")
    with ExitStack() as pctx:
        lp = Pool(p, pctx, "to_l", 2, [128, D], F32, queue="sync")
        sp = StorePool(p, pctx, "to_o", 2, [128, D], F32, queue="scalar")
        ident = g["ident_f"]
        y_flat = y_dram.ap().rearrange("b t d -> (b t) d")
        bank_gate = [None, None]
        for tc in range(NTC):
            i, ft = lp.begin(em)
            lp.dma(em, i, ft.ap().rearrange("p (c n) -> p c n", c=DCH),
                   r_dram.ap()[:, :, tc * 128:(tc + 1) * 128].rearrange("c p n -> p c n"))
            em.wait("tensor", lp.chan_of(i), lp.tok(i))
            par = tc % 2
            if bank_gate[par] is not None:
                em.wait("tensor", dve, bank_gate[par])
            tp_tok = None
            for c in range(DCH):
                bank = p.banks[2 * par + c // 4]
                tp_tok = em.op("tensor", pe,
                               lambda e, c=c, bank=bank, ft=ft: e.transpose(
                                   bank[:, (c % 4) * 128:(c % 4) * 128 + 128],
                                   ft[:, c * 128:(c + 1) * 128], ident[:]))
                if c == DCH - 1:
                    lp.set_release(i, pe, tp_tok)
            j, stg, gate = sp.begin()
            e0 = em.op("vector", dve, lambda e, stg=stg, par=par: e.tensor_copy(stg[:, 0:512], p.banks[2 * par][:]),
                       waits=[(pe, tp_tok)] + gate)
            e1 = em.op("vector", dve, lambda e, stg=stg, par=par: e.tensor_copy(stg[:, 512:1024], p.banks[2 * par + 1][:]))
            bank_gate[par] = e1
            sp.store(em, j, y_flat[tc * 128:(tc + 1) * 128, :], stg[:], waits=[(dve, e1)])
        quiesce_dma(p, em)
        em.replay()



def pass_stats(p, em, g, r_dram, sum_row, sq_row):
    nc = p.nc
    pe, dve, act = p.chan("pe"), p.chan("vector"), p.chan("scalar")
    ones_f = g["ones_col_f"]
    with ExitStack() as pctx:
        lp = Pool(p, pctx, "st_l", 2, [128, DCH, TB], F32, queue="sync")
        sq = [pctx.enter_context(nc.sbuf_tensor(p.fresh("stq"), [128, TB], F32)) for _ in range(2)]
        sq_gate = [None, None]
        row_gate = None
        for tb in range(NTB):
            i, rt = lp.begin(em)
            lp.dma(em, i, rt.ap(), r_dram.ap()[:, :, tb * TB:(tb + 1) * TB].rearrange("c p n -> p c n"))
            mm = None
            for c in range(DCH):
                k = c % 2
                wsq = [(lp.chan_of(i), lp.tok(i))]
                if sq_gate[k] is not None:
                    wsq.append((pe, sq_gate[k]))
                sq_tok = em.op("scalar", act, lambda e, c=c, k=k, rt=rt: e.activation(sq[k][:], rt[:, c, :], AF.Square),
                               waits=wsq)
                w = [(lp.chan_of(i), lp.tok(i)), (act, sq_tok)]
                if c == 0 and row_gate is not None:
                    w.append((dve, row_gate))
                em.op("tensor", pe, lambda e, c=c, rt=rt: e.matmul(
                    p.banks[6][0:1, 0:TB], ones_f[:], rt[:, c, :],
                    start=(c == 0), stop=(c == DCH - 1)), waits=w)
                mm = em.op("tensor", pe, lambda e, c=c, k=k: e.matmul(
                    p.banks[7][0:1, 0:TB], ones_f[:], sq[k][:],
                    start=(c == 0), stop=(c == DCH - 1)))
                if c == DCH - 1:
                    lp.set_release(i, pe, mm)
                sq_gate[k] = mm
            c0 = em.op("vector", dve, lambda e, tb=tb: e.tensor_copy(sum_row[0:1, tb * TB:(tb + 1) * TB], p.banks[6][0:1, 0:TB]),
                       waits=[(pe, mm)])
            row_gate = em.op("vector", dve, lambda e, tb=tb: e.tensor_copy(sq_row[0:1, tb * TB:(tb + 1) * TB], p.banks[7][0:1, 0:TB]))
        quiesce_dma(p, em)
        em.replay()



class Consts1:
    def __init__(self, p: Prog):
        self.p = p
        self.cache = {}

    def prewarm(self, em, vals=(1e-5, 1e-6)):
        for v in vals:
            self.get(em, v)

    def get(self, em, val):
        val = float(val)
        if val not in self.cache:
            t = self.p.ctx.enter_context(self.p.nc.sbuf_tensor(self.p.fresh("k1"), [1, 1], F32))
            tok = em.op("vector", self.p.chan("vector"), lambda e: e.memset(t[:], val))
            self.cache[val] = (t, tok)
        t, tok = self.cache[val]
        return t[0:1, 0:1], tok



def rows_ln(p, em, k1: Consts1, sum_row, sq_row, m_row, sinv_row):
    dve, act = p.chan("vector"), p.chan("scalar")
    b_eps, btok = k1.get(em, 1e-5)
    t1 = em.op("vector", dve, lambda e: e.tensor_scalar_mul(m_row[:], sum_row[:], 1.0 / D))
    t2 = em.op("vector", dve, lambda e: e.tensor_mul(sinv_row[:], m_row[:], m_row[:]), waits=[(dve, t1)])
    t3 = em.op("vector", dve, lambda e: e.scalar_tensor_tensor(sinv_row[:], sq_row[:], 1.0 / D, sinv_row[:],
                                                               ALU.mult, ALU.subtract), waits=[(dve, t2)])
    t4 = em.op("scalar", act, lambda e: e.activation(sinv_row[:], sinv_row[:], AF.Ln, bias=b_eps),
               waits=[(dve, t3), (dve, btok)])
    t5 = em.op("scalar", act, lambda e: e.activation(sinv_row[:], sinv_row[:], AF.Exp, scale=-0.5),
               waits=[(act, t4)])
    return t5


def rows_mul(p, em, out_row, a_row, b_row, wa, wb, scale=1.0):
    dve = p.chan("vector")
    t1 = em.op("vector", dve, lambda e: e.tensor_mul(out_row[:], a_row[:], b_row[:]), waits=list(wa) + list(wb))
    if scale != 1.0:
        t1 = em.op("vector", dve, lambda e: e.tensor_scalar_mul(out_row[:], out_row[:], scale), waits=[(dve, t1)])
    return t1


def row_to_cols(p, em, g, row_sbuf, n, out_cols, row_waits=()):
    pe, dve = p.chan("pe"), p.chan("vector")
    ident = g["ident_f"]
    tok = None
    prev = None
    for c in range(n // 128):
        w = (list(row_waits) + [(dve, dve.n)]) if c == 0 else []
        if prev is not None:
            w.append((dve, prev))
        tp = em.op("tensor", pe, lambda e, c=c: e.transpose(
            p.banks[5][0:128, 0:1], row_sbuf[0:1, c * 128:(c + 1) * 128], ident[0:1, 0:1]), waits=w)
        prev = em.op("vector", dve, lambda e, c=c: e.tensor_copy(out_cols[:, c:c + 1], p.banks[5][0:128, 0:1]),
                     waits=[(pe, tp)])
        tok = prev
    return tok



def pass_xn(p, em, g, name, r_dram, out_dram, out_dt, m_row, a_row, w_cols=None,
            row_waits=()):
    nc = p.nc
    pe, dve, act = p.chan("pe"), p.chan("vector"), p.chan("scalar")
    ones_m_f = g["ones_m_f"]
    with ExitStack() as pctx:
        lp = Pool(p, pctx, name + "_l", 2, [128, DCH, TB], F32, queue="sync")
        sp = StorePool(p, pctx, name + "_o", 2, [128, DCH, TB], out_dt, queue="scalar")
        mb = pctx.enter_context(nc.sbuf_tensor(p.fresh(name + "mb"), [128, TB], F32))
        ab = pctx.enter_context(nc.sbuf_tensor(p.fresh(name + "ab"), [128, TB], F32))
        tt = pctx.enter_context(nc.sbuf_tensor(p.fresh(name + "tt"), [128, TB], F32))
        bc_gate = None
        for tb in range(NTB):
            i, rt = lp.begin(em)
            lp.dma(em, i, rt.ap(), r_dram.ap()[:, :, tb * TB:(tb + 1) * TB].rearrange("c p n -> p c n"))
            w = list(row_waits) if tb == 0 else []
            if bc_gate is not None:
                w.append((dve, bc_gate))
            if m_row is not None:
                t_m = em.op("tensor", pe, lambda e, tb=tb: e.matmul(
                    p.banks[4][:, 0:TB], ones_m_f[0:1, :], m_row[0:1, tb * TB:(tb + 1) * TB],
                    start=True, stop=True), waits=w)
                em.op("scalar", act, lambda e: e.copy(mb[:], p.banks[4][:]), waits=[(pe, t_m)])
            t_a = em.op("tensor", pe, lambda e, tb=tb: e.matmul(
                p.banks[5][:, 0:TB], ones_m_f[0:1, :], a_row[0:1, tb * TB:(tb + 1) * TB],
                start=True, stop=True), waits=(w if m_row is None else []))
            ab_tok = em.op("scalar", act, lambda e: e.copy(ab[:], p.banks[5][:]), waits=[(pe, t_a)])
            j, stg, gate = sp.begin()
            last = None
            for c in range(DCH):
                wt = [(lp.chan_of(i), lp.tok(i)), (act, ab_tok)] + (gate if c == 0 else [])
                if m_row is not None:
                    t1 = em.op("vector", dve, lambda e, c=c, rt=rt: e.tensor_sub(tt[:], rt[:, c, :], mb[:]),
                               waits=wt)
                    src = tt
                    wt2 = [(dve, t1)]
                else:
                    src, wt2 = None, wt
                if w_cols is not None:
                    if src is None:
                        t2 = em.op("vector", dve, lambda e, c=c, rt=rt: e.tensor_mul(tt[:], rt[:, c, :], ab[:]), waits=wt2)
                    else:
                        t2 = em.op("vector", dve, lambda e: e.tensor_mul(tt[:], tt[:], ab[:]), waits=wt2)
                    last = em.op("vector", dve, lambda e, c=c, stg=stg: e.tensor_single_scalar(
                        stg[:, c, :], tt[:], w_cols[:, c:c + 1], ALU.mult), waits=[(dve, t2)])
                else:
                    if src is None:
                        last = em.op("vector", dve, lambda e, c=c, rt=rt, stg=stg: e.tensor_mul(stg[:, c, :], rt[:, c, :], ab[:]), waits=wt2)
                    else:
                        last = em.op("vector", dve, lambda e, c=c, stg=stg: e.tensor_mul(stg[:, c, :], tt[:], ab[:]), waits=wt2)
                if c == DCH - 1:
                    lp.set_release(i, dve, last)
            bc_gate = last
            sp.store(em, j, out_dram.ap()[:, :, tb * TB:(tb + 1) * TB].rearrange("c p n -> p c n"),
                     stg.ap(), waits=[(dve, last)])
        quiesce_dma(p, em)
        em.replay()



def pass_linear(p, em, g, name, xn_dram, nin, w_handle, nout, *,
                wsrc="f32", bias_row=None, bias_waits=(), w_col0=0, bias_col0=0,
                out_dram=None, epilogue=None):
    nc = p.nc
    pe, dve, act = p.chan("pe"), p.chan("vector"), p.chan("scalar")
    nin_ch = nin // 128
    noc = nout // 128
    ones_row_b = g["ones_row_b"]
    with ExitStack() as pctx:
        wp = Pool(p, pctx, name + ("_wf" if wsrc == "f32" else "_wb"), 2, [128, nin_ch, 128], BF16,
                  queue=("gpsimd" if wsrc == "f32" else "sync"))
        xn_sb = pctx.enter_context(nc.sbuf_tensor(p.fresh(name + "xn"), [128, nin_ch, T], BF16))
        xch = p.chan(name + "_x0")
        xtok = em.dma("sync", xn_sb.ap(), xn_dram.ap().rearrange("c p t -> p c t"), xch)
        if epilogue is None:
            sp = StorePool(p, pctx, name + "_o", 2, [128, TB], BF16, queue="scalar")
        else:
            sp = StorePool(p, pctx, name + "_o", 2, [128, TB], F32, queue="scalar")
            ap_ = Pool(p, pctx, name + "_a", 2, [128, TB], F32, queue="sync")
            ev = pctx.enter_context(nc.sbuf_tensor(p.fresh(name + "ev"), [128, TB], F32))
            ev_gate = None
        if epilogue is not None:
            kind, add_dram, res_dram, scale = epilogue
        bank_gate = {0: None, 1: None}
        for oc in range(noc):
            iw, wt = wp.begin(em)
            if wsrc == "f32":
                src = w_handle.ap().rearrange("(i p) o -> p i o", p=128)[:, :, w_col0 + oc * 128:w_col0 + (oc + 1) * 128]
            else:
                src = w_handle.ap()[:, :, w_col0 + oc * 128:w_col0 + (oc + 1) * 128].rearrange("i p o -> p i o")
            wp.dma(em, iw, wt.ap(), src)
            last_mm = None
            for tb in range(NTB):
                bk = tb % 2
                bank = p.banks[bk]
                w0 = [(wp.chan_of(iw), wp.tok(iw)), (xch, xtok)]
                if bank_gate[bk] is not None:
                    w0.append(bank_gate[bk])
                mm = None
                for i in range(nin_ch):
                    mm = em.op("tensor", pe, lambda e, i=i, tb=tb, wt=wt, bank=bank: e.matmul(
                        bank[:], wt[:, i, :], xn_sb[:, i, tb * TB:(tb + 1) * TB],
                        start=(i == 0), stop=(i == nin_ch - 1 and bias_row is None)),
                        waits=(w0 if i == 0 else []))
                if bias_row is not None:
                    mm = em.op("tensor", pe, lambda e, oc=oc, bank=bank: e.matmul(
                        bank[:], bias_row[0:1, bias_col0 + oc * 128:bias_col0 + (oc + 1) * 128],
                        ones_row_b[0:1, :], start=False, stop=True),
                        waits=(list(bias_waits) if (oc == 0 and tb == 0) else []))
                j, stg, gate = sp.begin()
                if epilogue is None:
                    e1 = em.op("scalar", act, lambda e, stg=stg, bank=bank: e.copy(stg[:], bank[:]),
                               waits=[(pe, mm)] + gate)
                    bank_gate[bk] = (act, e1)
                    sp.store(em, j, out_dram.ap()[oc, :, tb * TB:(tb + 1) * TB], stg.ap(),
                             waits=[(act, e1)])
                else:
                    ia, at = ap_.begin(em)
                    ap_.dma(em, ia, at.ap(), add_dram.ap()[oc, :, tb * TB:(tb + 1) * TB])
                    w1 = [(pe, mm)]
                    if ev_gate is not None:
                        w1.append((dve, ev_gate))
                    e1 = em.op("scalar", act, lambda e, bank=bank: e.activation(ev[:], bank[:], AF.Copy, scale=float(scale)),
                               waits=w1)
                    bank_gate[bk] = (act, e1)
                    e2 = em.op("vector", dve, lambda e, stg=stg, at=at: e.tensor_add(stg[:], ev[:], at[:]),
                               waits=[(act, e1), (ap_.chan_of(ia), ap_.tok(ia))] + gate)
                    ap_.set_release(ia, dve, e2)
                    ev_gate = e2
                    sp.store(em, j, res_dram.ap()[oc, :, tb * TB:(tb + 1) * TB], stg.ap(),
                             waits=[(dve, e2)])
                last_mm = mm
            wp.set_release(iw, pe, last_mm)
        quiesce_dma(p, em)
        em.replay()



def pass_linear_tm(p, em, g, name, xn_dram, nin, w_handle, nout, out_dram, *,
                   wsrc="f32", bias_row=None, bias_waits=(), w_col0=0, bias_col0=0):
    nc = p.nc
    pe, dve, act = p.chan("pe"), p.chan("vector"), p.chan("scalar")
    nin_ch = nin // 128
    ncb = (nout + 511) // 512
    ones_m_b = g["ones_m_b"]
    with ExitStack() as pctx:
        wt = pctx.enter_context(nc.sbuf_tensor(p.fresh(name + "w"), [128, nin_ch, nout], BF16))
        wch = p.chan(name + ("_wf0" if wsrc == "f32" else "_wb0"))
        if wsrc == "f32":
            wtok = em.dma("gpsimd", wt.ap(),
                          w_handle.ap().rearrange("(i p) o -> p i o", p=128)[:, :, w_col0:w_col0 + nout], wch)
        else:
            wtok = em.dma("sync", wt.ap(),
                          w_handle.ap()[:, :, w_col0:w_col0 + nout].rearrange("i p o -> p i o"), wch)
        xn_sb = pctx.enter_context(nc.sbuf_tensor(p.fresh(name + "xn"), [128, nin_ch, T], BF16))
        xch = p.chan(name + "_x0")
        xtok = em.dma("sync", xn_sb.ap(), xn_dram.ap().rearrange("c p t -> p c t"), xch)
        sp = StorePool(p, pctx, name + "_o", 2, [128, nout], BF16, queue="scalar")
        bank_gate = {0: None, 1: None}
        for tc in range(NTC):
            j, stg, gate = sp.begin()
            last_e = None
            for cb in range(ncb):
                n0, n1 = cb * 512, min((cb + 1) * 512, nout)
                bk = 2 + cb % 2
                bank = p.banks[bk]
                w0 = [(xch, xtok), (wch, wtok)]
                if bank_gate[cb % 2] is not None:
                    w0.append(bank_gate[cb % 2])
                mm = None
                for i in range(nin_ch):
                    mm = em.op("tensor", pe, lambda e, i=i, tc=tc, n0=n0, n1=n1, bank=bank: e.matmul(
                        bank[0:128, 0:n1 - n0], xn_sb[:, i, tc * 128:(tc + 1) * 128], wt[:, i, n0:n1],
                        start=(i == 0), stop=(i == nin_ch - 1 and bias_row is None)),
                        waits=(w0 if i == 0 else []))
                if bias_row is not None:
                    mm = em.op("tensor", pe, lambda e, n0=n0, n1=n1, bank=bank: e.matmul(
                        bank[0:128, 0:n1 - n0], ones_m_b[0:1, :], bias_row[0:1, bias_col0 + n0:bias_col0 + n1],
                        start=False, stop=True),
                        waits=(list(bias_waits) if tc == 0 and cb == 0 else []))
                e1 = em.op("scalar", act, lambda e, stg=stg, bank=bank, n0=n0, n1=n1: e.copy(
                    stg[:, n0:n1], bank[0:128, 0:n1 - n0]),
                    waits=[(pe, mm)] + (gate if cb == 0 else []))
                bank_gate[cb % 2] = (act, e1)
                last_e = e1
            sp.store(em, j, out_dram.ap()[tc * 128:(tc + 1) * 128, :], stg.ap(), waits=[(act, last_e)])
        quiesce_dma(p, em)
        em.replay()



def pass_att(p, em, g, name, spec, q_dram, k_dram, v_dram, out_dram, gamma_dram=None):
    nc = p.nc
    pe, dve, act = p.chan("pe"), p.chan("vector"), p.chan("scalar")
    nh, dk, dv = spec["nh"], spec["dk"], spec["dv"]
    scale = spec["scale"]
    kch_n, qch_n = spec["kch"], spec["qch"]
    dv_total = spec["dv_total"]
    NKC = SEQ // 128
    NQB = SEQ // TB
    NTB_S = SEQ // TB
    ones_col_b = g["ones_col_b"]
    ones_m_f = g["ones_m_f"]
    full_q = dk > 128
    with ExitStack() as pctx:
        kp = Pool(p, pctx, name + "_k", 1, [128, kch_n, SEQ], BF16, queue="sync")
        vp = Pool(p, pctx, name + "_v", 1, [128, NKC, dv_total], BF16, queue="sync")
        if full_q:
            qp = Pool(p, pctx, name + "_q", 2, [128, qch_n, TB], BF16, queue="sync")
        else:
            qp = Pool(p, pctx, name + "_q", 2, [128, SEQ], BF16, queue="sync")
        P_sb = pctx.enter_context(nc.sbuf_tensor(p.fresh(name + "P"), [128, NKC, TB], BF16))
        qn2 = pctx.enter_context(nc.sbuf_tensor(p.fresh(name + "qn2"), [1, SEQ], F32))
        negb = pctx.enter_context(nc.sbuf_tensor(p.fresh(name + "nb"), [1, SEQ], BF16))
        kn2 = pctx.enter_context(nc.sbuf_tensor(p.fresh(name + "kn2"), [1, SEQ], F32))
        mk = pctx.enter_context(nc.sbuf_tensor(p.fresh(name + "mk"), [1, 16], F32))
        sqt2 = [pctx.enter_context(nc.sbuf_tensor(p.fresh(name + "sqt"), [128, TB], BF16)) for _ in range(2)]
        nb_b = pctx.enter_context(nc.sbuf_tensor(p.fresh(name + "nbb"), [128, TB], F32))
        rs_row = pctx.enter_context(nc.sbuf_tensor(p.fresh(name + "rs"), [1, TB], F32))
        rb = pctx.enter_context(nc.sbuf_tensor(p.fresh(name + "rb"), [128, TB], F32))
        dtmp = pctx.enter_context(nc.sbuf_tensor(p.fresh(name + "dt"), [128, 128], F32))
        corr = pctx.enter_context(nc.sbuf_tensor(p.fresh(name + "cr"), [128, 128], F32))
        sp = StorePool(p, pctx, name + "_o", 2, [128, TB], BF16, queue="scalar")
        b6_gate = None
        nbb_gate = None
        rb_gate = None
        P_gate = {}
        bank_gate = {0: None, 1: None}
        av_gate = {0: None, 1: None}

        sq_par = [0]
        sq_gates = {0: None, 1: None}

        def stat_row(src_fn, n_acc, out_row, tb):
            nonlocal b6_gate
            mm = None
            for c in range(n_acc):
                par = sq_par[0] % 2
                sq_par[0] += 1
                sqt = sqt2[par]
                ap_, rows, base, extra_w = src_fn(c)
                wsq = list(extra_w)
                if sq_gates[par] is not None:
                    wsq.append(sq_gates[par])
                s1 = em.op("scalar", act, lambda e, ap_=ap_, rows=rows, base=base, sqt=sqt: e.activation(
                    sqt[base:base + rows, 0:TB], ap_, AF.Square), waits=wsq)
                w = [(act, s1)]
                if c == 0 and b6_gate is not None:
                    w.append(b6_gate)
                mm = em.op("tensor", pe, lambda e, rows=rows, base=base, c=c, sqt=sqt: e.matmul(
                    p.banks[6][0:1, 0:TB], ones_col_b[base:base + rows, 0:1], sqt[base:base + rows, 0:TB],
                    start=(c == 0), stop=(c == n_acc - 1)), waits=w)
                sq_gates[par] = (pe, mm)
            ev = em.op("vector", dve, lambda e, tb=tb: e.tensor_copy(out_row[0:1, tb * TB:(tb + 1) * TB],
                                                                     p.banks[6][0:1, 0:TB]), waits=[(pe, mm)])
            b6_gate = (dve, ev)
            return ev

        if spec.get("decay"):
            Dt = pctx.enter_context(nc.sbuf_tensor(p.fresh(name + "Dt"), [128, 128], F32))
            Dm1 = pctx.enter_context(nc.sbuf_tensor(p.fresh(name + "Dm"), [128, 128], F32))
            grow = pctx.enter_context(nc.sbuf_tensor(p.fresh(name + "g"), [1, 1], F32))
            gcol = pctx.enter_context(nc.sbuf_tensor(p.fresh(name + "gc"), [128, 1], F32))
            gch = p.chan(name + "_g0")
            gap = gamma_dram.ap()
            if gap.ndim == 0:
                gap = gap.rearrange("-> 1 1")
            elif gap.ndim == 1:
                gap = gap.rearrange("x -> 1 x")
            gtok = em.dma("sync", grow[:], gap, gch)
            t1 = em.op("scalar", act, lambda e: e.activation(grow[:], grow[:], AF.Ln), waits=[(gch, gtok)])
            t2 = em.op("tensor", pe, lambda e: e.matmul(p.banks[4][0:128, 0:1], ones_m_f[0:1, :], grow[0:1, 0:1],
                                                        start=True, stop=True), waits=[(act, t1)])
            t3 = em.op("vector", dve, lambda e: e.tensor_copy(gcol[:], p.banks[4][0:128, 0:1]), waits=[(pe, t2)])
            t4 = em.op("vector", dve, lambda e: e.tensor_single_scalar(Dt[:], g["tdiff_T"][:], gcol[:, 0:1], ALU.mult))
            t5 = em.op("scalar", act, lambda e: e.activation(Dt[:], Dt[:], AF.Exp), waits=[(dve, t4)])
            t6 = em.op("vector", dve, lambda e: e.tensor_mul(Dt[:], Dt[:], g["tlow_T"][:]), waits=[(act, t5)])
            decay_tok = em.op("vector", dve, lambda e: e.tensor_scalar(Dm1[:], Dt[:], -1.0, 1.0, ALU.mult, ALU.add))

        kranges = sorted({spec["k_loc"](h) for h in range(nh)})
        krange_idx = {kr: i for i, kr in enumerate(kranges)}

        for s in range(SPC):
            ik, k_sb = kp.begin(em)
            kp.dma(em, ik, k_sb.ap(), k_dram.ap()[:, :, s * SEQ:(s + 1) * SEQ].rearrange("c p t -> p c t"))
            ktok = (kp.chan_of(ik), kp.tok(ik))
            iv, v_sb = vp.begin(em)
            vp.dma(em, iv, v_sb.ap(),
                   v_dram.ap()[s * SEQ:(s + 1) * SEQ, :].rearrange("(a p) d -> p a d", p=128))
            vtok = (vp.chan_of(iv), vp.tok(iv))
            last_av = None
            mk_toks = {}
            for gi, (kc_ch, kc_off, krows) in enumerate(kranges):
                for tb in range(NTB_S):
                    stat_row(lambda c, kc_ch=kc_ch, kc_off=kc_off, krows=krows, tb=tb: (
                        k_sb[kc_off:kc_off + krows, kc_ch, tb * TB:(tb + 1) * TB], krows, kc_off,
                        [ktok]), 1, kn2, tb)
                s2 = em.op("scalar", act, lambda e: e.activation(kn2[:], kn2[:], AF.Ln), waits=[b6_gate])
                s3 = em.op("scalar", act, lambda e: e.activation(kn2[:], kn2[:], AF.Exp, scale=0.5))
                tok = em.op("vector", dve, lambda e, gi=gi: e.reduce_max(
                    mk[0:1, gi:gi + 1], kn2[0:1, :], axis=AX.X), waits=[(act, s3)])
                mk_toks[gi] = tok
            for h in range(nh):
                q_ch, q_off = spec["q_loc"](h)
                kc_ch, kc_off, krows = spec["k_loc"](h)
                gi = krange_idx[spec["k_loc"](h)]
                vcol = spec["v_col"](h)
                o_ch, o_off = spec["out_loc"](h)
                if not full_q:
                    iq, qt_h = qp.begin(em)
                    qp.dma(em, iq, qt_h[kc_off:kc_off + dk, :],
                           q_dram.ap()[q_ch, q_off:q_off + dk, s * SEQ:(s + 1) * SEQ])
                    q_tok = (qp.chan_of(iq), qp.tok(iq))
                    for tb in range(NTB_S):
                        stat_row(lambda c, tb=tb: (qt_h[kc_off:kc_off + dk, tb * TB:(tb + 1) * TB],
                                                   dk, kc_off, [q_tok]), 1, qn2, tb)
                else:
                    for tb in range(NTB_S):
                        iqb, qt_b = qp.begin(em)
                        qp.dma(em, iqb, qt_b.ap(),
                               q_dram.ap()[:, :, s * SEQ + tb * TB:s * SEQ + (tb + 1) * TB].rearrange("c p t -> p c t"))
                        ev_q = stat_row(lambda c, qt_b=qt_b, iqb=iqb: (qt_b[:, c, :], 128, 0,
                                                                      [(qp.chan_of(iqb), qp.tok(iqb))]),
                                        qch_n, qn2, tb)
                        qp.set_release(iqb, dve, ev_q)
                s2 = em.op("scalar", act, lambda e: e.activation(qn2[:], qn2[:], AF.Ln), waits=[b6_gate])
                s3 = em.op("scalar", act, lambda e: e.activation(qn2[:], qn2[:], AF.Exp, scale=0.5))
                nb_last = em.op("vector", dve, lambda e, gi=gi: e.tensor_scalar(
                    negb[0:1, :], qn2[0:1, :],
                    mk[0:1, gi:gi + 1], -1.0, ALU.mult, ALU.mult),
                    waits=[(act, s3), (dve, mk_toks[gi])])
                last_qk = None
                for qbl in range(NQB):
                    qb = s * NQB + qbl
                    blocks = spec["blocks"](qbl)
                    ncs = len(blocks)
                    if full_q:
                        iqb, qt_b = qp.begin(em)
                        qp.dma(em, iqb, qt_b.ap(),
                               q_dram.ap()[:, :, qb * TB:(qb + 1) * TB].rearrange("c p t -> p c t"))
                        q_tok = (qp.chan_of(iqb), qp.tok(iqb))
                        qt_cur = qt_b
                    if spec.get("decay"):
                        w = [(dve, nb_last)] + ([] if nbb_gate is None else [nbb_gate])
                        tb_ = em.op("tensor", pe, lambda e, qbl=qbl: e.matmul(
                            p.banks[4][:, 0:TB], g["ones_m_b"][0:1, :], negb[0:1, qbl * TB:(qbl + 1) * TB],
                            start=True, stop=True), waits=w)
                        nbb_tok = em.op("scalar", act, lambda e: e.copy(nb_b[:], p.banks[4][:]), waits=[(pe, tb_)])
                        nbb_gate = (act, nbb_tok)
                    m_toks = {}
                    for j, (kc, masks) in enumerate(blocks):
                        bk = j % 2
                        bank = p.banks[bk]
                        w0 = [ktok, q_tok]
                        if bank_gate[bk] is not None:
                            w0.append(bank_gate[bk])
                        if full_q:
                            for c in range(qch_n):
                                mm = em.op("tensor", pe, lambda e, c=c, kc=kc, bank=bank, qt_cur=qt_cur: e.matmul(
                                    bank[:], k_sb[:, c, kc * 128:(kc + 1) * 128], qt_cur[:, c, :],
                                    start=(c == 0), stop=False),
                                    waits=(w0 if c == 0 else []))
                        else:
                            qsl = qt_h[kc_off:kc_off + dk, qbl * TB:(qbl + 1) * TB]
                            ksl = k_sb[kc_off:kc_off + dk, kc_ch, kc * 128:(kc + 1) * 128]
                            mm = em.op("tensor", pe, lambda e, bank=bank, ksl=ksl, qsl=qsl: e.matmul(
                                bank[:], ksl, qsl, start=True, stop=False), waits=w0)
                        mm = em.op("tensor", pe, lambda e, qbl=qbl, bank=bank: e.matmul(
                            bank[:], g["ones_m_b"][0:1, :], negb[0:1, qbl * TB:(qbl + 1) * TB],
                            start=False, stop=True), waits=[(dve, nb_last)])
                        last_qk = mm
                        e_w = [(pe, mm)]
                        if j in P_gate:
                            e_w.append(P_gate[j])
                        ex = em.op("scalar", act, lambda e, j=j, bank=bank: e.activation(
                            P_sb[:, j, :], bank[:], AF.Exp, scale=scale), waits=e_w)
                        m_tok = (act, ex)
                        bank_gate[bk] = (act, ex)
                        for (ql, mname) in masks:
                            if mname == "decay":
                                d1 = em.op("vector", dve, lambda e, ql=ql, bank=bank: e.tensor_mul(
                                    dtmp[:], bank[:, ql * 128:(ql + 1) * 128], Dt[:]),
                                    waits=[(pe, mm), (dve, decay_tok), (act, ex)])
                                c1 = em.op("vector", dve, lambda e, ql=ql: e.tensor_mul(
                                    corr[:], Dm1[:], nb_b[:, ql * 128:(ql + 1) * 128]),
                                    waits=[(act, nbb_tok)])
                                c2 = em.op("vector", dve, lambda e: e.tensor_add(dtmp[:], dtmp[:], corr[:]))
                                mx = em.op("scalar", act, lambda e, j=j, ql=ql: e.activation(
                                    P_sb[:, j, ql * 128:(ql + 1) * 128], dtmp[:], AF.Exp, scale=scale),
                                    waits=[(dve, c2)])
                                m_tok = (act, mx)
                                bank_gate[bk] = (dve, d1)
                            else:
                                mx = em.op("vector", dve, lambda e, j=j, ql=ql, mname=mname: e.tensor_mul(
                                    P_sb[:, j, ql * 128:(ql + 1) * 128], P_sb[:, j, ql * 128:(ql + 1) * 128],
                                    g[mname][:]), waits=[(act, ex)])
                                m_tok = (dve, mx)
                        m_toks[j] = m_tok
                    cs_tok = None
                    for j in range(ncs):
                        w = [m_toks[j]]
                        if j == 0 and b6_gate is not None:
                            w.append(b6_gate)
                        cs_tok = em.op("tensor", pe, lambda e, j=j, sp=(j == ncs - 1): e.matmul(
                            p.banks[6][0:1, 0:TB], ones_col_b[:, 0:1], P_sb[:, j, :],
                            start=(j == 0), stop=sp), waits=w)
                    if full_q:
                        qp.set_release(iqb, pe, last_qk)
                    r1 = em.op("vector", dve, lambda e: e.tensor_copy(rs_row[:], p.banks[6][0:1, 0:TB]),
                               waits=[(pe, cs_tok)])
                    b6_gate = (dve, r1)
                    r2 = em.op("vector", dve, lambda e: e.reciprocal(rs_row[:], rs_row[:]))
                    w = [(dve, r2)] + ([] if rb_gate is None else [rb_gate])
                    if spec.get("decay"):
                        w.append((act, nbb_tok))
                    r3 = em.op("tensor", pe, lambda e: e.matmul(p.banks[4][:, 0:TB], ones_m_f[0:1, :], rs_row[0:1, :],
                                                                start=True, stop=True), waits=w)
                    r4 = em.op("scalar", act, lambda e: e.copy(rb[:], p.banks[4][:]), waits=[(pe, r3)])
                    ndc = (dv + 127) // 128
                    for dc in range(ndc):
                        c0 = vcol + dc * 128
                        m = min(128, dv - dc * 128)
                        bk = dc % 2
                        bank = p.banks[2 + bk]
                        w0 = [vtok]
                        if av_gate[bk] is not None:
                            w0.append(av_gate[bk])
                        for j, (kc, masks) in enumerate(blocks):
                            mm = em.op("tensor", pe, lambda e, j=j, kc=kc, c0=c0, m=m, bank=bank, sp=(j == ncs - 1): e.matmul(
                                bank[0:m, 0:TB], v_sb[:, kc, c0:c0 + m], P_sb[:, j, :],
                                start=(j == 0), stop=sp),
                                waits=(w0 if j == 0 else []))
                            if dc == ndc - 1:
                                P_gate[j] = (pe, mm)
                        last_av = mm
                        j2, stg, gate = sp.begin()
                        e1 = em.op("vector", dve, lambda e, m=m, bank=bank, stg=stg: e.tensor_mul(
                            stg[0:m, :], bank[0:m, 0:TB], rb[0:m, :]),
                            waits=[(pe, mm), (act, r4)] + gate)
                        av_gate[bk] = (dve, e1)
                        rb_gate = (dve, e1)
                        if dv <= 128:
                            dst = out_dram.ap()[o_ch, o_off:o_off + m, qb * TB:(qb + 1) * TB]
                        else:
                            dst = out_dram.ap()[dc, :, qb * TB:(qb + 1) * TB]
                        sp.store(em, j2, dst, stg[0:m, :], waits=[(dve, e1)])
                if not full_q:
                    qp.set_release(iq, pe, last_qk)
            kp.set_release(ik, pe, last_qk)
            vp.set_release(iv, pe, last_av)
        quiesce_dma(p, em)
        em.replay()



def blocks_causal(qbl):
    out = []
    hi = 4 * qbl + 3
    for kc in range(hi + 1):
        masks = []
        for ql in range(TBC):
            qc = 4 * qbl + ql
            if kc == qc:
                masks.append((ql, "bin_causal"))
            elif kc > qc:
                masks.append((ql, "bin_zero"))
        out.append((kc, masks))
    return out


def blocks_full_decay(qbl):
    out = []
    for kc in range(SEQ // 128):
        masks = []
        for ql in range(TBC):
            qc = 4 * qbl + ql
            if kc == qc:
                masks.append((ql, "decay"))
        out.append((kc, masks))
    return out


def blocks_window(qbl):
    qcs = [4 * qbl + i for i in range(TBC)]
    kcs = sorted(set([0] + [kc for qc in qcs for kc in (qc - 1, qc, qc + 1)
                            if 0 <= kc < SEQ // 128]))
    if qbl == 0:
        kcs = list(range(SEQ // 128))
    out = []
    for kc in kcs:
        masks = []
        for ql in range(TBC):
            qc = 4 * qbl + ql
            d = kc - qc
            if d == 0:
                continue
            elif d == 1:
                masks.append((ql, "bin_causal_col0" if qc == 0 else "bin_causal"))
            elif d == -1:
                if kc == 0:
                    masks.append((ql, "bin_tri_hi_row0"))
                else:
                    masks.append((ql, "bin_tri_hi"))
            else:
                if kc == 0:
                    masks.append((ql, "bin_row0"))
                elif qc == 0:
                    masks.append((ql, "bin_col0"))
                else:
                    masks.append((ql, "bin_zero"))
        out.append((kc, masks))
    return out


def pass_rope(p, em, g, name, r_dram, xr_dram, m_row, a_row, w1_dram, hc, row_waits=()):
    nc = p.nc
    pe, dve, act = p.chan("pe"), p.chan("vector"), p.chan("scalar")
    ones_m_f = g["ones_m_f"]
    HFCH = 4
    with ExitStack() as pctx:
        cos_sb = pctx.enter_context(nc.sbuf_tensor(p.fresh(name + "cos"), [128, HFCH, SEQ], F32))
        sin_sb = pctx.enter_context(nc.sbuf_tensor(p.fresh(name + "sin"), [128, HFCH, SEQ], F32))
        tch, tch2, wch = p.chan(name + "_t"), p.chan(name + "_t2"), p.chan(name + "_w")
        ctok = em.dma("sync", cos_sb.ap(), hc["rope_cos"].ap().rearrange("(c p) t -> p c t", p=128), tch)
        stok = em.dma("sync", sin_sb.ap(), hc["rope_sin"].ap().rearrange("(c p) t -> p c t", p=128), tch2)
        w1row = pctx.enter_context(nc.sbuf_tensor(p.fresh(name + "w1r"), [1, D], F32))
        wsel = pctx.enter_context(nc.sbuf_tensor(p.fresh(name + "wsel"), [1, D], F32))
        wcols = pctx.enter_context(nc.sbuf_tensor(p.fresh(name + "wc"), [128, 2 * HFCH], F32))
        wtok = em.dma("sync", w1row[:], w1_dram.ap().rearrange("(a d) -> a d", a=1), wch)
        c1 = em.op("vector", dve, lambda e: e.tensor_copy(wsel[0:1, 0:512], w1row[0:1, 0:D:2]), waits=[(wch, wtok)])
        c2 = em.op("vector", dve, lambda e: e.tensor_copy(wsel[0:1, 512:1024], w1row[0:1, 1:D:2]))
        wc_tok = row_to_cols(p, em, g, wsel, D, wcols, row_waits=[(dve, c2)])
        lp = Pool(p, pctx, name + "_l", 2, [128, TB], F32, queue="sync")
        lp2 = Pool(p, pctx, name + "_l2", 2, [128, TB], F32, queue="sync")
        sp = StorePool(p, pctx, name + "_o", 2, [128, 2, TB], BF16, queue="scalar")
        mb = pctx.enter_context(nc.sbuf_tensor(p.fresh(name + "mb"), [128, TB], F32))
        ab = pctx.enter_context(nc.sbuf_tensor(p.fresh(name + "ab"), [128, TB], F32))
        xe = pctx.enter_context(nc.sbuf_tensor(p.fresh(name + "xe"), [128, TB], F32))
        xo = pctx.enter_context(nc.sbuf_tensor(p.fresh(name + "xo"), [128, TB], F32))
        tt = pctx.enter_context(nc.sbuf_tensor(p.fresh(name + "tt"), [128, TB], F32))
        tt2 = pctx.enter_context(nc.sbuf_tensor(p.fresh(name + "t2"), [128, TB], F32))
        for tb in range(NTB):
            s = (tb * TB) // SEQ
            pos0 = tb * TB - s * SEQ
            w = list(row_waits) + [(dve, wc_tok)] if tb == 0 else []
            if tb > 0:
                w.append((dve, em.last["vector"][1]))
            t_m = em.op("tensor", pe, lambda e, tb=tb: e.matmul(
                p.banks[4][:, 0:TB], ones_m_f[0:1, :], m_row[0:1, tb * TB:(tb + 1) * TB],
                start=True, stop=True), waits=w)
            mtok = em.op("scalar", act, lambda e: e.copy(mb[:], p.banks[4][:]),
                         waits=[(pe, t_m)] + w)
            t_a = em.op("tensor", pe, lambda e, tb=tb: e.matmul(
                p.banks[5][:, 0:TB], ones_m_f[0:1, :], a_row[0:1, tb * TB:(tb + 1) * TB],
                start=True, stop=True), waits=(w if tb == 0 else []))
            atok = em.op("scalar", act, lambda e: e.copy(ab[:], p.banks[5][:]), waits=[(pe, t_a)])
            for c2_ in range(HFCH):
                ie, et = lp.begin(em)
                lp.dma(em, ie, et[:],
                       r_dram.ap()[2 * c2_:2 * c2_ + 2, 0:128:2, tb * TB:(tb + 1) * TB].rearrange("a p n -> (a p) n"))
                io, ot = lp2.begin(em)
                lp2.dma(em, io, ot[:],
                        r_dram.ap()[2 * c2_:2 * c2_ + 2, 1:128:2, tb * TB:(tb + 1) * TB].rearrange("a p n -> (a p) n"))
                e1 = em.op("vector", dve, lambda e, et=et: e.tensor_sub(xe[:], et[:], mb[:]),
                           waits=[(lp.chan_of(ie), lp.tok(ie)), (act, atok), (dve, wc_tok)])
                lp.set_release(ie, dve, e1)
                em.op("vector", dve, lambda e: e.tensor_mul(xe[:], xe[:], ab[:]))
                em.op("vector", dve, lambda e, c2_=c2_: e.tensor_single_scalar(xe[:], xe[:], wcols[:, c2_:c2_ + 1], ALU.mult))
                o1 = em.op("vector", dve, lambda e, ot=ot: e.tensor_sub(xo[:], ot[:], mb[:]),
                           waits=[(lp2.chan_of(io), lp2.tok(io))])
                lp2.set_release(io, dve, o1)
                em.op("vector", dve, lambda e: e.tensor_mul(xo[:], xo[:], ab[:]))
                em.op("vector", dve, lambda e, c2_=c2_: e.tensor_single_scalar(xo[:], xo[:], wcols[:, 4 + c2_:5 + c2_], ALU.mult))
                j, stg, gate = sp.begin()
                cw = [(tch, ctok), (tch2, stok)] + gate
                cs = cos_sb[:, c2_, pos0:pos0 + TB]
                sn = sin_sb[:, c2_, pos0:pos0 + TB]
                em.op("vector", dve, lambda e, cs=cs: e.tensor_mul(tt[:], xe[:], cs), waits=cw)
                em.op("vector", dve, lambda e, sn=sn: e.tensor_mul(tt2[:], xo[:], sn))
                l1 = em.op("vector", dve, lambda e, stg=stg: e.tensor_sub(stg[:, 0, :], tt[:], tt2[:]))
                em.op("vector", dve, lambda e, sn=sn: e.tensor_mul(tt[:], xe[:], sn))
                em.op("vector", dve, lambda e, cs=cs: e.tensor_mul(tt2[:], xo[:], cs))
                l2 = em.op("vector", dve, lambda e, stg=stg: e.tensor_add(stg[:, 1, :], tt[:], tt2[:]))
                sp.store(em, j, xr_dram.ap()[c2_, :, tb * TB:(tb + 1) * TB], stg[:, 0, :], waits=[(dve, l2)])
                sp.store(em, j, xr_dram.ap()[4 + c2_, :, tb * TB:(tb + 1) * TB], stg[:, 1, :])
        quiesce_dma(p, em)
        em.replay()



def pass_router(p, em, g, name, xn_dram, rw_dram, rb_dram, comb_dram):
    nc = p.nc
    pe, dve, act = p.chan("pe"), p.chan("vector"), p.chan("scalar")
    ones_row_b = g["ones_row_b"]
    ident = g["ident_f"]
    with ExitStack() as pctx:
        rw_sb = pctx.enter_context(nc.sbuf_tensor(p.fresh(name + "rw"), [128, DCH, 8], BF16))
        rb_sb = pctx.enter_context(nc.sbuf_tensor(p.fresh(name + "rb"), [1, 8], BF16))
        wch = p.chan(name + "_w")
        wtok = em.dma("gpsimd", rw_sb.ap(), rw_dram.ap().rearrange("(i p) o -> p i o", p=128), wch)
        btok = em.dma("gpsimd", rb_sb.ap(), rb_dram.ap().rearrange("(a o) -> a o", a=1), wch)
        xp = Pool(p, pctx, name + "_x", 2, [128, DCH, TB], BF16, queue="sync")
        sp = StorePool(p, pctx, name + "_o", 2, [8, TB], F32, queue="scalar")
        lg = pctx.enter_context(nc.sbuf_tensor(p.fresh(name + "lg"), [8, TB], F32))
        lgt = pctx.enter_context(nc.sbuf_tensor(p.fresh(name + "lt"), [128, TBC, 8], F32))
        v1 = pctx.enter_context(nc.sbuf_tensor(p.fresh(name + "v1"), [128, 1], F32))
        v2 = pctx.enter_context(nc.sbuf_tensor(p.fresh(name + "v2"), [128, 1], F32))
        m1 = pctx.enter_context(nc.sbuf_tensor(p.fresh(name + "m1"), [128, 8], F32))
        m2 = pctx.enter_context(nc.sbuf_tensor(p.fresh(name + "m2"), [128, 8], F32))
        l2t = pctx.enter_context(nc.sbuf_tensor(p.fresh(name + "l2"), [128, 8], F32))
        sg = pctx.enter_context(nc.sbuf_tensor(p.fresh(name + "sg"), [128, 1], F32))
        ta = pctx.enter_context(nc.sbuf_tensor(p.fresh(name + "ta"), [128, 1], F32))
        cmb = pctx.enter_context(nc.sbuf_tensor(p.fresh(name + "cm"), [128, 8], F32))
        lg_gate = None
        lg_rd_gate = None
        b2_gate = None
        b3_gate = None
        for tb in range(NTB):
            ix, xt = xp.begin(em)
            xp.dma(em, ix, xt.ap(), xn_dram.ap()[:, :, tb * TB:(tb + 1) * TB].rearrange("c p n -> p c n"))
            w0 = [(wch, wtok), (wch, btok), (xp.chan_of(ix), xp.tok(ix))]
            if lg_gate is not None:
                w0.append(lg_gate)
            mm = None
            for i in range(DCH):
                mm = em.op("tensor", pe, lambda e, i=i, xt=xt: e.matmul(
                    p.banks[0][0:8, 0:TB], rw_sb[:, i, :], xt[:, i, :],
                    start=(i == 0), stop=False),
                    waits=(w0 if i == 0 else []))
            mm = em.op("tensor", pe, lambda e: e.matmul(
                p.banks[0][0:8, 0:TB], rb_sb[0:1, :], ones_row_b[0:1, :],
                start=False, stop=True))
            xp.set_release(ix, pe, mm)
            wlg = [(pe, mm)]
            if lg_rd_gate is not None:
                wlg.append(lg_rd_gate)
            lg_tok = em.op("scalar", act, lambda e: e.copy(lg[:], p.banks[0][0:8, 0:TB]), waits=wlg)
            lg_gate = (act, lg_tok)
            tts = []
            for ql in range(TBC):
                w = [(act, lg_tok)]
                if b2_gate is not None and ql == 0:
                    w.append(b2_gate)
                tt_ = em.op("tensor", pe, lambda e, ql=ql: e.transpose(
                    p.banks[2][0:128, ql * 8:(ql + 1) * 8], lg[0:8, ql * 128:(ql + 1) * 128],
                    ident[0:8, 0:8]), waits=w)
                tts.append(tt_)
                lg_rd_gate = (pe, tt_)
            ev = em.op("vector", dve, lambda e: e.tensor_copy(
                lgt.ap().rearrange("p c o -> p (c o)"), p.banks[2][0:128, 0:TBC * 8]),
                waits=[(pe, tts[-1])])
            b2_gate = (dve, ev)
            for ql in range(TBC):
                src = lgt[:, ql, :]
                em.op("vector", dve, lambda e, src=src: e.reduce_max(v1[:], src, axis=AX.X))
                em.op("vector", dve, lambda e, src=src: e.tensor_single_scalar(m1[:], src, v1[:, 0:1], ALU.is_ge))
                em.op("vector", dve, lambda e, src=src: e.scalar_tensor_tensor(l2t[:], m1[:], -1e30, src, ALU.mult, ALU.add))
                em.op("vector", dve, lambda e: e.reduce_max(v2[:], l2t[:], axis=AX.X))
                sv = em.op("vector", dve, lambda e: e.tensor_sub(sg[:], v1[:], v2[:]))
                sgt = em.op("scalar", act, lambda e: e.activation(sg[:], sg[:], AF.Sigmoid), waits=[(dve, sv)])
                em.op("vector", dve, lambda e: e.tensor_single_scalar(m2[:], l2t[:], v2[:, 0:1], ALU.is_ge),
                      waits=[(act, sgt)])
                em.op("vector", dve, lambda e: e.tensor_scalar(ta[:], sg[:], -1.0, 1.0, ALU.mult, ALU.add))
                em.op("vector", dve, lambda e: e.tensor_single_scalar(m2[:], m2[:], ta[:, 0:1], ALU.mult))
                em.op("vector", dve, lambda e: e.tensor_single_scalar(m1[:], m1[:], sg[:, 0:1], ALU.mult))
                cm_tok = em.op("vector", dve, lambda e: e.tensor_add(cmb[:], m2[:], m1[:]))
                w = [(dve, cm_tok)]
                if b3_gate is not None:
                    w.append(b3_gate)
                tt_ = em.op("tensor", pe, lambda e: e.transpose(
                    p.banks[3][0:8, 0:128], cmb[:], ident[:]), waits=w)
                if ql == 0:
                    j, stg, gate = sp.begin()
                g2 = (gate if ql == 0 else [])
                evc = em.op("vector", dve, lambda e, ql=ql, stg=stg: e.tensor_copy(
                    stg[0:8, ql * 128:(ql + 1) * 128], p.banks[3][0:8, 0:128]),
                    waits=[(pe, tt_)] + g2)
                b3_gate = (dve, evc)
            sp.store(em, j, comb_dram.ap()[:, tb * TB:(tb + 1) * TB], stg.ap(), waits=[(dve, evc)])
        quiesce_dma(p, em)
        em.replay()



def pass_cast_dram(p, em, name, src_dram, dst_dram, n_moves, src_slice, dst_slice,
                   tile_shape=(128, 1024)):
    nc = p.nc
    with ExitStack() as pctx:
        bp = Pool(p, pctx, name + "_b", 2, list(tile_shape), BF16, queue="gpsimd")
        st = [p.chan(name + f"_s{i}") for i in range(2)]
        stok = {}
        for i in range(n_moves):
            ii, bt = bp.begin(em)
            slot = bp.slot(ii)
            if ii >= 2 and (ii - 2) in stok:
                pass
            bp.dma(em, ii, bt.ap(), src_slice(i))
            tok = em.dma("gpsimd", dst_slice(i), bt.ap(), st[slot],
                         waits=[(bp.chan_of(ii), bp.tok(ii))])
            stok[ii] = tok
            bp.set_release(ii, st[slot], tok)
        quiesce_dma(p, em)
        em.replay()



def pass_moe_ph1(p, em, g, name, xn_dram, w1_dram, b1c, w2_dram, b2c,
                 comb_dram, h_dram):
    nc = p.nc
    pe, dve, act = p.chan("pe"), p.chan("vector"), p.chan("scalar")
    ones_m_b = g["ones_m_b"]
    FCG = 4
    with ExitStack() as pctx:
        xn_sb = pctx.enter_context(nc.sbuf_tensor(p.fresh(name + "xn"), [128, DCH, T], BF16))
        xch = p.chan(name + "_x")
        xtok = em.dma("sync", xn_sb.ap(), xn_dram.ap().rearrange("c p t -> p c t"), xch)
        w1p = Pool(p, pctx, name + "_w1", 2, [128, DCH, FCG * 128], BF16, queue="gpsimd")
        w2p = Pool(p, pctx, name + "_w2", 2, [128, DCH, FCG * 128], BF16, queue="gpsimd")
        crow = [pctx.enter_context(nc.sbuf_tensor(p.fresh(name + "cr"), [1, T], BF16)) for _ in range(2)]
        cb_sb = pctx.enter_context(nc.sbuf_tensor(p.fresh(name + "cb"), [128, 2, NTB, TB], BF16))
        sl = pctx.enter_context(nc.sbuf_tensor(p.fresh(name + "sl"), [128, TB], F32))
        sp = StorePool(p, pctx, name + "_o", 3, [128, NTB, TB], BF16, queue="scalar")
        cch = [p.chan(name + f"_c{i}") for i in range(2)]
        h1_gate = {0: None, 1: None}
        h2_gate = {0: None, 1: None}
        cb_rd_gate = {0: None, 1: None}
        cb_mm_gate = {0: None, 1: None}
        sl_gate = None
        cbt_done = {}

        def emit_cb(ex):
            par = ex % 2
            wexp = [] if cb_mm_gate[par] is None else [cb_mm_gate[par]]
            ctok = em.dma("gpsimd", crow[par][:], comb_dram.ap()[ex:ex + 1, :], cch[par], waits=wexp)
            cbt_prev = None
            t_ = None
            for tb in range(NTB):
                w = [(cch[par], ctok)]
                if cb_rd_gate[par] is not None and tb == 0:
                    w.append(cb_rd_gate[par])
                if cbt_prev is not None:
                    w.append((act, cbt_prev))
                t_ = em.op("tensor", pe, lambda e, tb=tb, par=par: e.matmul(
                    p.banks[4][:, 0:TB], ones_m_b[0:1, :], crow[par][0:1, tb * TB:(tb + 1) * TB],
                    start=True, stop=True), waits=w)
                cbt = em.op("scalar", act, lambda e, tb=tb, par=par: e.copy(cb_sb[:, par, tb, :], p.banks[4][:]),
                            waits=[(pe, t_)])
                cbt_prev = cbt
            cb_mm_gate[par] = (pe, t_)
            cbt_done[ex] = (act, cbt_prev)

        emit_cb(0)
        for ex in range(NEXP):
            par = ex % 2
            cbt_prev = cbt_done[ex][1]
            for fcg in range(FCH // FCG):
                if fcg == 1 and ex + 1 < NEXP:
                    emit_cb(ex + 1)
                    cbt_prev = cbt_done[ex][1]
                i1, w1t = w1p.begin(em)
                w1p.dma(em, i1, w1t.ap(),
                        w1_dram.ap()[ex].rearrange("(i p) f -> p i f", p=128)[:, :, fcg * FCG * 128:(fcg + 1) * FCG * 128])
                i2, w2t = w2p.begin(em)
                w2p.dma(em, i2, w2t.ap(),
                        w2_dram.ap()[ex].rearrange("(i p) f -> p i f", p=128)[:, :, fcg * FCG * 128:(fcg + 1) * FCG * 128])
                for f4 in range(FCG):
                    fc = fcg * FCG + f4
                    j, stg, gate = sp.begin()
                    for tb in range(NTB):
                        bk = tb % 2
                        b_h1, b_h2 = p.banks[bk], p.banks[2 + bk]
                        w0 = [(xch, xtok), (w1p.chan_of(i1), w1p.tok(i1)), (w2p.chan_of(i2), w2p.tok(i2))]
                        if h1_gate[bk] is not None:
                            w0.append(h1_gate[bk])
                        if h2_gate[bk] is not None:
                            w0.append(h2_gate[bk])
                        mm1 = None
                        for i in range(DCH):
                            mm1 = em.op("tensor", pe, lambda e, i=i, tb=tb, f4=f4, w1t=w1t, b_h1=b_h1: e.matmul(
                                b_h1[:], w1t[:, i, f4 * 128:(f4 + 1) * 128], xn_sb[:, i, tb * TB:(tb + 1) * TB],
                                start=(i == 0), stop=(i == DCH - 1)),
                                waits=(w0 if i == 0 else []))
                        mm2 = None
                        for i in range(DCH):
                            mm2 = em.op("tensor", pe, lambda e, i=i, tb=tb, f4=f4, w2t=w2t, b_h2=b_h2: e.matmul(
                                b_h2[:], w2t[:, i, f4 * 128:(f4 + 1) * 128], xn_sb[:, i, tb * TB:(tb + 1) * TB],
                                start=(i == 0), stop=(i == DCH - 1)))
                        if tb == NTB - 1 and f4 == FCG - 1:
                            w1p.set_release(i1, pe, mm1)
                            w2p.set_release(i2, pe, mm2)
                        wsl = [(pe, mm1)]
                        if sl_gate is not None:
                            wsl.append(sl_gate)
                        s1 = em.op("scalar", act, lambda e, b_h1=b_h1, ex=ex, fc=fc: e.activation(
                            sl[:], b_h1[:], AF.Silu, bias=b1c[:, ex, fc:fc + 1]), waits=wsl)
                        h1_gate[bk] = (act, s1)
                        d1 = em.op("vector", dve, lambda e, b_h2=b_h2, ex=ex, fc=fc, tb=tb, stg=stg: e.scalar_tensor_tensor(
                            stg[:, tb, :], b_h2[:], b2c[:, ex, fc:fc + 1], sl[:], ALU.add, ALU.mult),
                            waits=[(pe, mm2), (act, s1)] + (gate if tb == 0 else []))
                        sl_gate = (dve, d1)
                        h2_gate[bk] = (dve, d1)
                        d2 = em.op("vector", dve, lambda e, tb=tb, stg=stg, par=par: e.tensor_mul(
                            stg[:, tb, :], stg[:, tb, :], cb_sb[:, par, tb, :]), waits=[cbt_done[ex]])
                        if tb == NTB - 1 and fc == FCH - 1:
                            cb_rd_gate[par] = (dve, d2)
                        last_d2 = d2
                    sp.store(em, j, h_dram.ap()[ex, fc, :, :].rearrange("p (a n) -> p a n", n=TB),
                             stg.ap(), waits=[(dve, last_d2)])
        quiesce_dma(p, em)
        em.replay()


def prep_bias_cols(p, em, g, name, b1_dram, b2_dram, pctx):
    nc = p.nc
    dve = p.chan("vector")
    ch = p.chan(name + "_b")
    row = pctx.enter_context(nc.sbuf_tensor(p.fresh(name + "r"), [1, F_FF], F32))
    b1c = pctx.enter_context(nc.sbuf_tensor(p.fresh(name + "1"), [128, NEXP, FCH], F32))
    b2c = pctx.enter_context(nc.sbuf_tensor(p.fresh(name + "2"), [128, NEXP, FCH], F32))
    tok = None
    for (dram, cols) in ((b1_dram, b1c), (b2_dram, b2c)):
        for ex in range(NEXP):
            t = em.dma("sync", row[:], dram.ap()[ex:ex + 1, :], ch)
            tok = row_to_cols(p, em, g, row, F_FF, cols.ap()[:, ex, :], row_waits=[(ch, t)])
            em.wait("sync", dve, tok)
    return b1c, b2c, tok



def pass_moe_ph2(p, em, g, name, h_dram, w3b_dram, b3_dram, comb_dram,
                 r_dram, h1h_dram, rout_dram):
    nc = p.nc
    pe, dve, act = p.chan("pe"), p.chan("vector"), p.chan("scalar")
    FCG = 4
    with ExitStack() as pctx:
        b3_sb = pctx.enter_context(nc.sbuf_tensor(p.fresh(name + "b3"), [8, D], BF16))
        cm_sb = pctx.enter_context(nc.sbuf_tensor(p.fresh(name + "cm"), [8, T], BF16))
        bch = p.chan(name + "_b")
        btok = em.dma("gpsimd", b3_sb.ap(), b3_dram.ap(), bch)
        ctok = em.dma("gpsimd", cm_sb.ap(), comb_dram.ap(), bch)
        hp = Pool(p, pctx, name + "_h", 3, [128, FCG, TB], BF16, queue="sync")
        wp = Pool(p, pctx, name + "_w", 3, [128, FCG, D], BF16, queue="sync")
        ap_ = Pool(p, pctx, name + "_a", 2, [128, DCH, TB], F32, queue="sync")
        rp = Pool(p, pctx, name + "_r", 2, [128, DCH, TB], F32, queue="sync")
        ev = pctx.enter_context(nc.sbuf_tensor(p.fresh(name + "ev"), [128, TB], F32))
        tt = pctx.enter_context(nc.sbuf_tensor(p.fresh(name + "tt"), [128, TB], F32))
        sp = StorePool(p, pctx, name + "_o", 2, [128, DCH, TB], F32, queue="scalar")
        bank_gate = [None] * 8
        ev_gate = None
        for tb in range(NTB):
            ia, at = ap_.begin(em)
            ap_.dma(em, ia, at.ap(), h1h_dram.ap()[:, :, tb * TB:(tb + 1) * TB].rearrange("c p n -> p c n"))
            ir, rt = rp.begin(em)
            rp.dma(em, ir, rt.ap(), r_dram.ap()[:, :, tb * TB:(tb + 1) * TB].rearrange("c p n -> p c n"))
            first = True
            mm_last = {}
            for ex in range(NEXP):
                for fcg in range(FCH // FCG):
                    ih, ht = hp.begin(em)
                    hp.dma(em, ih, ht.ap(),
                           h_dram.ap()[ex, fcg * FCG:(fcg + 1) * FCG, :, tb * TB:(tb + 1) * TB].rearrange("a p n -> p a n"))
                    iw, wt = wp.begin(em)
                    wp.dma(em, iw, wt.ap(), w3b_dram.ap()[ex, fcg * FCG:(fcg + 1) * FCG, :, :].rearrange("a p n -> p a n"))
                    w0 = [(hp.chan_of(ih), hp.tok(ih)), (wp.chan_of(iw), wp.tok(iw))]
                    mm = None
                    for f4 in range(FCG):
                        for dc in range(DCH):
                            w = list(w0) if (f4 == 0 and dc == 0) else []
                            if first and bank_gate[dc] is not None:
                                w.append(bank_gate[dc])
                            mm = em.op("tensor", pe, lambda e, dc=dc, f4=f4, wt=wt, ht=ht, first=first: e.matmul(
                                p.banks[dc][:], wt[:, f4, dc * 128:(dc + 1) * 128], ht[:, f4, :],
                                start=first, stop=False), waits=w)
                            mm_last[dc] = mm
                        first = False
                    hp.set_release(ih, pe, mm)
                    wp.set_release(iw, pe, mm)
            for dc in range(DCH):
                mm = em.op("tensor", pe, lambda e, dc=dc, tb=tb: e.matmul(
                    p.banks[dc][:], b3_sb[0:8, dc * 128:(dc + 1) * 128], cm_sb[0:8, tb * TB:(tb + 1) * TB],
                    start=False, stop=True),
                    waits=([(bch, btok), (bch, ctok)] if tb == 0 and dc == 0 else []))
                mm_last[dc] = mm
            j, stg, gate = sp.begin()
            last = None
            for dc in range(DCH):
                w1 = [(pe, mm_last[dc])]
                if ev_gate is not None:
                    w1.append(ev_gate)
                e1 = em.op("scalar", act, lambda e, dc=dc: e.activation(ev[:], p.banks[dc][:], AF.Copy, scale=0.5),
                           waits=w1)
                bank_gate[dc] = (act, e1)
                e2 = em.op("vector", dve, lambda e, dc=dc, at=at: e.tensor_add(tt[:], ev[:], at[:, dc, :]),
                           waits=[(act, e1), (ap_.chan_of(ia), ap_.tok(ia))] + (gate if dc == 0 else []))
                e3 = em.op("vector", dve, lambda e, dc=dc, rt=rt, stg=stg: e.tensor_add(
                    stg[:, dc, :], tt[:], rt[:, dc, :]),
                    waits=[(rp.chan_of(ir), rp.tok(ir))])
                ev_gate = (dve, e2)
                last = e3
            ap_.set_release(ia, dve, last)
            rp.set_release(ir, dve, last)
            sp.store(em, j, rout_dram.ap()[:, :, tb * TB:(tb + 1) * TB].rearrange("c p n -> p c n"),
                     stg.ap(), waits=[(dve, last)])
        quiesce_dma(p, em)
        em.replay()



ATT_GQA = dict(
    nh=16, dk=64, dv=64, dv_total=256, kch=2, qch=8, scale=1.0 / 8.0,
    q_loc=lambda h: (h // 2, 64 * (h % 2)),
    k_loc=lambda h: ((h // 4) // 2, 64 * ((h // 4) % 2), 64),
    v_col=lambda h: 64 * (h // 4),
    out_loc=lambda h: (h // 2, 64 * (h % 2)),
    blocks=blocks_causal,
)

ATT_RET = dict(
    nh=1, dk=D, dv=D, dv_total=D, kch=DCH, qch=DCH, scale=1.0 / 32.0,
    q_loc=lambda h: (0, 0), k_loc=lambda h: (0, 0, 128),
    v_col=lambda h: 0, out_loc=lambda h: (0, 0),
    blocks=blocks_full_decay, decay=True,
)

ATT_WIN = dict(
    nh=16, dk=64, dv=64, dv_total=D, kch=DCH, qch=DCH, scale=1.0 / 8.0,
    q_loc=lambda h: (h // 2, 64 * (h % 2)),
    k_loc=lambda h: (h // 2, 64 * (h % 2), 64),
    v_col=lambda h: 64 * h,
    out_loc=lambda h: (h // 2, 64 * (h % 2)),
    blocks=blocks_window,
)





def pass_wprep(p, em, g, name, src_dram, dst_dram, R, C, scale):
    nc = p.nc
    pe, dve = p.chan("pe"), p.chan("vector")
    ident_b = g["ident_b"]
    CCH = C // 128
    with ExitStack() as pctx:
        lp = Pool(p, pctx, name + "_l", 2, [128, C], BF16, queue="gpsimd")
        sc = pctx.enter_context(nc.sbuf_tensor(p.fresh(name + "sc"), [128, C], BF16))
        sp = StorePool(p, pctx, name + "_o", 2, [128, CCH, 128], BF16, queue="scalar")
        bank_gate = [None, None]
        for i in range(R // 128):
            ii, wt = lp.begin(em)
            lp.dma(em, ii, wt.ap(), src_dram.ap()[i * 128:(i + 1) * 128, :])
            s1 = em.op("vector", dve, lambda e, wt=wt: e.tensor_scalar_mul(sc[:], wt[:], float(scale)),
                       waits=[(lp.chan_of(ii), lp.tok(ii))])
            lp.set_release(ii, dve, s1)
            j, stg, gate = sp.begin()
            ev = None
            ngrp = (CCH + 7) // 8
            for ccg in range(ngrp):
                par = (i * ngrp + ccg) % 2
                bank_b = p.banks[par].ap().bitcast(BF16)
                ncc = min(8, CCH - ccg * 8)
                tp = None
                for k in range(ncc):
                    cc = ccg * 8 + k
                    w = [(dve, s1)]
                    if k == 0 and bank_gate[par] is not None:
                        w.append(bank_gate[par])
                    tp = em.op("tensor", pe, lambda e, cc=cc, k=k, bank_b=bank_b: e.transpose(
                        bank_b[:, k * 128:(k + 1) * 128],
                        sc[:, cc * 128:(cc + 1) * 128], ident_b[:]), waits=w)
                ev = em.op("vector", dve, lambda e, stg=stg, bank_b=bank_b, ccg=ccg, ncc=ncc: e.tensor_copy(
                    stg.ap().rearrange("p c n -> p (c n)")[:, ccg * 1024:ccg * 1024 + ncc * 128],
                    bank_b[:, 0:ncc * 128]),
                    waits=[(pe, tp)] + (gate if ccg == 0 else []))
                bank_gate[par] = (dve, ev)
            sp.store(em, j, dst_dram.ap()[:, :, i * 128:(i + 1) * 128].rearrange("c p n -> p c n"),
                     stg.ap(), waits=[(dve, ev)])
        quiesce_dma(p, em)
        em.replay()


def pass_l1ffn(p, em, g, name, xn_dram, wgT, wuT, wdT, r_dram, rout_dram):
    nc = p.nc
    pe, dve, act = p.chan("pe"), p.chan("vector"), p.chan("scalar")
    with ExitStack() as pctx:
        xp = Pool(p, pctx, name + "_x", 2, [128, DCH, TB], BF16, queue="sync")
        gp = Pool(p, pctx, name + "_g", 2, [128, DCH, 128], BF16, queue="sync")
        up = Pool(p, pctx, name + "_u", 2, [128, DCH, 128], BF16, queue="sync")
        dp = Pool(p, pctx, name + "_d", 2, [128, HCH, 128], BF16, queue="sync")
        rp = Pool(p, pctx, name + "_r", 2, [128, DCH, TB], F32, queue="sync")
        hdn = pctx.enter_context(nc.sbuf_tensor(p.fresh(name + "h"), [128, HCH, TB], BF16))
        sl = pctx.enter_context(nc.sbuf_tensor(p.fresh(name + "sl"), [128, TB], F32))
        ev = pctx.enter_context(nc.sbuf_tensor(p.fresh(name + "ev"), [128, TB], F32))
        sp = StorePool(p, pctx, name + "_o", 2, [128, DCH, TB], F32, queue="scalar")
        g_gate = {0: None, 1: None}
        u_gate = {0: None, 1: None}
        d_gate = {0: None, 1: None}
        sl_gate = None
        hdn_gate = {}
        ev_gate = None
        for tb in range(NTB):
            ix, xt = xp.begin(em)
            xp.dma(em, ix, xt.ap(), xn_dram.ap()[:, :, tb * TB:(tb + 1) * TB].rearrange("c p n -> p c n"))
            ir, rt = rp.begin(em)
            rp.dma(em, ir, rt.ap(), r_dram.ap()[:, :, tb * TB:(tb + 1) * TB].rearrange("c p n -> p c n"))
            last_up_mm = None
            for hc in range(HCH):
                ig, gt = gp.begin(em)
                gp.dma(em, ig, gt.ap(), wgT.ap()[:, :, hc * 128:(hc + 1) * 128].rearrange("c p n -> p c n"))
                iu, ut = up.begin(em)
                up.dma(em, iu, ut.ap(), wuT.ap()[:, :, hc * 128:(hc + 1) * 128].rearrange("c p n -> p c n"))
                bk = hc % 2
                b_g, b_u = p.banks[bk], p.banks[2 + bk]
                w0 = [(xp.chan_of(ix), xp.tok(ix)), (gp.chan_of(ig), gp.tok(ig))]
                if g_gate[bk] is not None:
                    w0.append(g_gate[bk])
                mmg = None
                for i in range(DCH):
                    mmg = em.op("tensor", pe, lambda e, i=i, tb=tb, gt=gt, b_g=b_g, xt=xt: e.matmul(
                        b_g[:], gt[:, i, :], xt[:, i, :],
                        start=(i == 0), stop=(i == DCH - 1)),
                        waits=(w0 if i == 0 else []))
                gp.set_release(ig, pe, mmg)
                w0b = [(up.chan_of(iu), up.tok(iu))]
                if u_gate[bk] is not None:
                    w0b.append(u_gate[bk])
                mmu = None
                for i in range(DCH):
                    mmu = em.op("tensor", pe, lambda e, i=i, tb=tb, ut=ut, b_u=b_u, xt=xt: e.matmul(
                        b_u[:], ut[:, i, :], xt[:, i, :],
                        start=(i == 0), stop=(i == DCH - 1)),
                        waits=(w0b if i == 0 else []))
                up.set_release(iu, pe, mmu)
                last_up_mm = mmu
                wsl = [(pe, mmg)]
                if sl_gate is not None:
                    wsl.append(sl_gate)
                s1 = em.op("scalar", act, lambda e, b_g=b_g: e.activation(sl[:], b_g[:], AF.Silu),
                           waits=wsl)
                g_gate[bk] = (act, s1)
                w1 = [(pe, mmu), (act, s1)]
                if hc in hdn_gate:
                    w1.append(hdn_gate[hc])
                d1 = em.op("vector", dve, lambda e, hc=hc, b_u=b_u: e.tensor_mul(hdn[:, hc, :], b_u[:], sl[:]),
                           waits=w1)
                sl_gate = (dve, d1)
                u_gate[bk] = (dve, d1)
            xp.set_release(ix, pe, last_up_mm)
            j, stg, gate = sp.begin()
            last = None
            for dc in range(DCH):
                idp, dt_ = dp.begin(em)
                dp.dma(em, idp, dt_.ap(), wdT.ap()[:, :, dc * 128:(dc + 1) * 128].rearrange("c p n -> p c n"))
                bk = 4 + dc % 2
                bank = p.banks[bk]
                w0 = [(dp.chan_of(idp), dp.tok(idp)), (dve, em.last["vector"][1])]
                if d_gate[dc % 2] is not None:
                    w0.append(d_gate[dc % 2])
                mm = None
                for hc in range(HCH):
                    mm = em.op("tensor", pe, lambda e, hc=hc, dt_=dt_, bank=bank: e.matmul(
                        bank[:], dt_[:, hc, :], hdn[:, hc, :],
                        start=(hc == 0), stop=(hc == HCH - 1)),
                        waits=(w0 if hc == 0 else []))
                    hdn_gate[hc] = (pe, mm)
                dp.set_release(idp, pe, mm)
                w1 = [(pe, mm)]
                if ev_gate is not None:
                    w1.append(ev_gate)
                e1 = em.op("scalar", act, lambda e, bank=bank: e.copy(ev[:], bank[:]), waits=w1)
                d_gate[dc % 2] = (act, e1)
                e2 = em.op("vector", dve, lambda e, dc=dc, rt=rt, stg=stg: e.tensor_add(
                    stg[:, dc, :], ev[:], rt[:, dc, :]),
                    waits=[(act, e1), (rp.chan_of(ir), rp.tok(ir))] + (gate if dc == 0 else []))
                ev_gate = (dve, e2)
                last = e2
            rp.set_release(ir, dve, last)
            sp.store(em, j, rout_dram.ap()[:, :, tb * TB:(tb + 1) * TB].rearrange("c p n -> p c n"),
                     stg.ap(), waits=[(dve, last)])
        quiesce_dma(p, em)
        em.replay()


def rows_ln_full(p, em, k1, sum_row, sq_row, m_row, var_row, sinv_row):
    dve, act = p.chan("vector"), p.chan("scalar")
    b_eps, btok = k1.get(em, 1e-5)
    t1 = em.op("vector", dve, lambda e: e.tensor_scalar_mul(m_row[:], sum_row[:], 1.0 / D))
    t2 = em.op("vector", dve, lambda e: e.tensor_mul(var_row[:], m_row[:], m_row[:]), waits=[(dve, t1)])
    t3 = em.op("vector", dve, lambda e: e.scalar_tensor_tensor(var_row[:], sq_row[:], 1.0 / D, var_row[:],
                                                               ALU.mult, ALU.subtract))
    t4 = em.op("scalar", act, lambda e: e.activation(sinv_row[:], var_row[:], AF.Ln, bias=b_eps),
               waits=[(dve, t3), (dve, btok)])
    t5 = em.op("scalar", act, lambda e: e.activation(sinv_row[:], sinv_row[:], AF.Exp, scale=-0.5))
    return t5


def rows_rms1(p, em, k1, var_row, sinv_row, a1_row, sh_row, ln_tok):
    dve, act = p.chan("vector"), p.chan("scalar")
    b_eps2, btok = k1.get(em, 1e-6)
    t1 = em.op("vector", dve, lambda e: e.tensor_scalar_add(a1_row[:], var_row[:], 1e-5), waits=[(act, ln_tok)])
    t2 = em.op("vector", dve, lambda e: e.reciprocal(a1_row[:], a1_row[:]))
    t3 = em.op("vector", dve, lambda e: e.tensor_mul(a1_row[:], var_row[:], a1_row[:]))
    t4 = em.op("scalar", act, lambda e: e.activation(a1_row[:], a1_row[:], AF.Ln, bias=b_eps2),
               waits=[(dve, t3), (dve, btok)])
    t5 = em.op("scalar", act, lambda e: e.activation(a1_row[:], a1_row[:], AF.Exp, scale=-0.5))
    t6 = em.op("vector", dve, lambda e: e.tensor_mul(a1_row[:], a1_row[:], sinv_row[:]), waits=[(act, t5)])
    t7 = em.op("vector", dve, lambda e: e.tensor_scalar_mul(sh_row[:], sinv_row[:], 0.5))
    return t7


def rows_rms2(p, em, k1, sq_row, a2_row):
    dve, act = p.chan("vector"), p.chan("scalar")
    b_eps2, btok = k1.get(em, 1e-6)
    t1 = em.op("vector", dve, lambda e: e.tensor_scalar_mul(a2_row[:], sq_row[:], 4.0 / D))
    t2 = em.op("scalar", act, lambda e: e.activation(a2_row[:], a2_row[:], AF.Ln, bias=b_eps2),
               waits=[(dve, t1), (dve, btok)])
    t3 = em.op("scalar", act, lambda e: e.activation(a2_row[:], a2_row[:], AF.Exp, scale=-0.5))
    t4 = em.op("vector", dve, lambda e: e.tensor_scalar_mul(a2_row[:], a2_row[:], 2.0), waits=[(act, t3)])
    return t4


def load_wcols(p, em, g, name, w_dram, pctx):
    nc = p.nc
    dve = p.chan("vector")
    wrow = pctx.enter_context(nc.sbuf_tensor(p.fresh(name + "r"), [1, D], F32))
    wcols = pctx.enter_context(nc.sbuf_tensor(p.fresh(name + "c"), [128, DCH], F32))
    ch = p.chan(name + "_w")
    tok = em.dma("sync", wrow[:], w_dram.ap().rearrange("(a d) -> a d", a=1), ch)
    t = row_to_cols(p, em, g, wrow, D, wcols, row_waits=[(ch, tok)])
    return wcols, t


class Rows:
    def __init__(self, p, ctx=None):
        ctx = ctx if ctx is not None else p.ctx
        self.r = [ctx.enter_context(p.nc.sbuf_tensor(p.fresh(f"row{i}"), [1, T], F32)) for i in range(5)]
        self.sum, self.sq, self.m, self.var, self.sinv = self.r
        self.a1 = self.sum
        self.sh = self.sq
        self.sq2 = self.var
        self.sum2 = self.sum
        self.a2 = self.sinv


def emit_l0ffn_app(p, g, k1, rows_unused, hc, W, r_in, r_out, scr, tag):
    pe = p.chan("pe")
    rctx = ExitStack()
    rows = Rows(p, rctx)
    em = Emit(p); pass_stats(p, em, g, r_in, rows.sum, rows.sq)
    em = Emit(p)
    em.wait("vector", pe, pe.n)
    t_ln = rows_ln_full(p, em, k1, rows.sum, rows.sq, rows.m, rows.var, rows.sinv)
    t_r1 = rows_rms1(p, em, k1, rows.var, rows.sinv, rows.a1, rows.sh, t_ln)
    em.replay()
    dve = p.chan("vector")
    rw = [(dve, t_r1)]
    em = Emit(p); pass_xn(p, em, g, "xn", r_in, scr["xn0h"], F32, rows.m, rows.sh, row_waits=rw)
    em = Emit(p); pass_rope(p, em, g, "rp", r_in, scr["xr"], rows.m, rows.a1, W["L0_ffn_norm1_w"], hc, row_waits=rw)
    em = Emit(p); pass_linear(p, em, g, "li", scr["xr"], D, W["L0_ffn_Wq"], D, wsrc="f32", out_dram=scr["qT"])
    em = Emit(p); pass_linear(p, em, g, "li", scr["xr"], D, W["L0_ffn_Wk"], 256, wsrc="f32", out_dram=scr["kT"])
    em = Emit(p); pass_linear_tm(p, em, g, "lt", scr["xr"], D, W["L0_ffn_Wv"], 256, scr["vtm"], wsrc="f32")
    em = Emit(p); pass_att(p, em, g, "at", ATT_GQA, scr["qT"], scr["kT"], scr["vtm"], scr["ao"])
    em = Emit(p); pass_linear(p, em, g, "li", scr["ao"], D, W["L0_ffn_Wo"], D, wsrc="f32",
                              epilogue=("res", scr["xn0h"], scr["h1h"], 0.5))
    em = Emit(p); pass_stats(p, em, g, scr["h1h"], rows.sum2, rows.sq2)
    em = Emit(p)
    em.wait("vector", pe, pe.n)
    t_a2 = rows_rms2(p, em, k1, rows.sq2, rows.a2)
    em.replay()
    em = Emit(p)
    with ExitStack() as wctx:
        w2cols, w2tok = load_wcols(p, em, g, "w2c", W["L0_ffn_norm2_w"], wctx)
        em.replay()
        em = Emit(p)
        pass_xn(p, em, g, "xn", scr["h1h"], scr["xn2"], BF16, None, rows.a2, w_cols=w2cols,
                row_waits=[(p.chan("vector"), max(t_a2, w2tok))])
    rctx.close()
    if not hasattr(p, "_b12c"):
        em = Emit(p)
        b1c, b2c, _ = prep_bias_cols(p, em, g, "b12", W["L0_ffn_b1"], W["L0_ffn_b2"], p.ctx)
        em.replay()
        p._b12c = (b1c, b2c)
    b1c, b2c = p._b12c
    em = Emit(p); pass_router(p, em, g, "rt", scr["xn2"], W["L0_ffn_router_w"], W["L0_ffn_router_b"], scr["comb"])
    em = Emit(p); pass_cast_dram(p, em, "w3c", W["L0_ffn_W3"], scr["w3b"], NEXP * FCH,
                                 lambda i: W["L0_ffn_W3"].ap()[i // FCH, (i % FCH) * 128:((i % FCH) + 1) * 128, :],
                                 lambda i: scr["w3b"].ap()[i // FCH, i % FCH, :, :])
    em = Emit(p); pass_moe_ph1(p, em, g, "p1", scr["xn2"], W["L0_ffn_W1"], b1c,
                               W["L0_ffn_W2"], b2c, scr["comb"], scr["h_d"])
    em = Emit(p); pass_moe_ph2(p, em, g, "p2", scr["h_d"], scr["w3b"], W["L0_ffn_b3"], scr["comb"],
                               r_in, scr["h1h"], r_out)


def emit_l0attn_app(p, g, k1, rows_unused, hc, W, r_in, r_out, scr):
    pe = p.chan("pe")
    sctx = ExitStack()
    brow = sctx.enter_context(p.nc.sbuf_tensor(p.fresh("bqkv"), [1, 3 * D], BF16))
    brow2 = sctx.enter_context(p.nc.sbuf_tensor(p.fresh("bprj"), [1, D], BF16))
    rctx = ExitStack()
    rows = Rows(p, rctx)
    em = Emit(p); pass_stats(p, em, g, r_in, rows.sum, rows.sq)
    em = Emit(p)
    em.wait("vector", pe, pe.n)
    t_ln = rows_ln_full(p, em, k1, rows.sum, rows.sq, rows.m, rows.var, rows.sinv)
    em.replay()
    rw = [(p.chan("scalar"), t_ln)]
    em = Emit(p); pass_xn(p, em, g, "xn", r_in, scr["xnb"], BF16, rows.m, rows.sinv, row_waits=rw)
    rctx.close()
    em = Emit(p)
    if True:
        bch = p.chan("bqkv_l")
        btok = em.dma("gpsimd", brow[:], W["L0_attn_bqkv"].ap().rearrange("(a d) -> a d", a=1), bch)
        em.replay()
        bw = [(bch, btok)]
        em = Emit(p); pass_linear(p, em, g, "li", scr["xnb"], D, W["L0_attn_Wqkv"], D, wsrc="f32",
                                  out_dram=scr["qT"], bias_row=brow, bias_waits=bw, w_col0=0)
        em = Emit(p); pass_linear(p, em, g, "li", scr["xnb"], D, W["L0_attn_Wqkv"], D, wsrc="f32",
                                  out_dram=scr["kTf"], bias_row=brow, bias_waits=bw, w_col0=D, bias_col0=D)
        em = Emit(p); pass_linear_tm(p, em, g, "lt", scr["xnb"], D, W["L0_attn_Wqkv"], D, scr["vtmf"],
                                     wsrc="f32", bias_row=brow, bias_waits=bw, w_col0=2 * D, bias_col0=2 * D)
    em = Emit(p); pass_att(p, em, g, "at", ATT_RET, scr["qT"], scr["kTf"], scr["vtmf"], scr["ao"],
                           gamma_dram=W["L0_attn_gamma"])
    em = Emit(p)
    if True:
        bch = p.chan("bprj_l")
        btok = em.dma("gpsimd", brow2[:], W["L0_attn_bproj"].ap().rearrange("(a d) -> a d", a=1), bch)
        em.replay()
        em = Emit(p); pass_linear(p, em, g, "li", scr["ao"], D, W["L0_attn_Wproj"], D, wsrc="f32",
                                  bias_row=brow2, bias_waits=[(bch, btok)],
                                  epilogue=("res", r_in, r_out, 1.0))
    sctx.close()


def emit_l1ffn_app(p, g, k1, rows_unused, hc, W, r_in, r_out, scr, do_prep=True):
    pe = p.chan("pe")
    rctx = ExitStack()
    rows = Rows(p, rctx)
    if do_prep:
        em = Emit(p); pass_wprep(p, em, g, "wp", W["L1_ffn_w_gate"], scr["wgT"], H_L1, D, 1.0 / 127.0)
        em = Emit(p); pass_wprep(p, em, g, "wp", W["L1_ffn_w_up"], scr["wuT"], H_L1, D, 1.0 / 127.0)
        em = Emit(p); pass_wprep(p, em, g, "wp", W["L1_ffn_w_down"], scr["wdT"], D, H_L1, 0.5 / 127.0)
    em = Emit(p); pass_stats(p, em, g, r_in, rows.sum, rows.sq)
    em = Emit(p)
    em.wait("vector", pe, pe.n)
    t_ln = rows_ln_full(p, em, k1, rows.sum, rows.sq, rows.m, rows.var, rows.sinv)
    em.replay()
    rw = [(p.chan("scalar"), t_ln)]
    em = Emit(p); pass_xn(p, em, g, "xn", r_in, scr["xnb"], BF16, rows.m, rows.sinv, row_waits=rw)
    rctx.close()
    em = Emit(p); pass_l1ffn(p, em, g, "ff", scr["xnb"], scr["wgT"], scr["wuT"], scr["wdT"], r_in, r_out)


def emit_l1attn_app(p, g, k1, rows_unused, hc, W, r_in, r_out, scr, do_prep=True):
    pe = p.chan("pe")
    rctx = ExitStack()
    rows = Rows(p, rctx)
    if do_prep:
        for nm, wt in [("pq", "L1_attn_Wq"), ("pk", "L1_attn_Wk"), ("pv", "L1_attn_Wv"), ("po", "L1_attn_Wo")]:
            em = Emit(p); pass_wprep(p, em, g, "wp", W[wt], scr[nm.replace("p", "w") + "T"], D, D, 1.0)
    em = Emit(p); pass_stats(p, em, g, r_in, rows.sum, rows.sq)
    em = Emit(p)
    em.wait("vector", pe, pe.n)
    t_ln = rows_ln_full(p, em, k1, rows.sum, rows.sq, rows.m, rows.var, rows.sinv)
    em.replay()
    rw = [(p.chan("scalar"), t_ln)]
    em = Emit(p); pass_xn(p, em, g, "xn", r_in, scr["xnb"], BF16, rows.m, rows.sinv, row_waits=rw)
    rctx.close()
    em = Emit(p); pass_linear(p, em, g, "li", scr["xnb"], D, scr["wqT"], D, wsrc="bf16T", out_dram=scr["qT"])
    em = Emit(p); pass_linear(p, em, g, "li", scr["xnb"], D, scr["wkT"], D, wsrc="bf16T", out_dram=scr["kTf"])
    em = Emit(p); pass_linear_tm(p, em, g, "lt", scr["xnb"], D, scr["wvT"], D, scr["vtmf"], wsrc="bf16T")
    em = Emit(p); pass_att(p, em, g, "at", ATT_WIN, scr["qT"], scr["kTf"], scr["vtmf"], scr["ao"])
    em = Emit(p); pass_linear(p, em, g, "li", scr["ao"], D, scr["woT"], D, wsrc="bf16T",
                              epilogue=("res", r_in, r_out, 1.0))



WEIGHT_SHAPES = {
    "L0_attn_Wqkv": ([D, 3 * D], F32), "L0_attn_bqkv": ([3 * D], F32),
    "L0_attn_Wproj": ([D, D], F32), "L0_attn_bproj": ([D], F32),
    "L0_attn_gamma": ([1, 1], F32),
    "L0_ffn_norm1_w": ([D], F32), "L0_ffn_Wq": ([D, D], F32),
    "L0_ffn_Wk": ([D, 256], F32), "L0_ffn_Wv": ([D, 256], F32),
    "L0_ffn_Wo": ([D, D], F32), "L0_ffn_norm2_w": ([D], F32),
    "L0_ffn_router_w": ([D, 8], F32), "L0_ffn_router_b": ([8], F32),
    "L0_ffn_W1": ([NEXP, D, F_FF], F32), "L0_ffn_b1": ([NEXP, F_FF], F32),
    "L0_ffn_W2": ([NEXP, D, F_FF], F32), "L0_ffn_b2": ([NEXP, F_FF], F32),
    "L0_ffn_W3": ([NEXP, F_FF, D], F32), "L0_ffn_b3": ([NEXP, D], F32),
    "L1_attn_Wq": ([D, D], F32), "L1_attn_Wk": ([D, D], F32),
    "L1_attn_Wv": ([D, D], F32), "L1_attn_Wo": ([D, D], F32),
    "L1_ffn_w_gate": ([H_L1, D], I32), "L1_ffn_w_up": ([H_L1, D], I32),
    "L1_ffn_w_down": ([D, H_L1], I32),
}


def declare_weights(nc, names):
    return {n: nc.dram_tensor(n, WEIGHT_SHAPES[n][0], WEIGHT_SHAPES[n][1], kind="ExternalInput")
            for n in names}


def _prog_common(consts):
    nc = bass.Bass("TRN2", target_bir_lowering=False)
    hc = declare_constants(nc, consts)
    return nc, hc


def build_prog_t_in():
    consts = GLOBAL_CONSTS
    nc, hc = _prog_common(consts)
    x = nc.dram_tensor("x_in", [SPC, SEQ, D], F32, kind="ExternalInput")
    r0 = nc.dram_tensor("r_out", [DCH, 128, T], F32, kind="ExternalOutput")
    with ExitStack() as ctx:
        p = Prog(nc, ctx)
        g = load_globals(p, hc)
        em = Emit(p); pass_t_in(p, em, g, x, r0)
    return nc, consts, ["x_in"]


def build_prog_t_out():
    consts = GLOBAL_CONSTS
    nc, hc = _prog_common(consts)
    r = nc.dram_tensor("r_in", [DCH, 128, T], F32, kind="ExternalInput")
    y = nc.dram_tensor("y_out", [SPC, SEQ, D], F32, kind="ExternalOutput")
    with ExitStack() as ctx:
        p = Prog(nc, ctx)
        g = load_globals(p, hc)
        em = Emit(p); pass_t_out(p, em, g, r, y)
    return nc, consts, ["r_in"]


def build_prog_l0ffn():
    consts = GLOBAL_CONSTS + ["rope_cos", "rope_sin"]
    nc, hc = _prog_common(consts)
    wn = [n for n in WEIGHT_SHAPES if n.startswith("L0_ffn_")]
    W = declare_weights(nc, wn)
    r_in = nc.dram_tensor("r_in", [DCH, 128, T], F32, kind="ExternalInput")
    r_out = nc.dram_tensor("r_out", [DCH, 128, T], F32, kind="ExternalOutput")
    scr = {
        "xn0h": nc.dram_tensor("s_xn0h", [DCH, 128, T], F32, kind="Internal"),
        "xr": nc.dram_tensor("s_xr", [DCH, 128, T], BF16, kind="Internal"),
        "qT": nc.dram_tensor("s_qT", [DCH, 128, T], BF16, kind="Internal"),
        "kT": nc.dram_tensor("s_kT", [2, 128, T], BF16, kind="Internal"),
        "vtm": nc.dram_tensor("s_vtm", [T, 256], BF16, kind="Internal"),
        "ao": nc.dram_tensor("s_ao", [DCH, 128, T], BF16, kind="Internal"),
        "h1h": nc.dram_tensor("s_h1h", [DCH, 128, T], F32, kind="Internal"),
        "xn2": nc.dram_tensor("s_xn2", [DCH, 128, T], BF16, kind="Internal"),
        "comb": nc.dram_tensor("s_comb", [8, T], F32, kind="Internal"),
        "w3b": nc.dram_tensor("s_w3b", [NEXP, FCH, 128, D], BF16, kind="Internal"),
        "h_d": nc.dram_tensor("s_hd", [NEXP, FCH, 128, T], BF16, kind="Internal"),
    }
    with ExitStack() as ctx:
        p = Prog(nc, ctx)
        k1 = Consts1(p)
        g = load_globals(p, hc)
        em = Emit(p); k1.prewarm(em); em.replay()
        emit_l0ffn_app(p, g, k1, None, hc, W, r_in, r_out, scr, "f")
    return nc, consts, ["r_in"] + wn


def build_prog_l0attn():
    consts = GLOBAL_CONSTS
    nc, hc = _prog_common(consts)
    wn = [n for n in WEIGHT_SHAPES if n.startswith("L0_attn_")]
    W = declare_weights(nc, wn)
    r_in = nc.dram_tensor("r_in", [DCH, 128, T], F32, kind="ExternalInput")
    r_out = nc.dram_tensor("r_out", [DCH, 128, T], F32, kind="ExternalOutput")
    scr = {
        "xnb": nc.dram_tensor("s_xnb", [DCH, 128, T], BF16, kind="Internal"),
        "qT": nc.dram_tensor("s_qT", [DCH, 128, T], BF16, kind="Internal"),
        "kTf": nc.dram_tensor("s_kTf", [DCH, 128, T], BF16, kind="Internal"),
        "vtmf": nc.dram_tensor("s_vtmf", [T, D], BF16, kind="Internal"),
        "ao": nc.dram_tensor("s_ao", [DCH, 128, T], BF16, kind="Internal"),
    }
    with ExitStack() as ctx:
        p = Prog(nc, ctx)
        k1 = Consts1(p)
        g = load_globals(p, hc)
        em = Emit(p); k1.prewarm(em); em.replay()
        emit_l0attn_app(p, g, k1, None, hc, W, r_in, r_out, scr)
    return nc, consts, ["r_in"] + wn


def build_prog_l1ffn():
    consts = GLOBAL_CONSTS
    nc, hc = _prog_common(consts)
    wn = ["L1_ffn_w_gate", "L1_ffn_w_up", "L1_ffn_w_down"]
    W = declare_weights(nc, wn)
    r_in = nc.dram_tensor("r_in", [DCH, 128, T], F32, kind="ExternalInput")
    r_out = nc.dram_tensor("r_out", [DCH, 128, T], F32, kind="ExternalOutput")
    scr = {
        "xnb": nc.dram_tensor("s_xnb", [DCH, 128, T], BF16, kind="Internal"),
        "wgT": nc.dram_tensor("s_wgT", [DCH, 128, H_L1], BF16, kind="Internal"),
        "wuT": nc.dram_tensor("s_wuT", [DCH, 128, H_L1], BF16, kind="Internal"),
        "wdT": nc.dram_tensor("s_wdT", [HCH, 128, D], BF16, kind="Internal"),
    }
    with ExitStack() as ctx:
        p = Prog(nc, ctx)
        k1 = Consts1(p)
        g = load_globals(p, hc)
        em = Emit(p); k1.prewarm(em); em.replay()
        emit_l1ffn_app(p, g, k1, None, hc, W, r_in, r_out, scr)
    return nc, consts, ["r_in"] + wn


def build_prog_l1attn():
    consts = GLOBAL_CONSTS
    nc, hc = _prog_common(consts)
    wn = ["L1_attn_Wq", "L1_attn_Wk", "L1_attn_Wv", "L1_attn_Wo"]
    W = declare_weights(nc, wn)
    r_in = nc.dram_tensor("r_in", [DCH, 128, T], F32, kind="ExternalInput")
    r_out = nc.dram_tensor("r_out", [DCH, 128, T], F32, kind="ExternalOutput")
    scr = {
        "xnb": nc.dram_tensor("s_xnb", [DCH, 128, T], BF16, kind="Internal"),
        "qT": nc.dram_tensor("s_qT", [DCH, 128, T], BF16, kind="Internal"),
        "kTf": nc.dram_tensor("s_kTf", [DCH, 128, T], BF16, kind="Internal"),
        "vtmf": nc.dram_tensor("s_vtmf", [T, D], BF16, kind="Internal"),
        "ao": nc.dram_tensor("s_ao", [DCH, 128, T], BF16, kind="Internal"),
        "wqT": nc.dram_tensor("s_wqT", [DCH, 128, D], BF16, kind="Internal"),
        "wkT": nc.dram_tensor("s_wkT", [DCH, 128, D], BF16, kind="Internal"),
        "wvT": nc.dram_tensor("s_wvT", [DCH, 128, D], BF16, kind="Internal"),
        "woT": nc.dram_tensor("s_woT", [DCH, 128, D], BF16, kind="Internal"),
    }
    with ExitStack() as ctx:
        p = Prog(nc, ctx)
        k1 = Consts1(p)
        g = load_globals(p, hc)
        em = Emit(p); k1.prewarm(em); em.replay()
        emit_l1attn_app(p, g, k1, None, hc, W, r_in, r_out, scr)
    return nc, consts, ["r_in"] + wn


_PROGS = {}

def get_prog(name):
    if name not in _PROGS:
        _PROGS[name] = {
            "t_in": build_prog_t_in, "t_out": build_prog_t_out,
            "l0ffn": build_prog_l0ffn, "l0attn": build_prog_l0attn,
            "l1ffn": build_prog_l1ffn, "l1attn": build_prog_l1attn,
            "fused": build_prog_fused,
        }[name]()
    return _PROGS[name]


def _launch(name, per_core_feeds, weights_np):
    nc, consts, innames = get_prog(name)
    cf = const_feeds(consts)
    in_maps = []
    for core in range(NCORES):
        m = dict(cf)
        for n in innames:
            if n in ("x_in", "r_in"):
                m[n] = per_core_feeds[core][n]
            else:
                m[n] = weights_np[n]
        in_maps.append(m)
    res = run_bass_kernel_spmd(nc, in_maps, core_ids=list(range(NCORES)))
    return res.results


def kernel_unfused(**inputs):
    x = np.ascontiguousarray(inputs["L0_attn_x"], dtype=np.float32)
    weights_np = {}
    for n, (shape, dt) in WEIGHT_SHAPES.items():
        if n == "L0_attn_x":
            continue
        v = np.asarray(inputs[n]) if n in inputs else None
        if n == "L0_attn_gamma":
            v = np.asarray(inputs[n]).reshape(1, 1).astype(np.float32)
        weights_np[n] = np.ascontiguousarray(v)
    shards = [np.ascontiguousarray(x[2 * i:2 * i + 2]) for i in range(NCORES)]

    feeds = [{"x_in": shards[i]} for i in range(NCORES)]
    res = _launch("t_in", feeds, weights_np)
    r = [res[i]["r_out"] for i in range(NCORES)]
    res = _launch("l0ffn", [{"r_in": r[i]} for i in range(NCORES)], weights_np)
    r = [res[i]["r_out"] for i in range(NCORES)]
    res = _launch("l0attn", [{"r_in": r[i]} for i in range(NCORES)], weights_np)
    r = [res[i]["r_out"] for i in range(NCORES)]
    res = _launch("l0ffn", [{"r_in": r[i]} for i in range(NCORES)], weights_np)
    r = [res[i]["r_out"] for i in range(NCORES)]
    res = _launch("l1ffn", [{"r_in": r[i]} for i in range(NCORES)], weights_np)
    r = [res[i]["r_out"] for i in range(NCORES)]
    res = _launch("l1attn", [{"r_in": r[i]} for i in range(NCORES)], weights_np)
    r = [res[i]["r_out"] for i in range(NCORES)]
    res = _launch("l1ffn", [{"r_in": r[i]} for i in range(NCORES)], weights_np)
    r = [res[i]["r_out"] for i in range(NCORES)]
    res = _launch("t_out", [{"r_in": r[i]} for i in range(NCORES)], weights_np)
    y = np.concatenate([res[i]["y_out"] for i in range(NCORES)], axis=0)
    return y.reshape(16, SEQ, D)



def build_prog_fused():
    consts = GLOBAL_CONSTS + ["rope_cos", "rope_sin"]
    nc, hc = _prog_common(consts)
    wn = list(WEIGHT_SHAPES.keys())
    W = declare_weights(nc, wn)
    x = nc.dram_tensor("x_in", [SPC, SEQ, D], F32, kind="ExternalInput")
    y = nc.dram_tensor("y_out", [SPC, SEQ, D], F32, kind="ExternalOutput")
    r_a = nc.dram_tensor("r_a", [DCH, 128, T], F32, kind="Internal")
    r_b = nc.dram_tensor("r_b", [DCH, 128, T], F32, kind="Internal")
    scr = {
        "xn0h": nc.dram_tensor("s_xn0h", [DCH, 128, T], F32, kind="Internal"),
        "xr": nc.dram_tensor("s_xr", [DCH, 128, T], BF16, kind="Internal"),
        "qT": nc.dram_tensor("s_qT", [DCH, 128, T], BF16, kind="Internal"),
        "kT": nc.dram_tensor("s_kT", [2, 128, T], BF16, kind="Internal"),
        "kTf": nc.dram_tensor("s_kTf", [DCH, 128, T], BF16, kind="Internal"),
        "vtm": nc.dram_tensor("s_vtm", [T, 256], BF16, kind="Internal"),
        "vtmf": nc.dram_tensor("s_vtmf", [T, D], BF16, kind="Internal"),
        "ao": nc.dram_tensor("s_ao", [DCH, 128, T], BF16, kind="Internal"),
        "h1h": nc.dram_tensor("s_h1h", [DCH, 128, T], F32, kind="Internal"),
        "xn2": nc.dram_tensor("s_xn2", [DCH, 128, T], BF16, kind="Internal"),
        "comb": nc.dram_tensor("s_comb", [8, T], F32, kind="Internal"),
        "w3b": nc.dram_tensor("s_w3b", [NEXP, FCH, 128, D], BF16, kind="Internal"),
        "h_d": nc.dram_tensor("s_hd", [NEXP, FCH, 128, T], BF16, kind="Internal"),
        "xnb": nc.dram_tensor("s_xnb", [DCH, 128, T], BF16, kind="Internal"),
        "wgT": nc.dram_tensor("s_wgT", [DCH, 128, H_L1], BF16, kind="Internal"),
        "wuT": nc.dram_tensor("s_wuT", [DCH, 128, H_L1], BF16, kind="Internal"),
        "wdT": nc.dram_tensor("s_wdT", [HCH, 128, D], BF16, kind="Internal"),
        "wqT": nc.dram_tensor("s_wqT", [DCH, 128, D], BF16, kind="Internal"),
        "wkT": nc.dram_tensor("s_wkT", [DCH, 128, D], BF16, kind="Internal"),
        "wvT": nc.dram_tensor("s_wvT", [DCH, 128, D], BF16, kind="Internal"),
        "woT": nc.dram_tensor("s_woT", [DCH, 128, D], BF16, kind="Internal"),
    }
    with ExitStack() as ctx:
        p = Prog(nc, ctx)
        k1 = Consts1(p)
        g = load_globals(p, hc)
        em = Emit(p); k1.prewarm(em); em.replay()
        em = Emit(p); pass_t_in(p, em, g, x, r_a)
        emit_l0ffn_app(p, g, k1, None, hc, W, r_a, r_b, scr, "s1")
        emit_l0attn_app(p, g, k1, None, hc, W, r_b, r_a, scr)
        emit_l0ffn_app(p, g, k1, None, hc, W, r_a, r_b, scr, "s3")
        emit_l1ffn_app(p, g, k1, None, hc, W, r_b, r_a, scr, do_prep=True)
        emit_l1attn_app(p, g, k1, None, hc, W, r_a, r_b, scr, do_prep=True)
        emit_l1ffn_app(p, g, k1, None, hc, W, r_b, r_a, scr, do_prep=False)
        em = Emit(p); pass_t_out(p, em, g, r_a, y)
    return nc, consts, ["x_in"] + wn


def kernel(**inputs):
    x = np.ascontiguousarray(inputs["L0_attn_x"], dtype=np.float32)
    weights_np = {}
    for n in WEIGHT_SHAPES:
        v = np.asarray(inputs[n])
        if n == "L0_attn_gamma":
            v = v.reshape(1, 1).astype(np.float32)
        weights_np[n] = np.ascontiguousarray(v)
    shards = [np.ascontiguousarray(x[2 * i:2 * i + 2]) for i in range(NCORES)]
    nc, consts, innames = get_prog("fused")
    cf = const_feeds(consts)
    in_maps = []
    for core in range(NCORES):
        m = dict(cf)
        for n in innames:
            m[n] = shards[core] if n == "x_in" else weights_np[n]
        in_maps.append(m)
    res = run_bass_kernel_spmd(nc, in_maps, core_ids=list(range(NCORES)))
    y = np.concatenate([res.results[i]["y_out"] for i in range(NCORES)], axis=0)
    return y.reshape(16, SEQ, D)
